# version 10
# baseline (speedup 1.0000x reference)
import numpy as np
from contextlib import ExitStack

import concourse.bass as bass
import concourse.mybir as mybir

F32 = mybir.dt.float32
BF16 = mybir.dt.bfloat16
AF = mybir.ActivationFunctionType
ALU = mybir.AluOpType
AX = mybir.AxisListType

T = 1024
DM = 4096
EC = DM // 128
NCOL = T // 512
IFFN = 11008
BLK_CH = [22, 22, 22, 20]
BLK_OFF = [0, 22, 44, 66]
NW = float(IFFN * DM)
SCL = 0.08838834764831845

ENGINES = ("sp", "pe", "dve", "act", "pool")


class Sched:
    SELF_EDGE = ("dve", "act", "pool")

    def __init__(self):
        self.seg = 0
        self.ops = {e: [] for e in ENGINES}
        self.count = {}
        self.lastwait = {e: {} for e in ENGINES}
        self.serialize = False
        self.last_tok = {}
        self.prev_tok = {e: None for e in ENGINES}

    def emit(self, eng, fn, waits=(), inc=None, inc_n=1, raw=False):
        is_dma = (inc is not None and inc.endswith("dma")) or raw
        if not is_dma and eng in self.SELF_EDGE:
            if inc is None:
                inc = eng
                inc_n = 1
            waits = list(waits) + [self.prev_tok[eng]]
        if self.serialize and not raw:
            waits = list(waits) + [t for e2, t in self.last_tok.items() if e2 != eng]
            if inc is None:
                inc = {"sp": "spdma", "pool": "pooldma"}.get(eng, eng)
                inc_n = 16 if inc.endswith("dma") else 1
        ws = []
        for w in waits:
            if w is None:
                continue
            s, v = w
            lw = self.lastwait[eng].get(s, 0)
            if v > lw:
                ws.append((s, v))
                self.lastwait[eng][s] = v
        tok = None
        if inc is not None:
            self.count[inc] = self.count.get(inc, 0) + inc_n
            tok = (inc, self.count[inc])
        if self.serialize and tok is not None:
            self.last_tok[eng] = tok
        if not is_dma and eng in self.SELF_EDGE and tok is not None:
            self.prev_tok[eng] = tok
        self.ops[eng].append((self.seg, ws, fn, inc, inc_n))
        return tok


def tmax(*toks):
    best = None
    for t in toks:
        if t is None:
            continue
        if best is None or t[1] > best[1]:
            best = t
    return best


def build_program(debug=(), phases=("p0", "p1", "p2", "p3", "p4", "p5"), ndev=8, rc=True, bar=True):
    nc = bass.Bass("TRN2", target_bir_lowering=False, debug=False, num_devices=ndev, detect_race_conditions=rc)
    nc.reset()
    dbg = set(debug)

    x_d = nc.dram_tensor("x", [T, DM], F32, kind="ExternalInput").ap()
    wq_d = nc.dram_tensor("wq", [DM, DM], F32, kind="ExternalInput").ap()
    bq_d = nc.dram_tensor("bq", [128, 32], F32, kind="ExternalInput").ap()
    wk_d = nc.dram_tensor("wk", [DM, 1024], F32, kind="ExternalInput").ap()
    bk_d = nc.dram_tensor("bk", [128, 8], F32, kind="ExternalInput").ap()
    wv_d = nc.dram_tensor("wv", [DM, 1024], F32, kind="ExternalInput").ap()
    bv_d = nc.dram_tensor("bv", [128, 8], F32, kind="ExternalInput").ap()
    HI = IFFN // 2
    wg_d = nc.dram_tensor("wg", [HI, DM], F32, kind="ExternalInput").ap()
    wu_d = nc.dram_tensor("wu", [HI, DM], F32, kind="ExternalInput").ap()
    wd_d = nc.dram_tensor("wd", [DM, HI], F32, kind="ExternalInput").ap()
    FAMC = 86 * 32 * 128
    wsh_h = nc.dram_tensor("wsh", [128, 2 * FAMC], BF16, kind="Internal",
                           addr_space="Shared")
    wsh = wsh_h.ap()
    wshd_h = nc.dram_tensor("wshd", [128, FAMC], BF16, kind="Internal",
                            addr_space="Shared")
    wshd = wshd_h.ap()
    wsc_h = nc.dram_tensor("wsc", [2, 4], F32, kind="Internal", addr_space="Shared")
    wsc = wsc_h.ap()
    MYC = 43 * 32 * 128
    wloc_h = nc.dram_tensor("wloc", [128, 3 * MYC], BF16, kind="Internal")
    wloc = wloc_h.ap()
    out_d = nc.dram_tensor("out", [T, DM], F32, kind="ExternalOutput").ap()

    def scratch(name, shape):
        kind = "ExternalOutput" if name in dbg else "Internal"
        return nc.dram_tensor(name, shape, F32, kind=kind).ap()

    o8_d = scratch("o8_scr", [T * 8, 128])
    wc_d = nc.dram_tensor("wqkv_scr", [3 * DM, 1024], BF16, kind="Internal").ap()
    part_d = scratch("part_scr", [DM, T])
    ffn_d = scratch("ffn_scr", [T, DM])
    dump_d = {}
    for nm, w in (("xnT", EC * T), ("QS_dm", T * 8), ("K_dm", T * 8), ("V_dm", T * 8),
                  ("cst", 16), ("acc", 8), ("interT", 22 * T),
                  ("gtmp", 1024)):
        if nm in dbg:
            dump_d[nm] = nc.dram_tensor("dump_" + nm, [128, w], F32,
                                        kind="ExternalOutput").ap()

    S = Sched()
    es = ExitStack()

    def mksb(stack, name, p, f, dt):
        return stack.enter_context(nc.sbuf_tensor(name, [p, f], dt))

    ident_bf = mksb(es, "ident_bf", 128, 128, BF16)
    ident_f = mksb(es, "ident_f", 128, 128, F32)
    xnT = mksb(es, "xnT", 128, EC * T, BF16)
    bq_sb = mksb(es, "bq_sb", 128, 32, F32)
    bk_sb = mksb(es, "bk_sb", 128, 8, F32)
    bv_sb = mksb(es, "bv_sb", 128, 8, F32)
    bqs = mksb(es, "bqs", 128, 8, F32)
    bscr = mksb(es, "bscr", 128, 16, F32)
    cst = mksb(es, "cst", 128, 16, F32)
    ones1 = mksb(es, "ones1", 1, 128, F32)
    acc = mksb(es, "acc", 128, 8, F32)
    epsc = mksb(es, "epsc", 128, 2, F32)
    pstg = mksb(es, "pstg", 1, 4, F32)

    pbt = es.enter_context(nc.psum_tensor("pb", [128, 4096], F32))

    def bank(k, lo=0, n=512):
        return pbt[:, k * 512 + lo: k * 512 + lo + n]

    sems = {}

    S.emit("pool", lambda e: e.memset(ident_bf[:], 0.0), inc="pool")
    t_id = S.emit("pool", lambda e: e.affine_select(
        out=ident_bf[:], in_=ident_bf[:], compare_op=ALU.not_equal, fill=1.0,
        base=0, pattern=[[-1, 128]], channel_multiplier=1), inc="pool")
    S.emit("pool", lambda e: e.memset(ident_f[:], 0.0), inc="pool")
    t_idf = S.emit("pool", lambda e: e.affine_select(
        out=ident_f[:], in_=ident_f[:], compare_op=ALU.not_equal, fill=1.0,
        base=0, pattern=[[-1, 128]], channel_multiplier=1), inc="pool")
    S.emit("pool", lambda e: e.memset(cst[:, 0:1], 1.0), inc="pool")
    S.emit("pool", lambda e: e.memset(ones1[:], 1.0), inc="pool")
    t_acc0 = S.emit("pool", lambda e: e.memset(acc[:], 0.0), inc="pool")
    t_pstg0 = S.emit("pool", lambda e: e.memset(pstg[:], 0.0), inc="pool")
    S.emit("pool", lambda e: e.memset(epsc[:, 0:1], 1e-5), inc="pool")
    t_eps = S.emit("pool", lambda e: e.memset(epsc[:, 1:2], 0.5e-5), inc="pool")
    t_bq = S.emit("pool", lambda e: e.dma_start(bq_sb[:], bq_d), inc="dbias", inc_n=16)
    S.emit("pool", lambda e: e.dma_start(bk_sb[:], bk_d), inc="dbias", inc_n=16)
    t_bv = S.emit("pool", lambda e: e.dma_start(bv_sb[:], bv_d), inc="dbias", inc_n=16)
    bqv = bq_sb[:].rearrange("p (h j) -> p h j", j=4)
    S.emit("dve", lambda e: e.tensor_add(bscr[:, 0:8], bqv[:, :, 0], bqv[:, :, 1]),
           waits=[t_bv], inc="dve")
    t_b1 = S.emit("dve", lambda e: e.tensor_add(bscr[:, 8:16], bqv[:, :, 2], bqv[:, :, 3]),
                  inc="dve")
    t_b2 = S.emit("dve", lambda e: e.tensor_add(bscr[:, 0:8], bscr[:, 0:8], bscr[:, 8:16]),
                  waits=[t_b1], inc="dve")
    t_bqs = S.emit("dve", lambda e: e.tensor_scalar_mul(bqs[:], bscr[:, 0:8], SCL),
                   waits=[t_b2], inc="dve")

    p12 = ExitStack()
    pab = []
    QS_dm = mksb(p12, "QS_dm", 128, T * 8, BF16)
    K_dm = mksb(p12, "K_dm", 128, T * 8, BF16)
    V_dm = mksb(p12, "V_dm", 128, T * 8, BF16)
    qkT = [mksb(p12, f"qkT{i}", 8, 1024, BF16) for i in range(2)]
    attn_bf = mksb(p12, "attn_bf", 128, 4 * 512, BF16)
    attn_s = mksb(p12, "attn_s", 128, 2 * 512, BF16)
    attnT_sb = mksb(p12, "attnT_sb", 128, 2 * 512, BF16)
    sums4 = mksb(p12, "sums4", 128, 16, F32)
    negm4 = mksb(p12, "negm4", 128, 16, F32)
    rec4 = mksb(p12, "rec4", 128, 16, F32)
    o8sb = [mksb(p12, f"o8sb{i}", 128, 512, F32) for i in range(2)]
    o8stg = [mksb(p12, f"o8stg{i}", 128, 512, F32) for i in range(2)]

    pa_pieces = []
    if "p3" in phases:
        for mi, wsrc in ((0, wg_d), (1, wu_d)):
            for ch in range(HI // 128):
                pa_pieces.append((wsrc[ch * 128:(ch + 1) * 128, :], DM, mi))
        for hc in range(32):
            pa_pieces.append((wd_d[hc * 128:(hc + 1) * 128, 0:4096], 4096, 2))
            pa_pieces.append((wd_d[hc * 128:(hc + 1) * 128, 4096:HI],
                              HI - 4096, 2))
    pa = {"i": 0, "slot_tok": [None, None], "par": [0, 0, 0],
          "acc_tok": [t_acc0, t_acc0, t_acc0], "scr_tok": None}

    def passA_next(n):
        for _ in range(n):
            i = pa["i"]
            if i >= len(pa_pieces):
                return
            pa["i"] = i + 1
            src, w, mi = pa_pieces[i]
            sl = i % 2
            td = S.emit("sp", lambda e, src=src, sl=sl, w=w: e.dma_start(
                pab[sl][:, 0:w], src), waits=[pa["slot_tok"][sl]],
                inc="dpab%d" % (i % 4), inc_n=16)
            tras = []
            for k0 in range(0, w, 2048):
                k1 = min(w, k0 + 2048)
                tras.append(S.emit("act", lambda e, sl=sl, k0=k0, k1=k1,
                                   kk=len(tras): e.activation(
                    pab[sl][:, k0:k1], pab[sl][:, k0:k1], AF.Abs,
                    accum_out=acc[:, 6 + kk:7 + kk]),
                    waits=[td, pa["scr_tok"]], inc="act"))
            if len(tras) > 1:
                tr = S.emit("dve", lambda e: e.tensor_add(
                    acc[:, 6:7], acc[:, 6:7], acc[:, 7:8]),
                    waits=tras, inc="dve")
            else:
                tr = tras[0]
            par = pa["par"][mi]
            ca, cb = mi * 2 + par, mi * 2 + (1 - par)
            ta = S.emit("dve", lambda e, ca=ca, cb=cb: e.tensor_add(
                acc[:, cb:cb + 1], acc[:, ca:ca + 1], acc[:, 6:7]),
                waits=[tr, pa["acc_tok"][mi]], inc="dve")
            pa["par"][mi] = 1 - par
            pa["acc_tok"][mi] = ta
            pa["slot_tok"][sl] = ta
            pa["scr_tok"] = ta

    t_xnT_done = None
    if "p0" in phases:
        p0es = ExitStack()
        xbuf = [mksb(p0es, f"xb{i}", 128, DM, F32) for i in range(2)]
        xn_bf = [mksb(p0es, f"xn{i}", 128, DM, BF16) for i in range(2)]
        stats = mksb(p0es, "lnst", 128, 8, F32)
        slot_tok = [None, None]
        xn_war = [None, None]
        evac_tok = {0: None, 1: None}
        for tt in range(T // 128):
            p = tt % 2
            td = S.emit("pool", lambda e, tt=tt, p=p: e.dma_start(
                xbuf[p][:], x_d[tt * 128:(tt + 1) * 128, :]),
                waits=[slot_tok[p]], inc="dxb%d" % p, inc_n=16)
            t1 = S.emit("dve", lambda e, p=p: e.reduce_sum(
                stats[:, 4:5], xbuf[p][:], axis=AX.X), waits=[td], inc="dve")
            t2 = S.emit("dve", lambda e, p=p: e.tensor_scalar_mul(
                stats[:, p:p + 1], stats[:, 4:5], -1.0 / DM), waits=[t1], inc="dve")
            t3a = S.emit("act", lambda e, p=p: e.activation(
                bank(2, 0, 2048), xbuf[p][:, 0:2048], AF.Square,
                bias=stats[:, p:p + 1], accum_out=stats[:, 2 + p:3 + p]),
                waits=[t2], inc="act")
            t3b = S.emit("act", lambda e, p=p: e.activation(
                bank(2, 0, 2048), xbuf[p][:, 2048:4096], AF.Square,
                bias=stats[:, p:p + 1], accum_out=stats[:, 6:7]),
                waits=[t3a], inc="act")
            t3c = S.emit("dve", lambda e, p=p: e.tensor_add(
                stats[:, 2 + p:3 + p], stats[:, 2 + p:3 + p], stats[:, 6:7]),
                waits=[t3b], inc="dve")
            t4 = S.emit("act", lambda e, p=p: e.activation(
                stats[:, 2 + p:3 + p], stats[:, 2 + p:3 + p], AF.Ln,
                bias=epsc[:, 0:1], scale=1.0 / DM), waits=[t3c, t_eps], inc="act")
            t5 = S.emit("act", lambda e, p=p: e.activation(
                stats[:, 2 + p:3 + p], stats[:, 2 + p:3 + p], AF.Exp,
                scale=-0.5), waits=[t4], inc="act")
            t6 = S.emit("dve", lambda e, p=p: e.tensor_scalar(
                xn_bf[p][:], xbuf[p][:], stats[:, p:p + 1], stats[:, 2 + p:3 + p],
                ALU.add, ALU.mult), waits=[t5, xn_war[p]], inc="dve")
            slot_tok[p] = t6
            last_tr = None
            for q in range(8):
                bk_ = q % 2
                tts = None
                for c in range(4 * q, 4 * q + 4):
                    tts = S.emit("pe", lambda e, p=p, c=c, bk_=bk_: e.matmul(
                        bank(bk_, (c % 4) * 128, 128)[0:128, :],
                        xn_bf[p][:, c * 128:(c + 1) * 128], ident_bf[:],
                        start=True, stop=True, skip_group_check=True),
                        waits=[t6, t_id, evac_tok[bk_]], inc="pe")
                last_tr = tts
                dst = xnT[:].rearrange("p (c t) -> p c t", t=T)[
                    :, 4 * q:4 * q + 4, tt * 128:(tt + 1) * 128]
                if q % 2 == 0:
                    evac_tok[bk_] = S.emit("dve", lambda e, bk_=bk_, dst=dst: e.tensor_copy(
                        dst, bank(bk_)[:].rearrange("p (c t) -> p c t", t=128)),
                        waits=[tts], inc="dve")
                else:
                    evac_tok[bk_] = S.emit("act", lambda e, bk_=bk_, dst=dst: e.activation(
                        dst, bank(bk_)[:].rearrange("p (c t) -> p c t", t=128),
                        AF.Copy), waits=[tts], inc="act")
            xn_war[p] = last_tr
        t_xnT_done = tmax(evac_tok[0], evac_tok[1])
        p0es.close()
    pab.extend(mksb(p12, f"pab{i}", 128, 4096, F32) for i in range(2))

    t_p1_all = None
    if "p1" in phases:
        p1es = ExitStack()
        wbuf = [mksb(p1es, f"wb{i}", 128, DM, F32) for i in range(2)]
        tmp1 = mksb(p1es, "tmp1", 128, 1024, F32)
        wbf = [mksb(p1es, f"wbf{i}", 128, 1024, BF16) for i in range(2)]
        wslot_tok = [None, None]
        wbf_tok = {0: [], 1: []}
        wpark = {}
        bank_evac = {h: None for h in range(8)}
        for col in range(NCOL):
            tw = 512
            for mat, (wsrc, bias_sb, bscl) in enumerate((
                    (wq_d, bqs, SCL), (wk_d, bk_sb, 1.0), (wv_d, bv_sb, 1.0))):
                dm_dst = (QS_dm, K_dm, V_dm)[mat]
                mm_toks = [None] * 8
                for ec in range(EC):
                    p = ec % 2
                    wide = DM if mat == 0 else 1024
                    if col == 0:
                        td = S.emit("sp", lambda e, p=p, ec=ec, wsrc=wsrc, wide=wide: e.dma_start(
                            wbuf[p][:, 0:wide], wsrc[ec * 128:(ec + 1) * 128, :]),
                            waits=[wslot_tok[p], t_xnT_done], inc="dwb%d" % p, inc_n=16)
                        if mat == 0:
                            wv4 = wbuf[p][:, 0:DM].rearrange("p (c j) -> p c j", j=4)
                            S.emit("dve", lambda e, wv4=wv4: e.tensor_add(
                                tmp1[:], wv4[:, :, 0], wv4[:, :, 1]), waits=[td], inc="dve")
                            S.emit("dve", lambda e, wv4=wv4: e.tensor_add(
                                tmp1[:], tmp1[:], wv4[:, :, 2]), inc="dve")
                            S.emit("dve", lambda e, wv4=wv4: e.tensor_add(
                                tmp1[:], tmp1[:], wv4[:, :, 3]), inc="dve")
                            tc = S.emit("dve", lambda e, p=p: e.tensor_copy(
                                wbf[p][:], tmp1[:]), waits=wbf_tok[p], inc="dve")
                        else:
                            tc = S.emit("dve", lambda e, p=p: e.tensor_copy(
                                wbf[p][:], wbuf[p][:, 0:1024]),
                                waits=[td] + wbf_tok[p], inc="dve")
                        wslot_tok[p] = tc
                        tpk = S.emit("pool", lambda e, p=p, mat=mat, ec=ec: e.dma_start(
                            wc_d[mat * DM + ec * 128: mat * DM + (ec + 1) * 128, :],
                            wbf[p][:]), waits=[tc], inc="dwc%d" % p, inc_n=16)
                        wpark[(mat, ec)] = tpk
                    else:
                        tc = S.emit("sp", lambda e, p=p, mat=mat, ec=ec: e.dma_start(
                            wbf[p][:],
                            wc_d[mat * DM + ec * 128: mat * DM + (ec + 1) * 128, :]),
                            waits=wbf_tok[p] + [wpark[(mat, ec)], wslot_tok[p]],
                            inc="dwb%d" % p, inc_n=16)
                        wslot_tok[p] = tc
                    for hp in range(8):
                        lhs = wbf[p][:, hp:1024:8]
                        rhs = xnT[:, ec * T + col * 512: ec * T + col * 512 + tw]
                        mm_toks[hp] = S.emit("pe", lambda e, lhs=lhs, rhs=rhs, hp=hp, ec=ec: e.matmul(
                            bank(hp, 0, tw), lhs, rhs,
                            start=(ec == 0), stop=(ec == EC - 1), skip_group_check=True),
                            waits=[tc, bank_evac[hp]], inc="pe")
                    wbf_tok[p] = [mm_toks[7]] + ([wpark[(mat, ec)]] if col == 0 else [])
                    if ec % 8 == 3:
                        passA_next(1)
                for hp in range(8):
                    dst = dm_dst[:].rearrange("p (t h) -> p t h", h=8)[
                        :, col * 512:(col + 1) * 512, hp]
                    bias_ap = bias_sb[:, hp:hp + 1]
                    bank_evac[hp] = S.emit("dve", lambda e, hp=hp, dst=dst,
                                           bias_ap=bias_ap, bscl=bscl, tw=tw: e.tensor_scalar(
                        dst, bank(hp, 0, tw), bscl, bias_ap, ALU.mult, ALU.add),
                        waits=[mm_toks[hp], t_bv, t_bqs], inc="dve")
        t_p1_all = bank_evac[7]
        p1es.close()

    t_p2_parks = None
    park_final = {}
    t_last_av = None
    if "p2" in phases:
        G = T // 4
        qkT_tok = [None, None]
        trq_war = None
        sc_war = [None, None]
        at_war = [None, None]
        o8_war = [None, None]
        o8sb_war = [None, None]
        o8stg_war = [None, None]
        st = {}
        for tick in range(G + 6):
            g = tick
            if g < G:
                tk = None
                for i in range(4):
                    t = 4 * g + i
                    S.emit("pe", lambda e, t=t, i=i: e.matmul(
                        bank(i // 2, (i % 2) * 256, 128)[0:8, :],
                        QS_dm[:, 8 * t:8 * t + 8], ident_bf[:], start=True, stop=True, skip_group_check=True),
                        waits=[t_p1_all, trq_war, t_id], inc="pe")
                    tk = S.emit("pe", lambda e, t=t, i=i: e.matmul(
                        bank(i // 2, (i % 2) * 256 + 128, 128)[0:8, :],
                        K_dm[:, 8 * t:8 * t + 8], ident_bf[:], start=True, stop=True, skip_group_check=True),
                        inc="pe")
                st[(g, "tp")] = tk
                st[(g, "qke")] = S.emit("dve", lambda e, g=g: e.tensor_copy(
                    qkT[g % 2][0:8, :], bank(0, 0, 1024)[0:8, :]),
                    waits=[st[(g, "tp")], qkT_tok[g % 2]], inc="dve")
                trq_war = st[(g, "qke")]
            g = tick - 1
            if 0 <= g < G:
                tk = None
                for i in range(4):
                    tk = S.emit("pe", lambda e, g=g, i=i: e.matmul(
                        bank(2 + g % 2, i * 128, 128),
                        qkT[g % 2][0:8, i * 256:i * 256 + 128],
                        qkT[g % 2][0:8, i * 256 + 128:i * 256 + 256],
                        start=True, stop=True, skip_group_check=True),
                        waits=[st[(g, "qke")], sc_war[g % 2]], inc="pe")
                st[(g, "sc")] = tk
                qkT_tok[g % 2] = tk
                st[(g, "nm")] = S.emit("dve", lambda e, g=g: e.reduce_max(
                    negm4[:, (g % 4):(g % 4) + 1], bank(2 + g % 2),
                    axis=AX.X, negate=True), waits=[st[(g, "sc")]], inc="dve")
            g = tick - 2
            if 0 <= g < G:
                tk = S.emit("act", lambda e, g=g: e.activation(
                    attn_bf[:, (g % 4) * 512:(g % 4) * 512 + 512],
                    bank(2 + g % 2), AF.Exp,
                    bias=negm4[:, (g % 4):(g % 4) + 1]),
                    waits=[st[(g, "nm")]], inc="act")
                tk = S.emit("dve", lambda e, g=g: e.reduce_sum(
                    sums4[:, (g % 4) * 4:(g % 4) * 4 + 4],
                    attn_bf[:, (g % 4) * 512:(g % 4) * 512 + 512].rearrange(
                        "p (a b) -> p a b", b=128), axis=AX.X),
                    waits=[tk], inc="dve")
                st[(g, "exp")] = tk
                sc_war[g % 2] = tk
            g = tick - 3
            if 0 <= g < G:
                tr = S.emit("dve", lambda e, g=g: e.reciprocal(
                    rec4[:, (g % 4) * 4:(g % 4) * 4 + 4],
                    sums4[:, (g % 4) * 4:(g % 4) * 4 + 4]),
                    waits=[st[(g, "exp")]], inc="dve")
                tk = None
                for i in range(4):
                    tk = S.emit("dve", lambda e, g=g, i=i: e.tensor_scalar_mul(
                        attn_s[:, (g % 2) * 512 + i * 128:(g % 2) * 512 + (i + 1) * 128],
                        attn_bf[:, (g % 4) * 512 + i * 128:(g % 4) * 512 + (i + 1) * 128],
                        rec4[:, (g % 4) * 4 + i:(g % 4) * 4 + i + 1]),
                        waits=[tr, st.get((g - 2, "atT"))], inc="dve")
                st[(g, "sca")] = tk
            g = tick - 4
            if 0 <= g < G:
                tk = None
                for i in range(4):
                    tk = S.emit("pe", lambda e, g=g, i=i: e.matmul(
                        bank(4 + g % 2, i * 128, 128),
                        attn_s[:, (g % 2) * 512 + i * 128:(g % 2) * 512 + (i + 1) * 128],
                        ident_bf[:], start=True, stop=True, skip_group_check=True),
                        waits=[st[(g, "sca")], at_war[g % 2]], inc="pe")
                st[(g, "atT")] = tk
                tk = None
                for i in range(4):
                    tk = S.emit("act", lambda e, g=g, i=i: e.activation(
                        attnT_sb[:, (g % 2) * 512 + i * 128:(g % 2) * 512 + (i + 1) * 128],
                        bank(4 + g % 2, i * 128, 128), AF.Copy),
                        waits=[st[(g, "atT")], st.get((g - 2, "av"))], inc="act")
                st[(g, "atE")] = tk
                at_war[g % 2] = tk
            g = tick - 5
            if 0 <= g < G:
                tb = (g // 16) % 2
                tk = None
                for i in range(4):
                    t = 4 * g + i
                    l = (g % 16) * 4 + i
                    tk = S.emit("pe", lambda e, i=i, g=g, t=t, l=l, tb=tb: e.matmul(
                        bank(6 + tb, l * 8, 8),
                        attnT_sb[:, (g % 2) * 512 + i * 128:(g % 2) * 512 + (i + 1) * 128],
                        V_dm[:, 8 * t:8 * t + 8], start=True, stop=True, skip_group_check=True),
                        waits=[st[(g, "atE")], o8_war[tb]], inc="pe")
                st[(g, "av")] = tk
                t_last_av = tk
                if g % 16 == 15:
                    tbk = g // 16
                    te = S.emit("act", lambda e, tb=tb, tbk=tbk: e.activation(
                        o8sb[tbk % 2][:], bank(6 + tb), AF.Copy),
                        waits=[tk, o8sb_war[tbk % 2]], inc="act")
                    tt4 = None
                    for j in range(4):
                        tt4 = S.emit("pe", lambda e, tb=tb, tbk=tbk, j=j: e.matmul(
                            bank(6 + tb, j * 128, 128),
                            o8sb[tbk % 2][:, j * 128:(j + 1) * 128], ident_f[:],
                            start=True, stop=True, skip_group_check=True), waits=[te, t_idf], inc="pe")
                    o8sb_war[tbk % 2] = tt4
                    tv = S.emit("dve", lambda e, tb=tb, tbk=tbk: e.tensor_copy(
                        o8stg[tbk % 2][:], bank(6 + tb)),
                        waits=[tt4, o8stg_war[tbk % 2]], inc="dve")
                    o8_war[tb] = tv
                    src = o8stg[tbk % 2][:].rearrange("p (j d) -> p j d", d=128)
                    dst = o8_d.rearrange("(b j p) d -> b p j d", j=4, p=128)[tbk]
                    tpk = S.emit("pool", lambda e, src=src, dst=dst: e.dma_start(
                        dst, src), waits=[tv], inc="dpk%d" % (tbk % 2), inc_n=16)
                    o8stg_war[tbk % 2] = tpk
                    park_final[tpk[0]] = tpk
            passA_next(2)
        t_p2_parks = list(park_final.values())

    if "p3" in phases:
        passA_next(len(pa_pieces))
        tev = None
        for mi in range(3):
            ca = mi * 2 + pa["par"][mi]
            tm = S.emit("pe", lambda e, ca=ca, mi=mi: e.matmul(
                bank(0, mi, 1)[0:1, :], acc[:, ca:ca + 1], cst[:, 0:1],
                start=True, stop=True, skip_group_check=True),
                waits=[pa["acc_tok"][mi], t_last_av], inc="pe")
            tev = S.emit("dve", lambda e, mi=mi: e.tensor_copy(
                pstg[0:1, mi:mi + 1], bank(0, mi, 1)[0:1, :]),
                waits=[tm, t_pstg0], inc="dve")
        def pub(e):
            q = e.partition_id() % 2
            return e.dma_start(bass.AP(wsc_h, q * 4, [[4, 1], [1, 4]]), pstg[0:1, :])
        tpub = S.emit("pool", pub, waits=[tev], inc="dwsc", inc_n=16)
        S.emit("pool", lambda e: e.wait_ge(sems["dwsc"], 16), raw=True)

    for nm, sbuf_t in (("xnT", xnT), ("QS_dm", QS_dm), ("K_dm", K_dm), ("V_dm", V_dm)):
        if nm in dump_d:
            S.emit("pool", lambda e, d=dump_d[nm], s=sbuf_t: e.dma_start(
                d, s[:]), waits=[t_p1_all, t_xnT_done, t_last_av], inc="ddump", inc_n=16)
    if "p2" in phases:
        S.emit("pool", lambda e: [e.wait_ge(sems[k], v) for k, v in
                                  {kk: vv for kk, vv in S.count.items()
                                   if kk.startswith("dpk")}.items()], raw=True)
    p12.close()

    t_quant_ready = None
    if "p3" in phases:
        S.seg = 1
        qwes = ExitStack()
        wsc_sb = mksb(qwes, "wsc_sb", 1, 8, F32)
        sbf = [mksb(qwes, f"sbf{i}", 128, 4096, F32) for i in range(3)]
        tqa1 = mksb(qwes, "tqa1", 128, 2048, BF16)
        tqa2 = mksb(qwes, "tqa2", 128, 2048, BF16)
        tqd1 = mksb(qwes, "tqd1", 128, 2048, BF16)
        tqd2 = mksb(qwes, "tqd2", 128, 2048, BF16)
        qg = [mksb(qwes, f"qg{i}", 128, 2048, BF16) for i in range(4)]
        stgw = [mksb(qwes, f"stgw{i}", 128, 8192, BF16) for i in range(3)]

        trd = S.emit("pool", lambda e: e.dma_start(
            wsc_sb[0:1, :], wsc.rearrange("a b -> (a b)").unsqueeze(0)),
            inc="dwsc", inc_n=16)
        ttot = S.emit("dve", lambda e: e.tensor_add(
            cst[0:1, 12:15], wsc_sb[0:1, 0:3], wsc_sb[0:1, 4:7]),
            waits=[trd], inc="dve")
        tsc = []
        for mi in range(3):
            tsc.append(S.emit("dve", lambda e, mi=mi: e.tensor_scalar(
                cst[0:1, 9 + mi:10 + mi], cst[0:1, 12 + mi:13 + mi],
                0.5 / NW, 0.5e-5, ALU.mult, ALU.add), waits=[ttot], inc="dve"))
        tsu = S.emit("dve", lambda e: e.tensor_scalar_mul(
            cst[0:1, 15:16], cst[0:1, 13:14], 1.0 / NW), waits=[ttot], inc="dve")
        tsg = S.emit("dve", lambda e: e.tensor_scalar_mul(
            cst[0:1, 13:14], cst[0:1, 12:13], 0.25 / NW),
            waits=[tsu], inc="dve")
        tsd = S.emit("dve", lambda e: e.tensor_scalar_mul(
            cst[0:1, 14:15], cst[0:1, 14:15], 0.5 / NW),
            waits=[tsc[2]], inc="dve")
        tguP = S.emit("dve", lambda e: e.tensor_mul(
            cst[0:1, 15:16], cst[0:1, 15:16], cst[0:1, 13:14]),
            waits=[tsu, tsg], inc="dve")
        bc = [(9, 1, tsc[0]), (10, 3, tsc[1]), (11, 5, tsc[2]),
              (15, 7, tguP), (14, 8, tsd)]
        tbc = []
        for sc_, dc, wt in bc:
            tm = S.emit("pe", lambda e, sc_=sc_, dc=dc: e.matmul(
                bank(1, dc, 1), ones1[0:1, :], cst[0:1, sc_:sc_ + 1],
                start=True, stop=True, skip_group_check=True), waits=[wt], inc="pe")
            tbc.append(S.emit("dve", lambda e, dc=dc: e.tensor_copy(
                cst[:, dc:dc + 1], bank(1, dc, 1)), waits=[tm], inc="dve"))
        tn1 = S.emit("dve", lambda e: e.tensor_scalar_mul(
            cst[:, 2:3], cst[:, 1:2], -1.0), waits=[tbc[0]], inc="dve")
        tn2 = S.emit("dve", lambda e: e.tensor_scalar_mul(
            cst[:, 4:5], cst[:, 3:4], -1.0), waits=[tbc[1]], inc="dve")
        tn3 = S.emit("dve", lambda e: e.tensor_scalar_mul(
            cst[:, 6:7], cst[:, 5:6], -1.0), waits=[tbc[2]], inc="dve")
        t_quant_ready = S.emit("dve", lambda e: e.engine_nop(),
                               waits=[tn1, tn2, tn3, tbc[3], tbc[4]], inc="dve")

        sb_tok = [None, None, None]
        qg_tok = [None, None, None, None]
        tr_war = {4: None, 5: None}
        stg_tok = [None, None, None]
        last_comb = [None, None]
        cki = [0]
        sbi = [0]
        wri = [0]
        last_wr = {}
        pid_cache = {}

        def myq(e):
            if id(e) not in pid_cache:
                pid_cache[id(e)] = e.snap(e.partition_id() % 2,
                                          min_val=0, max_val=1)
            return pid_cache[id(e)]

        def qw_load(src, width):
            sl = sbi[0] % 3
            qsem = sbi[0] % 6
            sbi[0] += 1
            t = S.emit("sp", lambda e, src=src, sl=sl, width=width: e.dma_start(
                sbf[sl][:, 0:width], src), waits=[sb_tok[sl]],
                inc="dsbf%d" % qsem, inc_n=16)
            return sl, t

        def qw_quant(sl, off, width, ccol, nccol, wdma):
            k = cki[0]
            cki[0] += 1
            path = k % 2
            qs = path * 2 + (k // 2) % 2
            if path == 0:
                ta = S.emit("act", lambda e, sl=sl, off=off, width=width, c=nccol: e.sign(
                    tqa1[:, 0:width], sbf[sl][:, off:off + width],
                    bias=cst[:, c:c + 1]),
                    waits=[wdma, t_quant_ready, last_comb[0]], inc="act")
                tb = S.emit("act", lambda e, sl=sl, off=off, width=width, c=ccol: e.sign(
                    tqa2[:, 0:width], sbf[sl][:, off:off + width],
                    bias=cst[:, c:c + 1]), inc="act")
                tc = S.emit("pool", lambda e, width=width, qs=qs: e.tensor_add(
                    qg[qs][:, 0:width], tqa1[:, 0:width], tqa2[:, 0:width]),
                    waits=[ta, tb, qg_tok[qs]], inc="pool")
                last_comb[0] = tc
            else:
                ta = S.emit("dve", lambda e, sl=sl, off=off, width=width, c=ccol: e.tensor_scalar(
                    tqd1[:, 0:width], sbf[sl][:, off:off + width],
                    cst[:, c:c + 1], 2.0, ALU.is_gt, ALU.mult),
                    waits=[wdma, t_quant_ready, last_comb[1]], inc="dve")
                tb = S.emit("dve", lambda e, sl=sl, off=off, width=width, c=nccol: e.tensor_scalar(
                    tqd2[:, 0:width], sbf[sl][:, off:off + width],
                    cst[:, c:c + 1], -2.0, ALU.is_lt, ALU.mult), inc="dve")
                tc = S.emit("pool", lambda e, width=width, qs=qs: e.tensor_add(
                    qg[qs][:, 0:width], tqd1[:, 0:width], tqd2[:, 0:width]),
                    waits=[ta, tb, qg_tok[qs]], inc="pool")
                last_comb[1] = tc
            return qs, tc

        def qw_transposes(qs, nblk, sg, stoff, tq):
            tpe = None
            ev = None
            for j0 in range(0, nblk, 4):
                bk_ = 4 + (j0 // 4) % 2
                nn = min(4, nblk - j0)
                tts = None
                for j in range(j0, j0 + nn):
                    tts = S.emit("pe", lambda e, qs=qs, j=j, bk_=bk_, j0=j0: e.matmul(
                        bank(bk_, (j - j0) * 128, 128),
                        qg[qs][:, j * 128:(j + 1) * 128], ident_bf[:],
                        start=True, stop=True, skip_group_check=True),
                        waits=[tq, tr_war[bk_], t_id], inc="pe")
                tpe = tts
                ev = S.emit("dve", lambda e, bk_=bk_, nn=nn, sg=sg,
                            c0=stoff + j0 * 128: e.tensor_copy(
                    stgw[sg][:, c0:c0 + nn * 128], bank(bk_, 0, nn * 128)),
                    waits=[tts, stg_tok[sg]], inc="dve")
                tr_war[bk_] = ev
            return tpe, ev

        def qw_write(dst_c0, width, ev):
            q_ = wri[0] % 4
            sg = wri[0] % 3
            wri[0] += 1
            tw = S.emit("pool", lambda e, c0=dst_c0, width=width, sg=sg: e.dma_start(
                wloc[:, c0:c0 + width], stgw[sg][:, 0:width]),
                waits=[ev, last_wr.get(q_)], inc="dwr%d" % q_, inc_n=16)
            stg_tok[sg] = tw
            last_wr[q_] = tw
            return tw

        def biggu(icl0, icl1, sem):
            def cp(e, icl0=icl0, icl1=icl1):
                q = myq(e)
                nc_ = (icl1 - icl0) * 8192
                dst = bass.AP(wsh_h, q * (2 * MYC) + icl0 * 8192,
                              [[2 * FAMC, 128], [1, nc_]])
                src = bass.AP(wloc_h, icl0 * 8192, [[3 * MYC, 128], [1, nc_]])
                return e.dma_start(dst, src)
            S.emit("pool", cp, waits=list(last_wr.values()), inc=sem, inc_n=16)

        for icl in range(43):
            sg = wri[0] % 3
            ev = None
            for fam, wsrc, ccol, nccol in ((0, wg_d, 1, 2), (1, wu_d, 3, 4)):
                sl, td = qw_load(wsrc[icl * 128:(icl + 1) * 128, :], DM)
                for hf in range(2):
                    qs, tc = qw_quant(sl, hf * 2048, 2048, ccol, nccol, td)
                    tpe, ev = qw_transposes(qs, 16, sg, fam * 4096 + hf * 2048, tc)
                    qg_tok[qs] = tpe
                    sb_tok[sl] = tc
            qw_write(icl * 8192, 8192, ev)
        biggu(0, 43, "dsh0")

        def bigdn(hc0, hc1, sem):
            def cp(e, hc0=hc0, hc1=hc1):
                q = myq(e)
                dst = bass.AP(wshd_h, q * (43 * 128) + hc0 * (86 * 128),
                              [[FAMC, 128], [86 * 128, hc1 - hc0], [1, 43 * 128]])
                src = bass.AP(wloc_h, 2 * MYC + hc0 * (43 * 128),
                              [[3 * MYC, 128], [43 * 128, hc1 - hc0],
                               [1, 43 * 128]])
                return e.dma_start(dst, src)
            S.emit("pool", cp, waits=list(last_wr.values()), inc=sem, inc_n=16)

        for hc in range(32):
            sg = wri[0] % 3
            ev = None
            sl_a, td_a = qw_load(wd_d[hc * 128:(hc + 1) * 128, 0:4096], 4096)
            sl_b, td_b = qw_load(wd_d[hc * 128:(hc + 1) * 128, 4096:HI], HI - 4096)
            for pc in range(3):
                sl, td = (sl_a, td_a) if pc < 2 else (sl_b, td_b)
                off = (pc % 2) * 2048 if pc < 2 else 0
                w = 2048 if pc < 2 else HI - 4096
                qs, tc = qw_quant(sl, off, w, 5, 6, td)
                tpe, ev = qw_transposes(qs, w // 128, sg, pc * 2048, tc)
                qg_tok[qs] = tpe
                sb_tok[sl] = tc
            qw_write(2 * MYC + hc * 5504, 5504, ev)
        bigdn(0, 32, "dsh1")
        S.emit("pool", lambda e: [e.wait_ge(sems["dsh%d" % f],
                                            S.count["dsh%d" % f])
                                  for f in range(2)], raw=True)
        qwes.close()
        S.seg = 2

    t_ffn_parks = None
    if "p4" in phases:
        p4es = ExitStack()
        interT = mksb(p4es, "interT", 128, 22 * T, BF16)
        slabGU = [mksb(p4es, f"slabGU{i}", 128, 8192, BF16) for i in range(2)]
        slabD = [mksb(p4es, f"slabD{i}", 128, 2816, BF16) for i in range(2)]
        gtmp = mksb(p4es, "gtmp", 128, 1024, BF16)
        ptile = [mksb(p4es, f"pt{i}", 128, 1024, F32) for i in range(2)]
        dstg = [mksb(p4es, f"dstg{i}", 128, 1024, F32) for i in range(2)]
        fbuf = [mksb(p4es, f"fbuf{i}", 128, 1024, F32) for i in range(8)]

        gu_war = {0: None, 1: None, 2: None, 3: None}
        dn_war = {4: None, 5: None, 6: None, 7: None}
        tr_war = {0: None, 1: None}
        slab_tok = {0: None, 1: None}
        slabD_tok = [None, None]
        gtmp_tok = [None, None]
        ptile_tok = [None, None]
        dstg_tok = [None, None]
        fbuf_tok = [None] * 8
        fbuf_ev = [None] * 8
        dff_last = {0: None, 1: None}
        down_read_done = None
        last_park = {}
        ffn_park = {}

        def prep_gu(ic):
            par = ic % 2
            tok = S.emit("sp", lambda e, ic=ic, par=par: e.dma_start(
                slabGU[par][:], wsh[:, ic * 8192:(ic + 1) * 8192]),
                waits=[slab_tok[par]], inc="dslGU%d" % par, inc_n=16)
            return tok, par

        def comp_gu(b, lc, ctx):
            tok, par = ctx
            mmg = {}
            mmu = {}
            for hb in range(32):
                for c in range(NCOL):
                    tw = 512
                    rhs = xnT[:, hb * T + c * 512: hb * T + c * 512 + tw]
                    mmg[c] = S.emit("pe", lambda e, hb=hb, c=c, rhs=rhs, par=par: e.matmul(
                        bank(c, 0, tw), slabGU[par][:, hb * 128:(hb + 1) * 128], rhs,
                        start=(hb == 0), stop=(hb == 31), skip_group_check=True),
                        waits=[tok, gu_war[c]], inc="pe")
                for c in range(NCOL):
                    tw = 512
                    rhs = xnT[:, hb * T + c * 512: hb * T + c * 512 + tw]
                    mmu[c] = S.emit("pe", lambda e, hb=hb, c=c, rhs=rhs, par=par: e.matmul(
                        bank(2 + c, 0, tw),
                        slabGU[par][:, 4096 + hb * 128:4096 + (hb + 1) * 128], rhs,
                        start=(hb == 0), stop=(hb == 31), skip_group_check=True),
                        waits=[tok, gu_war[2 + c]], inc="pe")
            slab_tok[par] = mmu[NCOL - 1]
            for c in range(NCOL):
                tw = 512
                tgv = S.emit("dve", lambda e, c=c, tw=tw: e.tensor_scalar(
                    gtmp[:, c * 512:c * 512 + tw], bank(c, 0, tw),
                    cst[:, 7:8], None, ALU.mult),
                    waits=[mmg[c], gtmp_tok[c], t_quant_ready], inc="dve")
                gu_war[c] = tgv
                ti = S.emit("dve", lambda e, c=c, lc=lc, tw=tw: e.tensor_mul(
                    interT[:, lc * T + c * 512: lc * T + c * 512 + tw],
                    gtmp[:, c * 512:c * 512 + tw], bank(2 + c, 0, tw)),
                    waits=[tgv, mmu[c], dn_read[b]], inc="dve")
                gu_war[2 + c] = ti
                gtmp_tok[c] = ti
                inter_tok[(b, lc)] = ti

        def prep_dn(b, hc):
            dsl = hc % 2
            nch = BLK_CH[b]
            c0 = (hc * 86 + BLK_OFF[b]) * 128
            src = wshd[:, c0:c0 + nch * 128]
            dst = slabD[dsl][:, 0:nch * 128]
            t = S.emit("sp", lambda e, src=src, dst=dst: e.dma_start(dst, src),
                       waits=[slabD_tok[dsl]], inc="dsld%d" % (hc % 4), inc_n=16)
            return t, dsl

        def comp_dn(b, hc, ctx):
            tld, dsl = ctx
            nch = BLK_CH[b]
            t_inter_all = tmax(*[inter_tok[(b, l)] for l in range(nch)])
            last_dn = None
            mmc = {}
            dbkc = {}
            for c in range(NCOL):
                dbkc[c] = 4 + (hc * NCOL + c) % 4
            for j in range(nch):
                for c in range(NCOL):
                    tw = 512
                    dbk = dbkc[c]
                    mm = S.emit("pe", lambda e, j=j, c=c, dbk=dbk, dsl=dsl, tw=tw: e.matmul(
                        bank(dbk, 0, tw), slabD[dsl][:, j * 128:(j + 1) * 128],
                        interT[:, j * T + c * 512: j * T + c * 512 + tw],
                        start=(j == 0), stop=(j == nch - 1), skip_group_check=True),
                        waits=[tld, t_inter_all, dn_war[dbk]], inc="pe")
                    mmc[c] = mm
                    last_dn = mm
            if b == 0:
                evs = []
                for c in range(NCOL):
                    tev = S.emit("act", lambda e, dbk=dbkc[c], dsl=dsl, c=c: e.activation(
                        dstg[dsl][:, c * 512:(c + 1) * 512], bank(dbk), AF.Copy),
                        waits=[mmc[c], dstg_tok[dsl]], inc="act")
                    dn_war[dbkc[c]] = tev
                    evs.append(tev)
                tp = S.emit("pool", lambda e, hc=hc, dsl=dsl: e.dma_start(
                    part_d[hc * 128:(hc + 1) * 128, :], dstg[dsl][:]),
                    waits=evs, inc="ddst%d" % dsl, inc_n=16)
                dstg_tok[dsl] = tp
                last_park[hc] = tp
            else:
                tl = S.emit("pool", lambda e, hc=hc, dsl=dsl: e.dma_start(
                    ptile[dsl][:], part_d[hc * 128:(hc + 1) * 128, :]),
                    waits=[ptile_tok[dsl], last_park[hc]],
                    inc="dpt%d" % dsl, inc_n=16)
                tas = []
                for c in range(NCOL):
                    ta = S.emit("dve", lambda e, dbk=dbkc[c], dsl=dsl, c=c: e.tensor_add(
                        dstg[dsl][:, c * 512:(c + 1) * 512], bank(dbk),
                        ptile[dsl][:, c * 512:(c + 1) * 512]),
                        waits=[mmc[c], tl, dstg_tok[dsl]], inc="dve")
                    dn_war[dbkc[c]] = ta
                    tas.append(ta)
                ptile_tok[dsl] = tas[-1]
                if b < 3:
                    tp = S.emit("pool", lambda e, hc=hc, dsl=dsl: e.dma_start(
                        part_d[hc * 128:(hc + 1) * 128, :], dstg[dsl][:]),
                        waits=tas, inc="ddst%d" % dsl, inc_n=16)
                    dstg_tok[dsl] = tp
                    last_park[hc] = tp
                else:
                    last_tmf = None
                    for c in range(NCOL):
                        for tbj in range(4):
                            bkf = tbj % 2
                            fi = c * 4 + tbj
                            tmf = S.emit("pe", lambda e, c=c, tbj=tbj, dsl=dsl, bkf=bkf: e.matmul(
                                bank(bkf, 0, 128),
                                dstg[dsl][:, c * 512 + tbj * 128:c * 512 + (tbj + 1) * 128],
                                ident_f[:], start=True, stop=True, skip_group_check=True),
                                waits=[tas[c], tr_war[bkf], t_idf], inc="pe")
                            last_tmf = tmf
                            tef = S.emit("dve", lambda e, fi=fi, hc=hc, bkf=bkf: e.tensor_scalar(
                                fbuf[fi][:, (hc % 8) * 128:(hc % 8 + 1) * 128],
                                bank(bkf, 0, 128), cst[:, 8:9], None, ALU.mult),
                                waits=[tmf, fbuf_tok[fi]], inc="dve")
                            tr_war[bkf] = tef
                            fbuf_ev[fi] = tef
                    dstg_tok[dsl] = last_tmf
                    if hc % 8 == 7:
                        g8 = hc // 8
                        for c in range(NCOL):
                            for tbj in range(4):
                                fi = c * 4 + tbj
                                qi = fi % 2
                                tw_ = S.emit("pool", lambda e, c=c, tbj=tbj, fi=fi, g8=g8: e.dma_start(
                                    ffn_d[c * 512 + tbj * 128:c * 512 + (tbj + 1) * 128,
                                          g8 * 1024:(g8 + 1) * 1024],
                                    fbuf[fi][:]),
                                    waits=[fbuf_ev[fi], dff_last[qi]],
                                    inc="dff%d" % qi, inc_n=16)
                                fbuf_tok[fi] = tw_
                                dff_last[qi] = tw_
                                ffn_park[tw_[0]] = tw_
            slabD_tok[dsl] = last_dn
            dn_read[b + 1] = last_dn

        inter_tok = {}
        dn_read = {0: None, 1: None, 2: None, 3: None, 4: None}
        stages = []
        for b in range(4):
            stages += [("gu", b, lc) for lc in range(BLK_CH[b])]
            stages += [("dn", b, hc) for hc in range(32)]

        ctxs = [None] * len(stages)
        for i in range(len(stages) + 1):
            if i >= 1:
                kind, b, idx = stages[i - 1]
                if kind == "gu":
                    comp_gu(b, idx, ctxs[i - 1])
                else:
                    comp_dn(b, idx, ctxs[i - 1])
            if i < len(stages):
                kind, b, idx = stages[i]
                ctxs[i] = prep_gu(BLK_OFF[b] + idx) if kind == "gu" else prep_dn(b, idx)
        t_ffn_parks = list(ffn_park.values())
        for nm, sb_t in (("interT", interT), ("gtmp", gtmp)):
            if nm in dump_d:
                S.emit("pool", lambda e, d=dump_d[nm], s_=sb_t: e.dma_start(
                    d, s_[:]), waits=list(t_ffn_parks), inc="ddump", inc_n=16)
        p4es.close()

    if "p5" in phases:
        ffparks = t_ffn_parks if isinstance(t_ffn_parks, list) else [t_ffn_parks]
        o8parks = t_p2_parks if isinstance(t_p2_parks, list) else [t_p2_parks]
        p5es = ExitStack()
        fx = [mksb(p5es, f"fx{i}", 128, DM, F32) for i in range(2)]
        ff = [mksb(p5es, f"ff{i}", 128, DM, F32) for i in range(2)]
        fo8 = [mksb(p5es, f"fo8{i}", 128, 1024, F32) for i in range(2)]
        facc = [mksb(p5es, f"facc{i}", 128, DM, F32) for i in range(2)]
        fout = [mksb(p5es, f"fout{i}", 128, DM, F32) for i in range(2)]
        ft = [None, None]
        fa = [None, None]
        for tt in range(T // 128):
            p = tt % 2
            t1 = S.emit("pool", lambda e, tt=tt, p=p: e.dma_start(
                fx[p][:], x_d[tt * 128:(tt + 1) * 128, :]),
                waits=[fa[p]] + ffparks, inc="dp5x%d" % p, inc_n=16)
            t2 = S.emit("pool", lambda e, tt=tt, p=p: e.dma_start(
                ff[p][:], ffn_d[tt * 128:(tt + 1) * 128, :]),
                waits=[fa[p]] + ffparks, inc="dp5f%d" % p, inc_n=16)
            src = o8_d.rearrange("(t h) d -> t h d", h=8)[tt * 128:(tt + 1) * 128]
            t3 = S.emit("pool", lambda e, p=p, src=src: e.dma_start(
                fo8[p][:].rearrange("p (h d) -> p h d", d=128), src),
                waits=[fa[p]] + o8parks, inc="dp5o%d" % p, inc_n=16)
            t4 = S.emit("dve", lambda e, p=p: e.tensor_add(
                facc[p][:], fx[p][:], ff[p][:]), waits=[t1, t2, fa[p]], inc="dve")
            tj = None
            for j in range(4):
                ov = fout[p][:].rearrange("p (d g j) -> p d g j", g=8, j=4)[:, :, :, j]
                av_ = facc[p][:].rearrange("p (d g j) -> p d g j", g=8, j=4)[:, :, :, j]
                o8v = fo8[p][:].rearrange("p (h d) -> p d h", d=128)
                tj = S.emit("dve", lambda e, ov=ov, av_=av_, o8v=o8v: e.tensor_add(
                    ov, av_, o8v), waits=[t4, t3] + ([ft[p]] if j == 0 else []),
                    inc="dve")
            fa[p] = tj
            t5 = S.emit("pool", lambda e, tt=tt, p=p: e.dma_start(
                out_d[tt * 128:(tt + 1) * 128, :], fout[p][:]),
                waits=[tj], inc="dout%d" % p, inc_n=16)
            ft[p] = t5
        p5es.close()

    for nm, sb_t in (("cst", cst), ("acc", acc)):
        if nm in dump_d:
            S.emit("pool", lambda e, d=dump_d[nm], s_=sb_t: e.dma_start(d, s_[:]),
                   waits=[t_quant_ready], inc="ddump", inc_n=16)
    final_counts = {k: v for k, v in S.count.items() if k.startswith("d")}
    S.emit("pool", lambda e: [e.wait_ge(sems[k], v) for k, v in final_counts.items()],
           raw=True)

    used_sems = set(S.count)
    for ops in S.ops.values():
        for _sg, ws, _, _, _ in ops:
            for s_, _v in ws:
                used_sems.add(s_)
    for s_ in sorted(used_sems):
        sems[s_] = es.enter_context(nc.semaphore(s_))

    def replay(eng_name, seg):
        def go(e):
            for sg, ws, fn, inc, inc_n in S.ops[eng_name]:
                if sg != seg:
                    continue
                for s_, v in ws:
                    e.wait_ge(sems[s_], v)
                instr = fn(e)
                if inc is not None and instr is not None:
                    instr.then_inc(sems[inc], inc_n)
        return go

    nseg = S.seg + 1
    for seg in range(nseg):
        with nc.Block() as block:
            block.sync(replay("sp", seg))
            block.tensor(replay("pe", seg))
            block.vector(replay("dve", seg))
            block.scalar(replay("act", seg))
            block.gpsimd(replay("pool", seg))
        if seg < nseg - 1 and bar:
            nc.all_core_barrier()

    es.close()
    return nc


def make_in_maps(inputs, n_cores=8):
    def cvt(a, shp=None):
        a = np.asarray(a)
        if shp is not None:
            a = a.reshape(shp)
        return np.ascontiguousarray(a.astype(np.float32, copy=False))
    x = cvt(inputs["L0_attn_x"], (8192, 4096))
    wq = cvt(inputs["L0_attn_Wq"], (4096, 4096))
    bq = cvt(inputs["L0_attn_bq"])
    wk = cvt(inputs["L0_attn_Wk"], (4096, 1024))
    bk = cvt(inputs["L0_attn_bk"])
    wv = cvt(inputs["L0_attn_Wv"], (4096, 1024))
    bv = cvt(inputs["L0_attn_bv"])
    wg = cvt(inputs["L0_ffn_w_gate"])
    wu = cvt(inputs["L0_ffn_w_up"])
    wd = cvt(inputs["L0_ffn_w_down"])
    HI = 11008 // 2
    maps = []
    for c in range(n_cores):
        q = c % 2
        maps.append({
            "x": x[c * T:(c + 1) * T], "wq": wq, "bq": bq, "wk": wk, "bk": bk,
            "wv": wv, "bv": bv,
            "wg": np.ascontiguousarray(wg[q * HI:(q + 1) * HI]),
            "wu": np.ascontiguousarray(wu[q * HI:(q + 1) * HI]),
            "wd": np.ascontiguousarray(wd[:, q * HI:(q + 1) * HI]),
        })
    return maps


def kernel(**inputs):
    from concourse.bass_utils import run_bass_kernel_spmd
    nc = build_program()
    maps = make_in_maps(inputs)
    res = run_bass_kernel_spmd(nc, maps, core_ids=list(range(8)))
    out = np.concatenate([np.asarray(res.results[c]["out"]) for c in range(8)], axis=0)
    return out.reshape(2, 4096, 4096)


# version 11
# speedup vs baseline: 1.0081x; 1.0081x over previous
import numpy as np
from contextlib import ExitStack

import concourse.bass as bass
import concourse.mybir as mybir

F32 = mybir.dt.float32
BF16 = mybir.dt.bfloat16
AF = mybir.ActivationFunctionType
ALU = mybir.AluOpType
AX = mybir.AxisListType

T = 1024
DM = 4096
EC = DM // 128
NCOL = T // 512
IFFN = 11008
BLK_CH = [22, 22, 22, 20]
BLK_OFF = [0, 22, 44, 66]
NW = float(IFFN * DM)
SCL = 0.08838834764831845

ENGINES = ("sp", "pe", "dve", "act", "pool")


class Sched:
    SELF_EDGE = ("dve", "act", "pool")

    def __init__(self):
        self.seg = 0
        self.ops = {e: [] for e in ENGINES}
        self.count = {}
        self.lastwait = {e: {} for e in ENGINES}
        self.serialize = False
        self.last_tok = {}
        self.prev_tok = {e: None for e in ENGINES}

    def emit(self, eng, fn, waits=(), inc=None, inc_n=1, raw=False):
        is_dma = (inc is not None and inc.endswith("dma")) or raw
        if not is_dma and eng in self.SELF_EDGE:
            if inc is None:
                inc = eng
                inc_n = 1
            waits = list(waits) + [self.prev_tok[eng]]
        if self.serialize and not raw:
            waits = list(waits) + [t for e2, t in self.last_tok.items() if e2 != eng]
            if inc is None:
                inc = {"sp": "spdma", "pool": "pooldma"}.get(eng, eng)
                inc_n = 16 if inc.endswith("dma") else 1
        ws = []
        for w in waits:
            if w is None:
                continue
            s, v = w
            lw = self.lastwait[eng].get(s, 0)
            if v > lw:
                ws.append((s, v))
                self.lastwait[eng][s] = v
        tok = None
        if inc is not None:
            self.count[inc] = self.count.get(inc, 0) + inc_n
            tok = (inc, self.count[inc])
        if self.serialize and tok is not None:
            self.last_tok[eng] = tok
        if not is_dma and eng in self.SELF_EDGE and tok is not None:
            self.prev_tok[eng] = tok
        self.ops[eng].append((self.seg, ws, fn, inc, inc_n))
        return tok


def tmax(*toks):
    best = None
    for t in toks:
        if t is None:
            continue
        if best is None or t[1] > best[1]:
            best = t
    return best


def build_program(debug=(), phases=("p0", "p1", "p2", "p3", "p4", "p5"), ndev=8, rc=True, bar=True):
    nc = bass.Bass("TRN2", target_bir_lowering=False, debug=False, num_devices=ndev, detect_race_conditions=rc)
    nc.reset()
    dbg = set(debug)

    x_d = nc.dram_tensor("x", [T, DM], F32, kind="ExternalInput").ap()
    wq_d = nc.dram_tensor("wq", [DM, DM], F32, kind="ExternalInput").ap()
    bq_d = nc.dram_tensor("bq", [128, 32], F32, kind="ExternalInput").ap()
    wk_d = nc.dram_tensor("wk", [DM, 1024], F32, kind="ExternalInput").ap()
    bk_d = nc.dram_tensor("bk", [128, 8], F32, kind="ExternalInput").ap()
    wv_d = nc.dram_tensor("wv", [DM, 1024], F32, kind="ExternalInput").ap()
    bv_d = nc.dram_tensor("bv", [128, 8], F32, kind="ExternalInput").ap()
    HI = IFFN // 2
    wg_d = nc.dram_tensor("wg", [HI, DM], F32, kind="ExternalInput").ap()
    wu_d = nc.dram_tensor("wu", [HI, DM], F32, kind="ExternalInput").ap()
    wd_d = nc.dram_tensor("wd", [DM, HI], F32, kind="ExternalInput").ap()
    FAMC = 86 * 32 * 128
    wsh_h = nc.dram_tensor("wsh", [128, 2 * FAMC], BF16, kind="Internal",
                           addr_space="Shared")
    wsh = wsh_h.ap()
    wshd_h = nc.dram_tensor("wshd", [128, FAMC], BF16, kind="Internal",
                            addr_space="Shared")
    wshd = wshd_h.ap()
    wsc_h = nc.dram_tensor("wsc", [2, 4], F32, kind="Internal", addr_space="Shared")
    wsc = wsc_h.ap()
    MYC = 43 * 32 * 128
    wloc_h = nc.dram_tensor("wloc", [128, 3 * MYC], BF16, kind="Internal")
    wloc = wloc_h.ap()
    out_d = nc.dram_tensor("out", [T, DM], F32, kind="ExternalOutput").ap()

    def scratch(name, shape):
        kind = "ExternalOutput" if name in dbg else "Internal"
        return nc.dram_tensor(name, shape, F32, kind=kind).ap()

    o8_d = scratch("o8_scr", [T * 8, 128])
    wc_d = nc.dram_tensor("wqkv_scr", [3 * DM, 1024], BF16, kind="Internal").ap()
    part_d = scratch("part_scr", [DM, T])
    ffn_d = scratch("ffn_scr", [T, DM])
    dump_d = {}
    for nm, w in (("xnT", EC * T), ("QS_dm", T * 8), ("K_dm", T * 8), ("V_dm", T * 8),
                  ("cst", 16), ("acc", 8), ("interT", 22 * T),
                  ("gtmp", 1024)):
        if nm in dbg:
            dump_d[nm] = nc.dram_tensor("dump_" + nm, [128, w], F32,
                                        kind="ExternalOutput").ap()

    S = Sched()
    es = ExitStack()

    def mksb(stack, name, p, f, dt):
        return stack.enter_context(nc.sbuf_tensor(name, [p, f], dt))

    ident_bf = mksb(es, "ident_bf", 128, 128, BF16)
    ident_f = mksb(es, "ident_f", 128, 128, F32)
    xnT = mksb(es, "xnT", 128, EC * T, BF16)
    bq_sb = mksb(es, "bq_sb", 128, 32, F32)
    bk_sb = mksb(es, "bk_sb", 128, 8, F32)
    bv_sb = mksb(es, "bv_sb", 128, 8, F32)
    bqs = mksb(es, "bqs", 128, 8, F32)
    bscr = mksb(es, "bscr", 128, 16, F32)
    cst = mksb(es, "cst", 128, 16, F32)
    ones1 = mksb(es, "ones1", 1, 128, F32)
    acc = mksb(es, "acc", 128, 8, F32)
    epsc = mksb(es, "epsc", 128, 2, F32)
    pstg = mksb(es, "pstg", 1, 4, F32)

    pbt = es.enter_context(nc.psum_tensor("pb", [128, 4096], F32))

    def bank(k, lo=0, n=512):
        return pbt[:, k * 512 + lo: k * 512 + lo + n]

    sems = {}

    S.emit("pool", lambda e: e.memset(ident_bf[:], 0.0), inc="pool")
    t_id = S.emit("pool", lambda e: e.affine_select(
        out=ident_bf[:], in_=ident_bf[:], compare_op=ALU.not_equal, fill=1.0,
        base=0, pattern=[[-1, 128]], channel_multiplier=1), inc="pool")
    S.emit("pool", lambda e: e.memset(ident_f[:], 0.0), inc="pool")
    t_idf = S.emit("pool", lambda e: e.affine_select(
        out=ident_f[:], in_=ident_f[:], compare_op=ALU.not_equal, fill=1.0,
        base=0, pattern=[[-1, 128]], channel_multiplier=1), inc="pool")
    S.emit("pool", lambda e: e.memset(cst[:, 0:1], 1.0), inc="pool")
    S.emit("pool", lambda e: e.memset(ones1[:], 1.0), inc="pool")
    t_acc0 = S.emit("pool", lambda e: e.memset(acc[:], 0.0), inc="pool")
    t_pstg0 = S.emit("pool", lambda e: e.memset(pstg[:], 0.0), inc="pool")
    S.emit("pool", lambda e: e.memset(epsc[:, 0:1], 1e-5), inc="pool")
    t_eps = S.emit("pool", lambda e: e.memset(epsc[:, 1:2], 0.5e-5), inc="pool")
    t_bq = S.emit("pool", lambda e: e.dma_start(bq_sb[:], bq_d), inc="dbias", inc_n=16)
    S.emit("pool", lambda e: e.dma_start(bk_sb[:], bk_d), inc="dbias", inc_n=16)
    t_bv = S.emit("pool", lambda e: e.dma_start(bv_sb[:], bv_d), inc="dbias", inc_n=16)
    bqv = bq_sb[:].rearrange("p (h j) -> p h j", j=4)
    S.emit("dve", lambda e: e.tensor_add(bscr[:, 0:8], bqv[:, :, 0], bqv[:, :, 1]),
           waits=[t_bv], inc="dve")
    t_b1 = S.emit("dve", lambda e: e.tensor_add(bscr[:, 8:16], bqv[:, :, 2], bqv[:, :, 3]),
                  inc="dve")
    t_b2 = S.emit("dve", lambda e: e.tensor_add(bscr[:, 0:8], bscr[:, 0:8], bscr[:, 8:16]),
                  waits=[t_b1], inc="dve")
    t_bqs = S.emit("dve", lambda e: e.tensor_scalar_mul(bqs[:], bscr[:, 0:8], SCL),
                   waits=[t_b2], inc="dve")

    p12 = ExitStack()
    pab = []
    QS_dm = mksb(p12, "QS_dm", 128, T * 8, BF16)
    K_dm = mksb(p12, "K_dm", 128, T * 8, BF16)
    V_dm = mksb(p12, "V_dm", 128, T * 8, BF16)
    qkT = [mksb(p12, f"qkT{i}", 8, 1024, BF16) for i in range(2)]
    attn_bf = mksb(p12, "attn_bf", 128, 4 * 512, BF16)
    attn_s = mksb(p12, "attn_s", 128, 2 * 512, BF16)
    attnT_sb = mksb(p12, "attnT_sb", 128, 2 * 512, BF16)
    sums4 = mksb(p12, "sums4", 128, 16, F32)
    negm4 = mksb(p12, "negm4", 128, 16, F32)
    rec4 = mksb(p12, "rec4", 128, 16, F32)
    o8sb = [mksb(p12, f"o8sb{i}", 128, 512, F32) for i in range(2)]
    o8stg = [mksb(p12, f"o8stg{i}", 128, 512, F32) for i in range(2)]

    pa_pieces = []
    if "p3" in phases:
        for mi, wsrc in ((0, wg_d), (1, wu_d)):
            for ch in range(HI // 128):
                pa_pieces.append((wsrc[ch * 128:(ch + 1) * 128, :], DM, mi))
        for hc in range(32):
            pa_pieces.append((wd_d[hc * 128:(hc + 1) * 128, 0:4096], 4096, 2))
            pa_pieces.append((wd_d[hc * 128:(hc + 1) * 128, 4096:HI],
                              HI - 4096, 2))
    pa = {"i": 0, "slot_tok": [None, None], "par": [0, 0, 0],
          "acc_tok": [t_acc0, t_acc0, t_acc0], "scr_tok": None}

    def passA_next(n):
        for _ in range(n):
            i = pa["i"]
            if i >= len(pa_pieces):
                return
            pa["i"] = i + 1
            src, w, mi = pa_pieces[i]
            sl = i % 2
            td = S.emit("sp", lambda e, src=src, sl=sl, w=w: e.dma_start(
                pab[sl][:, 0:w], src), waits=[pa["slot_tok"][sl]],
                inc="dpab%d" % (i % 4), inc_n=16)
            tras = []
            for k0 in range(0, w, 2048):
                k1 = min(w, k0 + 2048)
                tras.append(S.emit("act", lambda e, sl=sl, k0=k0, k1=k1,
                                   kk=len(tras): e.activation(
                    pab[sl][:, k0:k1], pab[sl][:, k0:k1], AF.Abs,
                    accum_out=acc[:, 6 + kk:7 + kk]),
                    waits=[td, pa["scr_tok"]], inc="act"))
            if len(tras) > 1:
                tr = S.emit("dve", lambda e: e.tensor_add(
                    acc[:, 6:7], acc[:, 6:7], acc[:, 7:8]),
                    waits=tras, inc="dve")
            else:
                tr = tras[0]
            par = pa["par"][mi]
            ca, cb = mi * 2 + par, mi * 2 + (1 - par)
            ta = S.emit("dve", lambda e, ca=ca, cb=cb: e.tensor_add(
                acc[:, cb:cb + 1], acc[:, ca:ca + 1], acc[:, 6:7]),
                waits=[tr, pa["acc_tok"][mi]], inc="dve")
            pa["par"][mi] = 1 - par
            pa["acc_tok"][mi] = ta
            pa["slot_tok"][sl] = ta
            pa["scr_tok"] = ta

    t_xnT_done = None
    if "p0" in phases:
        p0es = ExitStack()
        xbuf = [mksb(p0es, f"xb{i}", 128, DM, F32) for i in range(2)]
        xn_bf = [mksb(p0es, f"xn{i}", 128, DM, BF16) for i in range(2)]
        stats = mksb(p0es, "lnst", 128, 8, F32)
        slot_tok = [None, None]
        xn_war = [None, None]
        evac_tok = {0: None, 1: None}
        for tt in range(T // 128):
            p = tt % 2
            td = S.emit("pool", lambda e, tt=tt, p=p: e.dma_start(
                xbuf[p][:], x_d[tt * 128:(tt + 1) * 128, :]),
                waits=[slot_tok[p]], inc="dxb%d" % p, inc_n=16)
            t1 = S.emit("dve", lambda e, p=p: e.reduce_sum(
                stats[:, 4:5], xbuf[p][:], axis=AX.X), waits=[td], inc="dve")
            t2 = S.emit("dve", lambda e, p=p: e.tensor_scalar_mul(
                stats[:, p:p + 1], stats[:, 4:5], -1.0 / DM), waits=[t1], inc="dve")
            t3a = S.emit("act", lambda e, p=p: e.activation(
                bank(2, 0, 2048), xbuf[p][:, 0:2048], AF.Square,
                bias=stats[:, p:p + 1], accum_out=stats[:, 2 + p:3 + p]),
                waits=[t2], inc="act")
            t3b = S.emit("act", lambda e, p=p: e.activation(
                bank(2, 0, 2048), xbuf[p][:, 2048:4096], AF.Square,
                bias=stats[:, p:p + 1], accum_out=stats[:, 6:7]),
                waits=[t3a], inc="act")
            t3c = S.emit("dve", lambda e, p=p: e.tensor_add(
                stats[:, 2 + p:3 + p], stats[:, 2 + p:3 + p], stats[:, 6:7]),
                waits=[t3b], inc="dve")
            t4 = S.emit("act", lambda e, p=p: e.activation(
                stats[:, 2 + p:3 + p], stats[:, 2 + p:3 + p], AF.Ln,
                bias=epsc[:, 0:1], scale=1.0 / DM), waits=[t3c, t_eps], inc="act")
            t5 = S.emit("act", lambda e, p=p: e.activation(
                stats[:, 2 + p:3 + p], stats[:, 2 + p:3 + p], AF.Exp,
                scale=-0.5), waits=[t4], inc="act")
            t6 = S.emit("dve", lambda e, p=p: e.tensor_scalar(
                xn_bf[p][:], xbuf[p][:], stats[:, p:p + 1], stats[:, 2 + p:3 + p],
                ALU.add, ALU.mult), waits=[t5, xn_war[p]], inc="dve")
            slot_tok[p] = t6
            last_tr = None
            for q in range(8):
                bk_ = q % 2
                tts = None
                for c in range(4 * q, 4 * q + 4):
                    tts = S.emit("pe", lambda e, p=p, c=c, bk_=bk_: e.matmul(
                        bank(bk_, (c % 4) * 128, 128)[0:128, :],
                        xn_bf[p][:, c * 128:(c + 1) * 128], ident_bf[:],
                        start=True, stop=True, skip_group_check=True),
                        waits=[t6, t_id, evac_tok[bk_]], inc="pe")
                last_tr = tts
                dst = xnT[:].rearrange("p (c t) -> p c t", t=T)[
                    :, 4 * q:4 * q + 4, tt * 128:(tt + 1) * 128]
                if q % 2 == 0:
                    evac_tok[bk_] = S.emit("dve", lambda e, bk_=bk_, dst=dst: e.tensor_copy(
                        dst, bank(bk_)[:].rearrange("p (c t) -> p c t", t=128)),
                        waits=[tts], inc="dve")
                else:
                    evac_tok[bk_] = S.emit("act", lambda e, bk_=bk_, dst=dst: e.activation(
                        dst, bank(bk_)[:].rearrange("p (c t) -> p c t", t=128),
                        AF.Copy), waits=[tts], inc="act")
            xn_war[p] = last_tr
        t_xnT_done = tmax(evac_tok[0], evac_tok[1])
        p0es.close()
    pab.extend(mksb(p12, f"pab{i}", 128, 4096, F32) for i in range(2))

    t_p1_all = None
    if "p1" in phases:
        p1es = ExitStack()
        wbuf = [mksb(p1es, f"wb{i}", 128, DM, F32) for i in range(2)]
        tmp1 = mksb(p1es, "tmp1", 128, 1024, F32)
        wbf = [mksb(p1es, f"wbf{i}", 128, 1024, BF16) for i in range(2)]
        wslot_tok = [None, None]
        wbf_tok = {0: [], 1: []}
        wpark = {}
        bank_evac = {h: None for h in range(8)}
        for col in range(NCOL):
            tw = 512
            for mat, (wsrc, bias_sb, bscl) in enumerate((
                    (wq_d, bqs, SCL), (wk_d, bk_sb, 1.0), (wv_d, bv_sb, 1.0))):
                dm_dst = (QS_dm, K_dm, V_dm)[mat]
                mm_toks = [None] * 8
                for ec in range(EC):
                    p = ec % 2
                    wide = DM if mat == 0 else 1024
                    if col == 0:
                        td = S.emit("sp", lambda e, p=p, ec=ec, wsrc=wsrc, wide=wide: e.dma_start(
                            wbuf[p][:, 0:wide], wsrc[ec * 128:(ec + 1) * 128, :]),
                            waits=[wslot_tok[p], t_xnT_done], inc="dwb%d" % p, inc_n=16)
                        if mat == 0:
                            wv4 = wbuf[p][:, 0:DM].rearrange("p (c j) -> p c j", j=4)
                            S.emit("dve", lambda e, wv4=wv4: e.tensor_add(
                                tmp1[:], wv4[:, :, 0], wv4[:, :, 1]), waits=[td], inc="dve")
                            S.emit("dve", lambda e, wv4=wv4: e.tensor_add(
                                tmp1[:], tmp1[:], wv4[:, :, 2]), inc="dve")
                            S.emit("dve", lambda e, wv4=wv4: e.tensor_add(
                                tmp1[:], tmp1[:], wv4[:, :, 3]), inc="dve")
                            tc = S.emit("dve", lambda e, p=p: e.tensor_copy(
                                wbf[p][:], tmp1[:]), waits=wbf_tok[p], inc="dve")
                        else:
                            tc = S.emit("dve", lambda e, p=p: e.tensor_copy(
                                wbf[p][:], wbuf[p][:, 0:1024]),
                                waits=[td] + wbf_tok[p], inc="dve")
                        wslot_tok[p] = tc
                        tpk = S.emit("pool", lambda e, p=p, mat=mat, ec=ec: e.dma_start(
                            wc_d[mat * DM + ec * 128: mat * DM + (ec + 1) * 128, :],
                            wbf[p][:]), waits=[tc], inc="dwc%d" % p, inc_n=16)
                        wpark[(mat, ec)] = tpk
                    else:
                        tc = S.emit("sp", lambda e, p=p, mat=mat, ec=ec: e.dma_start(
                            wbf[p][:],
                            wc_d[mat * DM + ec * 128: mat * DM + (ec + 1) * 128, :]),
                            waits=wbf_tok[p] + [wpark[(mat, ec)], wslot_tok[p]],
                            inc="dwb%d" % p, inc_n=16)
                        wslot_tok[p] = tc
                    for hp in range(8):
                        lhs = wbf[p][:, hp:1024:8]
                        rhs = xnT[:, ec * T + col * 512: ec * T + col * 512 + tw]
                        mm_toks[hp] = S.emit("pe", lambda e, lhs=lhs, rhs=rhs, hp=hp, ec=ec: e.matmul(
                            bank(hp, 0, tw), lhs, rhs,
                            start=(ec == 0), stop=(ec == EC - 1), skip_group_check=True),
                            waits=[tc, bank_evac[hp]], inc="pe")
                    wbf_tok[p] = [mm_toks[7]] + ([wpark[(mat, ec)]] if col == 0 else [])
                    if ec % 8 == 3:
                        passA_next(1)
                for hp in range(8):
                    dst = dm_dst[:].rearrange("p (t h) -> p t h", h=8)[
                        :, col * 512:(col + 1) * 512, hp]
                    bias_ap = bias_sb[:, hp:hp + 1]
                    bank_evac[hp] = S.emit("dve", lambda e, hp=hp, dst=dst,
                                           bias_ap=bias_ap, bscl=bscl, tw=tw: e.tensor_scalar(
                        dst, bank(hp, 0, tw), bscl, bias_ap, ALU.mult, ALU.add),
                        waits=[mm_toks[hp], t_bv, t_bqs], inc="dve")
        t_p1_all = bank_evac[7]
        p1es.close()

    t_p2_parks = None
    park_final = {}
    t_last_av = None
    if "p2" in phases:
        G = T // 4
        qkT_tok = [None, None]
        trq_war = None
        sc_war = [None, None]
        at_war = [None, None]
        o8_war = [None, None]
        o8sb_war = [None, None]
        o8stg_war = [None, None]
        st = {}
        for tick in range(G + 6):
            g = tick
            if g < G:
                tk = None
                for i in range(4):
                    t = 4 * g + i
                    S.emit("pe", lambda e, t=t, i=i: e.matmul(
                        bank(i // 2, (i % 2) * 256, 128)[0:8, :],
                        QS_dm[:, 8 * t:8 * t + 8], ident_bf[:], start=True, stop=True, skip_group_check=True),
                        waits=[t_p1_all, trq_war, t_id], inc="pe")
                    tk = S.emit("pe", lambda e, t=t, i=i: e.matmul(
                        bank(i // 2, (i % 2) * 256 + 128, 128)[0:8, :],
                        K_dm[:, 8 * t:8 * t + 8], ident_bf[:], start=True, stop=True, skip_group_check=True),
                        inc="pe")
                st[(g, "tp")] = tk
                st[(g, "qke")] = S.emit("dve", lambda e, g=g: e.tensor_copy(
                    qkT[g % 2][0:8, :], bank(0, 0, 1024)[0:8, :]),
                    waits=[st[(g, "tp")], qkT_tok[g % 2]], inc="dve")
                trq_war = st[(g, "qke")]
            g = tick - 1
            if 0 <= g < G:
                tk = None
                for i in range(4):
                    tk = S.emit("pe", lambda e, g=g, i=i: e.matmul(
                        bank(2 + g % 2, i * 128, 128),
                        qkT[g % 2][0:8, i * 256:i * 256 + 128],
                        qkT[g % 2][0:8, i * 256 + 128:i * 256 + 256],
                        start=True, stop=True, skip_group_check=True),
                        waits=[st[(g, "qke")], sc_war[g % 2]], inc="pe")
                st[(g, "sc")] = tk
                qkT_tok[g % 2] = tk
                st[(g, "nm")] = S.emit("dve", lambda e, g=g: e.reduce_max(
                    negm4[:, (g % 4):(g % 4) + 1], bank(2 + g % 2),
                    axis=AX.X, negate=True), waits=[st[(g, "sc")]], inc="dve")
            g = tick - 2
            if 0 <= g < G:
                tk = S.emit("act", lambda e, g=g: e.activation(
                    attn_bf[:, (g % 4) * 512:(g % 4) * 512 + 512],
                    bank(2 + g % 2), AF.Exp,
                    bias=negm4[:, (g % 4):(g % 4) + 1]),
                    waits=[st[(g, "nm")]], inc="act")
                tk = S.emit("dve", lambda e, g=g: e.reduce_sum(
                    sums4[:, (g % 4) * 4:(g % 4) * 4 + 4],
                    attn_bf[:, (g % 4) * 512:(g % 4) * 512 + 512].rearrange(
                        "p (a b) -> p a b", b=128), axis=AX.X),
                    waits=[tk], inc="dve")
                st[(g, "exp")] = tk
                sc_war[g % 2] = tk
            g = tick - 3
            if 0 <= g < G:
                tr = S.emit("dve", lambda e, g=g: e.reciprocal(
                    rec4[:, (g % 4) * 4:(g % 4) * 4 + 4],
                    sums4[:, (g % 4) * 4:(g % 4) * 4 + 4]),
                    waits=[st[(g, "exp")]], inc="dve")
                tk = None
                for i in range(4):
                    tk = S.emit("dve", lambda e, g=g, i=i: e.tensor_scalar_mul(
                        attn_s[:, (g % 2) * 512 + i * 128:(g % 2) * 512 + (i + 1) * 128],
                        attn_bf[:, (g % 4) * 512 + i * 128:(g % 4) * 512 + (i + 1) * 128],
                        rec4[:, (g % 4) * 4 + i:(g % 4) * 4 + i + 1]),
                        waits=[tr, st.get((g - 2, "atT"))], inc="dve")
                st[(g, "sca")] = tk
            g = tick - 4
            if 0 <= g < G:
                tk = None
                for i in range(4):
                    tk = S.emit("pe", lambda e, g=g, i=i: e.matmul(
                        bank(4 + g % 2, i * 128, 128),
                        attn_s[:, (g % 2) * 512 + i * 128:(g % 2) * 512 + (i + 1) * 128],
                        ident_bf[:], start=True, stop=True, skip_group_check=True),
                        waits=[st[(g, "sca")], at_war[g % 2]], inc="pe")
                st[(g, "atT")] = tk
                tk = None
                for i in range(4):
                    tk = S.emit("act", lambda e, g=g, i=i: e.activation(
                        attnT_sb[:, (g % 2) * 512 + i * 128:(g % 2) * 512 + (i + 1) * 128],
                        bank(4 + g % 2, i * 128, 128), AF.Copy),
                        waits=[st[(g, "atT")], st.get((g - 2, "av"))], inc="act")
                st[(g, "atE")] = tk
                at_war[g % 2] = tk
            g = tick - 5
            if 0 <= g < G:
                tb = (g // 16) % 2
                tk = None
                for i in range(4):
                    t = 4 * g + i
                    l = (g % 16) * 4 + i
                    tk = S.emit("pe", lambda e, i=i, g=g, t=t, l=l, tb=tb: e.matmul(
                        bank(6 + tb, l * 8, 8),
                        attnT_sb[:, (g % 2) * 512 + i * 128:(g % 2) * 512 + (i + 1) * 128],
                        V_dm[:, 8 * t:8 * t + 8], start=True, stop=True, skip_group_check=True),
                        waits=[st[(g, "atE")], o8_war[tb]], inc="pe")
                st[(g, "av")] = tk
                t_last_av = tk
                if g % 16 == 15:
                    tbk = g // 16
                    te = S.emit("act", lambda e, tb=tb, tbk=tbk: e.activation(
                        o8sb[tbk % 2][:], bank(6 + tb), AF.Copy),
                        waits=[tk, o8sb_war[tbk % 2]], inc="act")
                    tt4 = None
                    for j in range(4):
                        tt4 = S.emit("pe", lambda e, tb=tb, tbk=tbk, j=j: e.matmul(
                            bank(6 + tb, j * 128, 128),
                            o8sb[tbk % 2][:, j * 128:(j + 1) * 128], ident_f[:],
                            start=True, stop=True, skip_group_check=True), waits=[te, t_idf], inc="pe")
                    o8sb_war[tbk % 2] = tt4
                    tv = S.emit("dve", lambda e, tb=tb, tbk=tbk: e.tensor_copy(
                        o8stg[tbk % 2][:], bank(6 + tb)),
                        waits=[tt4, o8stg_war[tbk % 2]], inc="dve")
                    o8_war[tb] = tv
                    src = o8stg[tbk % 2][:].rearrange("p (j d) -> p j d", d=128)
                    dst = o8_d.rearrange("(b j p) d -> b p j d", j=4, p=128)[tbk]
                    tpk = S.emit("pool", lambda e, src=src, dst=dst: e.dma_start(
                        dst, src), waits=[tv], inc="dpk%d" % (tbk % 2), inc_n=16)
                    o8stg_war[tbk % 2] = tpk
                    park_final[tpk[0]] = tpk
            passA_next(2)
        t_p2_parks = list(park_final.values())

    if "p3" in phases:
        passA_next(len(pa_pieces))
        tev = None
        for mi in range(3):
            ca = mi * 2 + pa["par"][mi]
            tm = S.emit("pe", lambda e, ca=ca, mi=mi: e.matmul(
                bank(0, mi, 1)[0:1, :], acc[:, ca:ca + 1], cst[:, 0:1],
                start=True, stop=True, skip_group_check=True),
                waits=[pa["acc_tok"][mi], t_last_av], inc="pe")
            tev = S.emit("dve", lambda e, mi=mi: e.tensor_copy(
                pstg[0:1, mi:mi + 1], bank(0, mi, 1)[0:1, :]),
                waits=[tm, t_pstg0], inc="dve")
        def pub(e):
            q = e.partition_id() % 2
            return e.dma_start(bass.AP(wsc_h, q * 4, [[4, 1], [1, 4]]), pstg[0:1, :])
        tpub = S.emit("pool", pub, waits=[tev], inc="dwsc", inc_n=16)
        S.emit("pool", lambda e: e.wait_ge(sems["dwsc"], 16), raw=True)

    for nm, sbuf_t in (("xnT", xnT), ("QS_dm", QS_dm), ("K_dm", K_dm), ("V_dm", V_dm)):
        if nm in dump_d:
            S.emit("pool", lambda e, d=dump_d[nm], s=sbuf_t: e.dma_start(
                d, s[:]), waits=[t_p1_all, t_xnT_done, t_last_av], inc="ddump", inc_n=16)
    if "p2" in phases:
        S.emit("pool", lambda e: [e.wait_ge(sems[k], v) for k, v in
                                  {kk: vv for kk, vv in S.count.items()
                                   if kk.startswith("dpk")}.items()], raw=True)
    p12.close()

    t_quant_ready = None
    if "p3" in phases:
        S.seg = 1
        qwes = ExitStack()
        wsc_sb = mksb(qwes, "wsc_sb", 1, 8, F32)
        sbf = [mksb(qwes, f"sbf{i}", 128, 2048, F32) for i in range(6)]
        tqa1 = mksb(qwes, "tqa1", 128, 2048, BF16)
        tqa2 = mksb(qwes, "tqa2", 128, 2048, BF16)
        tqd1 = mksb(qwes, "tqd1", 128, 2048, BF16)
        tqd2 = mksb(qwes, "tqd2", 128, 2048, BF16)
        qg = [mksb(qwes, f"qg{i}", 128, 2048, BF16) for i in range(4)]
        stgw = [mksb(qwes, f"stgw{i}", 128, 8192, BF16) for i in range(3)]

        trd = S.emit("pool", lambda e: e.dma_start(
            wsc_sb[0:1, :], wsc.rearrange("a b -> (a b)").unsqueeze(0)),
            inc="dwsc", inc_n=16)
        ttot = S.emit("dve", lambda e: e.tensor_add(
            cst[0:1, 12:15], wsc_sb[0:1, 0:3], wsc_sb[0:1, 4:7]),
            waits=[trd], inc="dve")
        tsc = []
        for mi in range(3):
            tsc.append(S.emit("dve", lambda e, mi=mi: e.tensor_scalar(
                cst[0:1, 9 + mi:10 + mi], cst[0:1, 12 + mi:13 + mi],
                0.5 / NW, 0.5e-5, ALU.mult, ALU.add), waits=[ttot], inc="dve"))
        tsu = S.emit("dve", lambda e: e.tensor_scalar_mul(
            cst[0:1, 15:16], cst[0:1, 13:14], 1.0 / NW), waits=[ttot], inc="dve")
        tsg = S.emit("dve", lambda e: e.tensor_scalar_mul(
            cst[0:1, 13:14], cst[0:1, 12:13], 0.25 / NW),
            waits=[tsu], inc="dve")
        tsd = S.emit("dve", lambda e: e.tensor_scalar_mul(
            cst[0:1, 14:15], cst[0:1, 14:15], 0.5 / NW),
            waits=[tsc[2]], inc="dve")
        tguP = S.emit("dve", lambda e: e.tensor_mul(
            cst[0:1, 15:16], cst[0:1, 15:16], cst[0:1, 13:14]),
            waits=[tsu, tsg], inc="dve")
        bc = [(9, 1, tsc[0]), (10, 3, tsc[1]), (11, 5, tsc[2]),
              (15, 7, tguP), (14, 8, tsd)]
        tbc = []
        for sc_, dc, wt in bc:
            tm = S.emit("pe", lambda e, sc_=sc_, dc=dc: e.matmul(
                bank(1, dc, 1), ones1[0:1, :], cst[0:1, sc_:sc_ + 1],
                start=True, stop=True, skip_group_check=True), waits=[wt], inc="pe")
            tbc.append(S.emit("dve", lambda e, dc=dc: e.tensor_copy(
                cst[:, dc:dc + 1], bank(1, dc, 1)), waits=[tm], inc="dve"))
        tn1 = S.emit("dve", lambda e: e.tensor_scalar_mul(
            cst[:, 2:3], cst[:, 1:2], -1.0), waits=[tbc[0]], inc="dve")
        tn2 = S.emit("dve", lambda e: e.tensor_scalar_mul(
            cst[:, 4:5], cst[:, 3:4], -1.0), waits=[tbc[1]], inc="dve")
        tn3 = S.emit("dve", lambda e: e.tensor_scalar_mul(
            cst[:, 6:7], cst[:, 5:6], -1.0), waits=[tbc[2]], inc="dve")
        t_quant_ready = S.emit("dve", lambda e: e.engine_nop(),
                               waits=[tn1, tn2, tn3, tbc[3], tbc[4]], inc="dve")

        sb_tok = [None] * 6
        qg_tok = [None, None, None, None]
        tr_war = {4: None, 5: None}
        stg_tok = [None, None, None]
        last_comb = [None, None]
        cki = [0]
        sbi = [0]
        wri = [0]
        last_wr = {}
        pid_cache = {}

        def myq(e):
            if id(e) not in pid_cache:
                pid_cache[id(e)] = e.snap(e.partition_id() % 2,
                                          min_val=0, max_val=1)
            return pid_cache[id(e)]

        def qw_load(src, width):
            sl = sbi[0] % 6
            sbi[0] += 1
            t = S.emit("sp", lambda e, src=src, sl=sl, width=width: e.dma_start(
                sbf[sl][:, 0:width], src), waits=[sb_tok[sl]],
                inc="dsbf%d" % sl, inc_n=16)
            return sl, t

        def qw_quant(sl, off, width, ccol, nccol, wdma):
            k = cki[0]
            cki[0] += 1
            path = k % 2
            qs = path * 2 + (k // 2) % 2
            if path == 0:
                ta = S.emit("act", lambda e, sl=sl, off=off, width=width, c=nccol: e.sign(
                    tqa1[:, 0:width], sbf[sl][:, off:off + width],
                    bias=cst[:, c:c + 1]),
                    waits=[wdma, t_quant_ready, last_comb[0]], inc="act")
                tb = S.emit("act", lambda e, sl=sl, off=off, width=width, c=ccol: e.sign(
                    tqa2[:, 0:width], sbf[sl][:, off:off + width],
                    bias=cst[:, c:c + 1]), inc="act")
                tc = S.emit("pool", lambda e, width=width, qs=qs: e.tensor_add(
                    qg[qs][:, 0:width], tqa1[:, 0:width], tqa2[:, 0:width]),
                    waits=[ta, tb, qg_tok[qs]], inc="pool")
                last_comb[0] = tc
            else:
                ta = S.emit("dve", lambda e, sl=sl, off=off, width=width, c=ccol: e.tensor_scalar(
                    tqd1[:, 0:width], sbf[sl][:, off:off + width],
                    cst[:, c:c + 1], 2.0, ALU.is_gt, ALU.mult),
                    waits=[wdma, t_quant_ready, last_comb[1]], inc="dve")
                tb = S.emit("dve", lambda e, sl=sl, off=off, width=width, c=nccol: e.tensor_scalar(
                    tqd2[:, 0:width], sbf[sl][:, off:off + width],
                    cst[:, c:c + 1], -2.0, ALU.is_lt, ALU.mult), inc="dve")
                tc = S.emit("pool", lambda e, width=width, qs=qs: e.tensor_add(
                    qg[qs][:, 0:width], tqd1[:, 0:width], tqd2[:, 0:width]),
                    waits=[ta, tb, qg_tok[qs]], inc="pool")
                last_comb[1] = tc
            return qs, tc

        def qw_transposes(qs, nblk, sg, stoff, tq):
            tpe = None
            ev = None
            for j0 in range(0, nblk, 4):
                bk_ = 4 + (j0 // 4) % 2
                nn = min(4, nblk - j0)
                tts = None
                for j in range(j0, j0 + nn):
                    tts = S.emit("pe", lambda e, qs=qs, j=j, bk_=bk_, j0=j0: e.matmul(
                        bank(bk_, (j - j0) * 128, 128),
                        qg[qs][:, j * 128:(j + 1) * 128], ident_bf[:],
                        start=True, stop=True, skip_group_check=True),
                        waits=[tq, tr_war[bk_], t_id], inc="pe")
                tpe = tts
                ev = S.emit("dve", lambda e, bk_=bk_, nn=nn, sg=sg,
                            c0=stoff + j0 * 128: e.tensor_copy(
                    stgw[sg][:, c0:c0 + nn * 128], bank(bk_, 0, nn * 128)),
                    waits=[tts, stg_tok[sg]], inc="dve")
                tr_war[bk_] = ev
            return tpe, ev

        def qw_write(dst_c0, width, ev):
            q_ = wri[0] % 4
            sg = wri[0] % 3
            wri[0] += 1
            tw = S.emit("pool", lambda e, c0=dst_c0, width=width, sg=sg: e.dma_start(
                wloc[:, c0:c0 + width], stgw[sg][:, 0:width]),
                waits=[ev, last_wr.get(q_)], inc="dwr%d" % q_, inc_n=16)
            stg_tok[sg] = tw
            last_wr[q_] = tw
            return tw

        def biggu(icl0, icl1, sem):
            def cp(e, icl0=icl0, icl1=icl1):
                q = myq(e)
                nc_ = (icl1 - icl0) * 8192
                dst = bass.AP(wsh_h, q * (2 * MYC) + icl0 * 8192,
                              [[2 * FAMC, 128], [1, nc_]])
                src = bass.AP(wloc_h, icl0 * 8192, [[3 * MYC, 128], [1, nc_]])
                return e.dma_start(dst, src)
            S.emit("pool", cp, waits=list(last_wr.values()), inc=sem, inc_n=16)

        for icl in range(43):
            sg = wri[0] % 3
            ev = None
            for fam, wsrc, ccol, nccol in ((0, wg_d, 1, 2), (1, wu_d, 3, 4)):
                for hf in range(2):
                    sl, td = qw_load(
                        wsrc[icl * 128:(icl + 1) * 128,
                             hf * 2048:(hf + 1) * 2048], 2048)
                    qs, tc = qw_quant(sl, 0, 2048, ccol, nccol, td)
                    tpe, ev = qw_transposes(qs, 16, sg, fam * 4096 + hf * 2048, tc)
                    qg_tok[qs] = tpe
                    sb_tok[sl] = tc
            qw_write(icl * 8192, 8192, ev)
        biggu(0, 43, "dsh0")

        def bigdn(hc0, hc1, sem):
            def cp(e, hc0=hc0, hc1=hc1):
                q = myq(e)
                dst = bass.AP(wshd_h, q * (43 * 128) + hc0 * (86 * 128),
                              [[FAMC, 128], [86 * 128, hc1 - hc0], [1, 43 * 128]])
                src = bass.AP(wloc_h, 2 * MYC + hc0 * (43 * 128),
                              [[3 * MYC, 128], [43 * 128, hc1 - hc0],
                               [1, 43 * 128]])
                return e.dma_start(dst, src)
            S.emit("pool", cp, waits=list(last_wr.values()), inc=sem, inc_n=16)

        for hc in range(32):
            sg = wri[0] % 3
            ev = None
            for pc in range(3):
                w0 = pc * 2048
                w1 = min(HI, w0 + 2048)
                sl, td = qw_load(wd_d[hc * 128:(hc + 1) * 128, w0:w1], w1 - w0)
                qs, tc = qw_quant(sl, 0, w1 - w0, 5, 6, td)
                tpe, ev = qw_transposes(qs, (w1 - w0) // 128, sg, pc * 2048, tc)
                qg_tok[qs] = tpe
                sb_tok[sl] = tc
            qw_write(2 * MYC + hc * 5504, 5504, ev)
        bigdn(0, 32, "dsh1")
        S.emit("pool", lambda e: [e.wait_ge(sems["dsh%d" % f],
                                            S.count["dsh%d" % f])
                                  for f in range(2)], raw=True)
        qwes.close()
        S.seg = 2

    t_ffn_parks = None
    if "p4" in phases:
        p4es = ExitStack()
        interT = mksb(p4es, "interT", 128, 22 * T, BF16)
        slabGU = [mksb(p4es, f"slabGU{i}", 128, 8192, BF16) for i in range(2)]
        slabD = [mksb(p4es, f"slabD{i}", 128, 2816, BF16) for i in range(2)]
        gtmp = mksb(p4es, "gtmp", 128, 1024, BF16)
        ptile = [mksb(p4es, f"pt{i}", 128, 1024, F32) for i in range(2)]
        dstg = [mksb(p4es, f"dstg{i}", 128, 1024, F32) for i in range(2)]
        fbuf = [mksb(p4es, f"fbuf{i}", 128, 1024, F32) for i in range(8)]

        gu_war = {0: None, 1: None, 2: None, 3: None}
        dn_war = {4: None, 5: None, 6: None, 7: None}
        tr_war = {0: None, 1: None}
        slab_tok = {0: None, 1: None}
        slabD_tok = [None, None]
        gtmp_tok = [None, None]
        ptile_tok = [None, None]
        dstg_tok = [None, None]
        fbuf_tok = [None] * 8
        fbuf_ev = [None] * 8
        dff_last = {0: None, 1: None}
        down_read_done = None
        last_park = {}
        ffn_park = {}

        def prep_gu(ic):
            par = ic % 2
            tok = S.emit("sp", lambda e, ic=ic, par=par: e.dma_start(
                slabGU[par][:], wsh[:, ic * 8192:(ic + 1) * 8192]),
                waits=[slab_tok[par]], inc="dslGU%d" % par, inc_n=16)
            return tok, par

        def comp_gu(b, lc, ctx):
            tok, par = ctx
            mmg = {}
            mmu = {}
            for hb in range(32):
                for c in range(NCOL):
                    tw = 512
                    rhs = xnT[:, hb * T + c * 512: hb * T + c * 512 + tw]
                    mmg[c] = S.emit("pe", lambda e, hb=hb, c=c, rhs=rhs, par=par: e.matmul(
                        bank(c, 0, tw), slabGU[par][:, hb * 128:(hb + 1) * 128], rhs,
                        start=(hb == 0), stop=(hb == 31), skip_group_check=True),
                        waits=[tok, gu_war[c]], inc="pe")
                for c in range(NCOL):
                    tw = 512
                    rhs = xnT[:, hb * T + c * 512: hb * T + c * 512 + tw]
                    mmu[c] = S.emit("pe", lambda e, hb=hb, c=c, rhs=rhs, par=par: e.matmul(
                        bank(2 + c, 0, tw),
                        slabGU[par][:, 4096 + hb * 128:4096 + (hb + 1) * 128], rhs,
                        start=(hb == 0), stop=(hb == 31), skip_group_check=True),
                        waits=[tok, gu_war[2 + c]], inc="pe")
            slab_tok[par] = mmu[NCOL - 1]
            for c in range(NCOL):
                tw = 512
                tgv = S.emit("dve", lambda e, c=c, tw=tw: e.tensor_scalar(
                    gtmp[:, c * 512:c * 512 + tw], bank(c, 0, tw),
                    cst[:, 7:8], None, ALU.mult),
                    waits=[mmg[c], gtmp_tok[c], t_quant_ready], inc="dve")
                gu_war[c] = tgv
                ti = S.emit("dve", lambda e, c=c, lc=lc, tw=tw: e.tensor_mul(
                    interT[:, lc * T + c * 512: lc * T + c * 512 + tw],
                    gtmp[:, c * 512:c * 512 + tw], bank(2 + c, 0, tw)),
                    waits=[tgv, mmu[c], dn_read[b]], inc="dve")
                gu_war[2 + c] = ti
                gtmp_tok[c] = ti
                inter_tok[(b, lc)] = ti

        def prep_dn(b, hc):
            dsl = hc % 2
            nch = BLK_CH[b]
            c0 = (hc * 86 + BLK_OFF[b]) * 128
            src = wshd[:, c0:c0 + nch * 128]
            dst = slabD[dsl][:, 0:nch * 128]
            t = S.emit("sp", lambda e, src=src, dst=dst: e.dma_start(dst, src),
                       waits=[slabD_tok[dsl]], inc="dsld%d" % (hc % 4), inc_n=16)
            return t, dsl

        def comp_dn(b, hc, ctx):
            tld, dsl = ctx
            nch = BLK_CH[b]
            t_inter_all = tmax(*[inter_tok[(b, l)] for l in range(nch)])
            last_dn = None
            mmc = {}
            dbkc = {}
            for c in range(NCOL):
                dbkc[c] = 4 + (hc * NCOL + c) % 4
            for j in range(nch):
                for c in range(NCOL):
                    tw = 512
                    dbk = dbkc[c]
                    mm = S.emit("pe", lambda e, j=j, c=c, dbk=dbk, dsl=dsl, tw=tw: e.matmul(
                        bank(dbk, 0, tw), slabD[dsl][:, j * 128:(j + 1) * 128],
                        interT[:, j * T + c * 512: j * T + c * 512 + tw],
                        start=(j == 0), stop=(j == nch - 1), skip_group_check=True),
                        waits=[tld, t_inter_all, dn_war[dbk]], inc="pe")
                    mmc[c] = mm
                    last_dn = mm
            if b == 0:
                evs = []
                for c in range(NCOL):
                    tev = S.emit("act", lambda e, dbk=dbkc[c], dsl=dsl, c=c: e.activation(
                        dstg[dsl][:, c * 512:(c + 1) * 512], bank(dbk), AF.Copy),
                        waits=[mmc[c], dstg_tok[dsl]], inc="act")
                    dn_war[dbkc[c]] = tev
                    evs.append(tev)
                tp = S.emit("pool", lambda e, hc=hc, dsl=dsl: e.dma_start(
                    part_d[hc * 128:(hc + 1) * 128, :], dstg[dsl][:]),
                    waits=evs, inc="ddst%d" % dsl, inc_n=16)
                dstg_tok[dsl] = tp
                last_park[hc] = tp
            else:
                tl = S.emit("pool", lambda e, hc=hc, dsl=dsl: e.dma_start(
                    ptile[dsl][:], part_d[hc * 128:(hc + 1) * 128, :]),
                    waits=[ptile_tok[dsl], last_park[hc]],
                    inc="dpt%d" % dsl, inc_n=16)
                tas = []
                for c in range(NCOL):
                    ta = S.emit("dve", lambda e, dbk=dbkc[c], dsl=dsl, c=c: e.tensor_add(
                        dstg[dsl][:, c * 512:(c + 1) * 512], bank(dbk),
                        ptile[dsl][:, c * 512:(c + 1) * 512]),
                        waits=[mmc[c], tl, dstg_tok[dsl]], inc="dve")
                    dn_war[dbkc[c]] = ta
                    tas.append(ta)
                ptile_tok[dsl] = tas[-1]
                if b < 3:
                    tp = S.emit("pool", lambda e, hc=hc, dsl=dsl: e.dma_start(
                        part_d[hc * 128:(hc + 1) * 128, :], dstg[dsl][:]),
                        waits=tas, inc="ddst%d" % dsl, inc_n=16)
                    dstg_tok[dsl] = tp
                    last_park[hc] = tp
                else:
                    last_tmf = None
                    for c in range(NCOL):
                        for tbj in range(4):
                            bkf = tbj % 2
                            fi = c * 4 + tbj
                            tmf = S.emit("pe", lambda e, c=c, tbj=tbj, dsl=dsl, bkf=bkf: e.matmul(
                                bank(bkf, 0, 128),
                                dstg[dsl][:, c * 512 + tbj * 128:c * 512 + (tbj + 1) * 128],
                                ident_f[:], start=True, stop=True, skip_group_check=True),
                                waits=[tas[c], tr_war[bkf], t_idf], inc="pe")
                            last_tmf = tmf
                            tef = S.emit("dve", lambda e, fi=fi, hc=hc, bkf=bkf: e.tensor_scalar(
                                fbuf[fi][:, (hc % 8) * 128:(hc % 8 + 1) * 128],
                                bank(bkf, 0, 128), cst[:, 8:9], None, ALU.mult),
                                waits=[tmf, fbuf_tok[fi]], inc="dve")
                            tr_war[bkf] = tef
                            fbuf_ev[fi] = tef
                    dstg_tok[dsl] = last_tmf
                    if hc % 8 == 7:
                        g8 = hc // 8
                        for c in range(NCOL):
                            for tbj in range(4):
                                fi = c * 4 + tbj
                                qi = fi % 2
                                tw_ = S.emit("pool", lambda e, c=c, tbj=tbj, fi=fi, g8=g8: e.dma_start(
                                    ffn_d[c * 512 + tbj * 128:c * 512 + (tbj + 1) * 128,
                                          g8 * 1024:(g8 + 1) * 1024],
                                    fbuf[fi][:]),
                                    waits=[fbuf_ev[fi], dff_last[qi]],
                                    inc="dff%d" % qi, inc_n=16)
                                fbuf_tok[fi] = tw_
                                dff_last[qi] = tw_
                                ffn_park[tw_[0]] = tw_
            slabD_tok[dsl] = last_dn
            dn_read[b + 1] = last_dn

        inter_tok = {}
        dn_read = {0: None, 1: None, 2: None, 3: None, 4: None}
        stages = []
        for b in range(4):
            stages += [("gu", b, lc) for lc in range(BLK_CH[b])]
            stages += [("dn", b, hc) for hc in range(32)]

        ctxs = [None] * len(stages)
        for i in range(len(stages) + 1):
            if i >= 1:
                kind, b, idx = stages[i - 1]
                if kind == "gu":
                    comp_gu(b, idx, ctxs[i - 1])
                else:
                    comp_dn(b, idx, ctxs[i - 1])
            if i < len(stages):
                kind, b, idx = stages[i]
                ctxs[i] = prep_gu(BLK_OFF[b] + idx) if kind == "gu" else prep_dn(b, idx)
        t_ffn_parks = list(ffn_park.values())
        for nm, sb_t in (("interT", interT), ("gtmp", gtmp)):
            if nm in dump_d:
                S.emit("pool", lambda e, d=dump_d[nm], s_=sb_t: e.dma_start(
                    d, s_[:]), waits=list(t_ffn_parks), inc="ddump", inc_n=16)
        p4es.close()

    if "p5" in phases:
        ffparks = t_ffn_parks if isinstance(t_ffn_parks, list) else [t_ffn_parks]
        o8parks = t_p2_parks if isinstance(t_p2_parks, list) else [t_p2_parks]
        p5es = ExitStack()
        fx = [mksb(p5es, f"fx{i}", 128, DM, F32) for i in range(2)]
        ff = [mksb(p5es, f"ff{i}", 128, DM, F32) for i in range(2)]
        fo8 = [mksb(p5es, f"fo8{i}", 128, 1024, F32) for i in range(2)]
        facc = [mksb(p5es, f"facc{i}", 128, DM, F32) for i in range(2)]
        fout = [mksb(p5es, f"fout{i}", 128, DM, F32) for i in range(2)]
        ft = [None, None]
        fa = [None, None]
        for tt in range(T // 128):
            p = tt % 2
            t1 = S.emit("pool", lambda e, tt=tt, p=p: e.dma_start(
                fx[p][:], x_d[tt * 128:(tt + 1) * 128, :]),
                waits=[fa[p]] + ffparks, inc="dp5x%d" % p, inc_n=16)
            t2 = S.emit("pool", lambda e, tt=tt, p=p: e.dma_start(
                ff[p][:], ffn_d[tt * 128:(tt + 1) * 128, :]),
                waits=[fa[p]] + ffparks, inc="dp5f%d" % p, inc_n=16)
            src = o8_d.rearrange("(t h) d -> t h d", h=8)[tt * 128:(tt + 1) * 128]
            t3 = S.emit("pool", lambda e, p=p, src=src: e.dma_start(
                fo8[p][:].rearrange("p (h d) -> p h d", d=128), src),
                waits=[fa[p]] + o8parks, inc="dp5o%d" % p, inc_n=16)
            t4 = S.emit("dve", lambda e, p=p: e.tensor_add(
                facc[p][:], fx[p][:], ff[p][:]), waits=[t1, t2, fa[p]], inc="dve")
            tj = None
            for j in range(4):
                ov = fout[p][:].rearrange("p (d g j) -> p d g j", g=8, j=4)[:, :, :, j]
                av_ = facc[p][:].rearrange("p (d g j) -> p d g j", g=8, j=4)[:, :, :, j]
                o8v = fo8[p][:].rearrange("p (h d) -> p d h", d=128)
                tj = S.emit("dve", lambda e, ov=ov, av_=av_, o8v=o8v: e.tensor_add(
                    ov, av_, o8v), waits=[t4, t3] + ([ft[p]] if j == 0 else []),
                    inc="dve")
            fa[p] = tj
            t5 = S.emit("pool", lambda e, tt=tt, p=p: e.dma_start(
                out_d[tt * 128:(tt + 1) * 128, :], fout[p][:]),
                waits=[tj], inc="dout%d" % p, inc_n=16)
            ft[p] = t5
        p5es.close()

    for nm, sb_t in (("cst", cst), ("acc", acc)):
        if nm in dump_d:
            S.emit("pool", lambda e, d=dump_d[nm], s_=sb_t: e.dma_start(d, s_[:]),
                   waits=[t_quant_ready], inc="ddump", inc_n=16)
    final_counts = {k: v for k, v in S.count.items() if k.startswith("d")}
    S.emit("pool", lambda e: [e.wait_ge(sems[k], v) for k, v in final_counts.items()],
           raw=True)

    used_sems = set(S.count)
    for ops in S.ops.values():
        for _sg, ws, _, _, _ in ops:
            for s_, _v in ws:
                used_sems.add(s_)
    for s_ in sorted(used_sems):
        sems[s_] = es.enter_context(nc.semaphore(s_))

    def replay(eng_name, seg):
        def go(e):
            for sg, ws, fn, inc, inc_n in S.ops[eng_name]:
                if sg != seg:
                    continue
                for s_, v in ws:
                    e.wait_ge(sems[s_], v)
                instr = fn(e)
                if inc is not None and instr is not None:
                    instr.then_inc(sems[inc], inc_n)
        return go

    nseg = S.seg + 1
    for seg in range(nseg):
        with nc.Block() as block:
            block.sync(replay("sp", seg))
            block.tensor(replay("pe", seg))
            block.vector(replay("dve", seg))
            block.scalar(replay("act", seg))
            block.gpsimd(replay("pool", seg))
        if seg < nseg - 1 and bar:
            nc.all_core_barrier()

    es.close()
    return nc


def make_in_maps(inputs, n_cores=8):
    def cvt(a, shp=None):
        a = np.asarray(a)
        if shp is not None:
            a = a.reshape(shp)
        return np.ascontiguousarray(a.astype(np.float32, copy=False))
    x = cvt(inputs["L0_attn_x"], (8192, 4096))
    wq = cvt(inputs["L0_attn_Wq"], (4096, 4096))
    bq = cvt(inputs["L0_attn_bq"])
    wk = cvt(inputs["L0_attn_Wk"], (4096, 1024))
    bk = cvt(inputs["L0_attn_bk"])
    wv = cvt(inputs["L0_attn_Wv"], (4096, 1024))
    bv = cvt(inputs["L0_attn_bv"])
    wg = cvt(inputs["L0_ffn_w_gate"])
    wu = cvt(inputs["L0_ffn_w_up"])
    wd = cvt(inputs["L0_ffn_w_down"])
    HI = 11008 // 2
    maps = []
    for c in range(n_cores):
        q = c % 2
        maps.append({
            "x": x[c * T:(c + 1) * T], "wq": wq, "bq": bq, "wk": wk, "bk": bk,
            "wv": wv, "bv": bv,
            "wg": np.ascontiguousarray(wg[q * HI:(q + 1) * HI]),
            "wu": np.ascontiguousarray(wu[q * HI:(q + 1) * HI]),
            "wd": np.ascontiguousarray(wd[:, q * HI:(q + 1) * HI]),
        })
    return maps


def kernel(**inputs):
    from concourse.bass_utils import run_bass_kernel_spmd
    nc = build_program()
    maps = make_in_maps(inputs)
    res = run_bass_kernel_spmd(nc, maps, core_ids=list(range(8)))
    out = np.concatenate([np.asarray(res.results[c]["out"]) for c in range(8)], axis=0)
    return out.reshape(2, 4096, 4096)


# version 12
# speedup vs baseline: 1.0223x; 1.0140x over previous
import numpy as np
from contextlib import ExitStack

import concourse.bass as bass
import concourse.mybir as mybir

F32 = mybir.dt.float32
BF16 = mybir.dt.bfloat16
AF = mybir.ActivationFunctionType
ALU = mybir.AluOpType
AX = mybir.AxisListType

T = 1024
DM = 4096
EC = DM // 128
NCOL = T // 512
IFFN = 11008
BLK_CH = [22, 22, 22, 20]
BLK_OFF = [0, 22, 44, 66]
NW = float(IFFN * DM)
SCL = 0.08838834764831845

ENGINES = ("sp", "pe", "dve", "act", "pool")


class Sched:
    SELF_EDGE = ("dve", "act", "pool")

    def __init__(self):
        self.seg = 0
        self.ops = {e: [] for e in ENGINES}
        self.count = {}
        self.lastwait = {e: {} for e in ENGINES}
        self.serialize = False
        self.last_tok = {}
        self.prev_tok = {e: None for e in ENGINES}

    def emit(self, eng, fn, waits=(), inc=None, inc_n=1, raw=False):
        is_dma = (inc is not None and inc.endswith("dma")) or raw
        if not is_dma and eng in self.SELF_EDGE:
            if inc is None:
                inc = eng
                inc_n = 1
            waits = list(waits) + [self.prev_tok[eng]]
        if self.serialize and not raw:
            waits = list(waits) + [t for e2, t in self.last_tok.items() if e2 != eng]
            if inc is None:
                inc = {"sp": "spdma", "pool": "pooldma"}.get(eng, eng)
                inc_n = 16 if inc.endswith("dma") else 1
        ws = []
        for w in waits:
            if w is None:
                continue
            s, v = w
            lw = self.lastwait[eng].get(s, 0)
            if v > lw:
                ws.append((s, v))
                self.lastwait[eng][s] = v
        tok = None
        if inc is not None:
            self.count[inc] = self.count.get(inc, 0) + inc_n
            tok = (inc, self.count[inc])
        if self.serialize and tok is not None:
            self.last_tok[eng] = tok
        if not is_dma and eng in self.SELF_EDGE and tok is not None:
            self.prev_tok[eng] = tok
        self.ops[eng].append((self.seg, ws, fn, inc, inc_n))
        return tok


def tmax(*toks):
    best = None
    for t in toks:
        if t is None:
            continue
        if best is None or t[1] > best[1]:
            best = t
    return best


def build_program(debug=(), phases=("p0", "p1", "p2", "p3", "p4", "p5"), ndev=8, rc=True, bar=True):
    nc = bass.Bass("TRN2", target_bir_lowering=False, debug=False, num_devices=ndev, detect_race_conditions=rc)
    nc.reset()
    dbg = set(debug)

    x_d = nc.dram_tensor("x", [T, DM], F32, kind="ExternalInput").ap()
    wq_d = nc.dram_tensor("wq", [DM, DM], F32, kind="ExternalInput").ap()
    bq_d = nc.dram_tensor("bq", [128, 32], F32, kind="ExternalInput").ap()
    wk_d = nc.dram_tensor("wk", [DM, 1024], F32, kind="ExternalInput").ap()
    bk_d = nc.dram_tensor("bk", [128, 8], F32, kind="ExternalInput").ap()
    wv_d = nc.dram_tensor("wv", [DM, 1024], F32, kind="ExternalInput").ap()
    bv_d = nc.dram_tensor("bv", [128, 8], F32, kind="ExternalInput").ap()
    HI = IFFN // 2
    wg_d = nc.dram_tensor("wg", [HI, DM], F32, kind="ExternalInput").ap()
    wu_d = nc.dram_tensor("wu", [HI, DM], F32, kind="ExternalInput").ap()
    wd_d = nc.dram_tensor("wd", [DM, HI], F32, kind="ExternalInput").ap()
    FAMC = 86 * 32 * 128
    wsh_h = nc.dram_tensor("wsh", [128, 2 * FAMC], BF16, kind="Internal",
                           addr_space="Shared")
    wsh = wsh_h.ap()
    wshd_h = nc.dram_tensor("wshd", [128, FAMC], BF16, kind="Internal",
                            addr_space="Shared")
    wshd = wshd_h.ap()
    wsc_h = nc.dram_tensor("wsc", [2, 4], F32, kind="Internal", addr_space="Shared")
    wsc = wsc_h.ap()
    MYC = 43 * 32 * 128
    wloc_h = nc.dram_tensor("wloc", [128, 3 * MYC], BF16, kind="Internal")
    wloc = wloc_h.ap()
    out_d = nc.dram_tensor("out", [T, DM], F32, kind="ExternalOutput").ap()

    def scratch(name, shape):
        kind = "ExternalOutput" if name in dbg else "Internal"
        return nc.dram_tensor(name, shape, F32, kind=kind).ap()

    o8_d = scratch("o8_scr", [T * 8, 128])
    wc_d = nc.dram_tensor("wqkv_scr", [3 * DM, 1024], BF16, kind="Internal").ap()
    part_d = scratch("part_scr", [DM, T])
    ffn_d = scratch("ffn_scr", [T, DM])
    dump_d = {}
    for nm, w in (("xnT", EC * T), ("QS_dm", T * 8), ("K_dm", T * 8), ("V_dm", T * 8),
                  ("cst", 16), ("acc", 8), ("interT", 22 * T),
                  ("gtmp", 1024)):
        if nm in dbg:
            dump_d[nm] = nc.dram_tensor("dump_" + nm, [128, w], F32,
                                        kind="ExternalOutput").ap()

    S = Sched()
    es = ExitStack()

    def mksb(stack, name, p, f, dt):
        return stack.enter_context(nc.sbuf_tensor(name, [p, f], dt))

    ident_bf = mksb(es, "ident_bf", 128, 128, BF16)
    ident_f = mksb(es, "ident_f", 128, 128, F32)
    xnT = mksb(es, "xnT", 128, EC * T, BF16)
    bq_sb = mksb(es, "bq_sb", 128, 32, F32)
    bk_sb = mksb(es, "bk_sb", 128, 8, F32)
    bv_sb = mksb(es, "bv_sb", 128, 8, F32)
    bqs = mksb(es, "bqs", 128, 8, F32)
    bscr = mksb(es, "bscr", 128, 16, F32)
    cst = mksb(es, "cst", 128, 16, F32)
    ones1 = mksb(es, "ones1", 1, 128, F32)
    acc = mksb(es, "acc", 128, 8, F32)
    epsc = mksb(es, "epsc", 128, 2, F32)
    pstg = mksb(es, "pstg", 1, 4, F32)

    pbt = es.enter_context(nc.psum_tensor("pb", [128, 4096], F32))

    def bank(k, lo=0, n=512):
        return pbt[:, k * 512 + lo: k * 512 + lo + n]

    sems = {}

    S.emit("pool", lambda e: e.memset(ident_bf[:], 0.0), inc="pool")
    t_id = S.emit("pool", lambda e: e.affine_select(
        out=ident_bf[:], in_=ident_bf[:], compare_op=ALU.not_equal, fill=1.0,
        base=0, pattern=[[-1, 128]], channel_multiplier=1), inc="pool")
    S.emit("pool", lambda e: e.memset(ident_f[:], 0.0), inc="pool")
    t_idf = S.emit("pool", lambda e: e.affine_select(
        out=ident_f[:], in_=ident_f[:], compare_op=ALU.not_equal, fill=1.0,
        base=0, pattern=[[-1, 128]], channel_multiplier=1), inc="pool")
    S.emit("pool", lambda e: e.memset(cst[:, 0:1], 1.0), inc="pool")
    S.emit("pool", lambda e: e.memset(ones1[:], 1.0), inc="pool")
    t_acc0 = S.emit("pool", lambda e: e.memset(acc[:], 0.0), inc="pool")
    t_pstg0 = S.emit("pool", lambda e: e.memset(pstg[:], 0.0), inc="pool")
    S.emit("pool", lambda e: e.memset(epsc[:, 0:1], 1e-5), inc="pool")
    t_eps = S.emit("pool", lambda e: e.memset(epsc[:, 1:2], 0.5e-5), inc="pool")
    t_bq = S.emit("pool", lambda e: e.dma_start(bq_sb[:], bq_d), inc="dbias", inc_n=16)
    S.emit("pool", lambda e: e.dma_start(bk_sb[:], bk_d), inc="dbias", inc_n=16)
    t_bv = S.emit("pool", lambda e: e.dma_start(bv_sb[:], bv_d), inc="dbias", inc_n=16)
    bqv = bq_sb[:].rearrange("p (h j) -> p h j", j=4)
    S.emit("dve", lambda e: e.tensor_add(bscr[:, 0:8], bqv[:, :, 0], bqv[:, :, 1]),
           waits=[t_bv], inc="dve")
    t_b1 = S.emit("dve", lambda e: e.tensor_add(bscr[:, 8:16], bqv[:, :, 2], bqv[:, :, 3]),
                  inc="dve")
    t_b2 = S.emit("dve", lambda e: e.tensor_add(bscr[:, 0:8], bscr[:, 0:8], bscr[:, 8:16]),
                  waits=[t_b1], inc="dve")
    t_bqs = S.emit("dve", lambda e: e.tensor_scalar_mul(bqs[:], bscr[:, 0:8], SCL),
                   waits=[t_b2], inc="dve")

    p12 = ExitStack()
    pab = []
    QS_dm = mksb(p12, "QS_dm", 128, T * 8, BF16)
    K_dm = mksb(p12, "K_dm", 128, T * 8, BF16)
    V_dm = mksb(p12, "V_dm", 128, T * 8, BF16)
    qkT = [mksb(p12, f"qkT{i}", 8, 1024, BF16) for i in range(2)]
    attn_bf = mksb(p12, "attn_bf", 128, 4 * 512, BF16)
    attn_s = mksb(p12, "attn_s", 128, 2 * 512, BF16)
    attnT_sb = mksb(p12, "attnT_sb", 128, 2 * 512, BF16)
    sums4 = mksb(p12, "sums4", 128, 16, F32)
    negm4 = mksb(p12, "negm4", 128, 16, F32)
    rec4 = mksb(p12, "rec4", 128, 16, F32)
    o8sb = [mksb(p12, f"o8sb{i}", 128, 512, F32) for i in range(2)]
    o8stg = [mksb(p12, f"o8stg{i}", 128, 512, F32) for i in range(2)]

    pa_pieces = []
    if "p3" in phases:
        for mi, wsrc in ((0, wg_d), (1, wu_d)):
            for ch in range(HI // 128):
                pa_pieces.append((wsrc[ch * 128:(ch + 1) * 128, :], DM, mi))
        for hc in range(32):
            pa_pieces.append((wd_d[hc * 128:(hc + 1) * 128, 0:4096], 4096, 2))
            pa_pieces.append((wd_d[hc * 128:(hc + 1) * 128, 4096:HI],
                              HI - 4096, 2))
    pa = {"i": 0, "slot_tok": [None, None], "par": [0, 0, 0],
          "acc_tok": [t_acc0, t_acc0, t_acc0], "scr_tok": None}

    def passA_next(n):
        for _ in range(n):
            i = pa["i"]
            if i >= len(pa_pieces):
                return
            pa["i"] = i + 1
            src, w, mi = pa_pieces[i]
            sl = i % 2
            td = S.emit("sp", lambda e, src=src, sl=sl, w=w: e.dma_start(
                pab[sl][:, 0:w], src), waits=[pa["slot_tok"][sl]],
                inc="dpab%d" % (i % 4), inc_n=16)
            tras = []
            for k0 in range(0, w, 2048):
                k1 = min(w, k0 + 2048)
                tras.append(S.emit("act", lambda e, sl=sl, k0=k0, k1=k1,
                                   kk=len(tras): e.activation(
                    pab[sl][:, k0:k1], pab[sl][:, k0:k1], AF.Abs,
                    accum_out=acc[:, 6 + kk:7 + kk]),
                    waits=[td, pa["scr_tok"]], inc="act"))
            if len(tras) > 1:
                tr = S.emit("dve", lambda e: e.tensor_add(
                    acc[:, 6:7], acc[:, 6:7], acc[:, 7:8]),
                    waits=tras, inc="dve")
            else:
                tr = tras[0]
            par = pa["par"][mi]
            ca, cb = mi * 2 + par, mi * 2 + (1 - par)
            ta = S.emit("dve", lambda e, ca=ca, cb=cb: e.tensor_add(
                acc[:, cb:cb + 1], acc[:, ca:ca + 1], acc[:, 6:7]),
                waits=[tr, pa["acc_tok"][mi]], inc="dve")
            pa["par"][mi] = 1 - par
            pa["acc_tok"][mi] = ta
            pa["slot_tok"][sl] = ta
            pa["scr_tok"] = ta

    t_xnT_done = None
    if "p0" in phases:
        p0es = ExitStack()
        xbuf = [mksb(p0es, f"xb{i}", 128, DM, F32) for i in range(2)]
        xn_bf = [mksb(p0es, f"xn{i}", 128, DM, BF16) for i in range(2)]
        stats = mksb(p0es, "lnst", 128, 8, F32)
        slot_tok = [None, None]
        xn_war = [None, None]
        evac_tok = {0: None, 1: None}
        for tt in range(T // 128):
            p = tt % 2
            td = S.emit("pool", lambda e, tt=tt, p=p: e.dma_start(
                xbuf[p][:], x_d[tt * 128:(tt + 1) * 128, :]),
                waits=[slot_tok[p]], inc="dxb%d" % p, inc_n=16)
            t1 = S.emit("dve", lambda e, p=p: e.reduce_sum(
                stats[:, 4:5], xbuf[p][:], axis=AX.X), waits=[td], inc="dve")
            t2 = S.emit("dve", lambda e, p=p: e.tensor_scalar_mul(
                stats[:, p:p + 1], stats[:, 4:5], -1.0 / DM), waits=[t1], inc="dve")
            t3a = S.emit("act", lambda e, p=p: e.activation(
                bank(2, 0, 2048), xbuf[p][:, 0:2048], AF.Square,
                bias=stats[:, p:p + 1], accum_out=stats[:, 2 + p:3 + p]),
                waits=[t2], inc="act")
            t3b = S.emit("act", lambda e, p=p: e.activation(
                bank(2, 0, 2048), xbuf[p][:, 2048:4096], AF.Square,
                bias=stats[:, p:p + 1], accum_out=stats[:, 6:7]),
                waits=[t3a], inc="act")
            t3c = S.emit("dve", lambda e, p=p: e.tensor_add(
                stats[:, 2 + p:3 + p], stats[:, 2 + p:3 + p], stats[:, 6:7]),
                waits=[t3b], inc="dve")
            t4 = S.emit("act", lambda e, p=p: e.activation(
                stats[:, 2 + p:3 + p], stats[:, 2 + p:3 + p], AF.Ln,
                bias=epsc[:, 0:1], scale=1.0 / DM), waits=[t3c, t_eps], inc="act")
            t5 = S.emit("act", lambda e, p=p: e.activation(
                stats[:, 2 + p:3 + p], stats[:, 2 + p:3 + p], AF.Exp,
                scale=-0.5), waits=[t4], inc="act")
            t6 = S.emit("dve", lambda e, p=p: e.tensor_scalar(
                xn_bf[p][:], xbuf[p][:], stats[:, p:p + 1], stats[:, 2 + p:3 + p],
                ALU.add, ALU.mult), waits=[t5, xn_war[p]], inc="dve")
            slot_tok[p] = t6
            last_tr = None
            for q in range(8):
                bk_ = q % 2
                tts = None
                for c in range(4 * q, 4 * q + 4):
                    tts = S.emit("pe", lambda e, p=p, c=c, bk_=bk_: e.matmul(
                        bank(bk_, (c % 4) * 128, 128)[0:128, :],
                        xn_bf[p][:, c * 128:(c + 1) * 128], ident_bf[:],
                        start=True, stop=True, skip_group_check=True),
                        waits=[t6, t_id, evac_tok[bk_]], inc="pe")
                last_tr = tts
                dst = xnT[:].rearrange("p (c t) -> p c t", t=T)[
                    :, 4 * q:4 * q + 4, tt * 128:(tt + 1) * 128]
                if q % 2 == 0:
                    evac_tok[bk_] = S.emit("dve", lambda e, bk_=bk_, dst=dst: e.tensor_copy(
                        dst, bank(bk_)[:].rearrange("p (c t) -> p c t", t=128)),
                        waits=[tts], inc="dve")
                else:
                    evac_tok[bk_] = S.emit("act", lambda e, bk_=bk_, dst=dst: e.activation(
                        dst, bank(bk_)[:].rearrange("p (c t) -> p c t", t=128),
                        AF.Copy), waits=[tts], inc="act")
            xn_war[p] = last_tr
        t_xnT_done = tmax(evac_tok[0], evac_tok[1])
        p0es.close()
    pab.extend(mksb(p12, f"pab{i}", 128, 4096, F32) for i in range(2))

    t_p1_all = None
    if "p1" in phases:
        p1es = ExitStack()
        wbuf = [mksb(p1es, f"wb{i}", 128, DM, F32) for i in range(2)]
        tmp1 = mksb(p1es, "tmp1", 128, 1024, F32)
        wbf = [mksb(p1es, f"wbf{i}", 128, 1024, BF16) for i in range(2)]
        wslot_tok = [None, None]
        wbf_tok = {0: [], 1: []}
        wpark = {}
        bank_evac = {h: None for h in range(8)}
        for col in range(NCOL):
            tw = 512
            for mat, (wsrc, bias_sb, bscl) in enumerate((
                    (wq_d, bqs, SCL), (wk_d, bk_sb, 1.0), (wv_d, bv_sb, 1.0))):
                dm_dst = (QS_dm, K_dm, V_dm)[mat]
                mm_toks = [None] * 8
                for ec in range(EC):
                    p = ec % 2
                    wide = DM if mat == 0 else 1024
                    if col == 0:
                        td = S.emit("sp", lambda e, p=p, ec=ec, wsrc=wsrc, wide=wide: e.dma_start(
                            wbuf[p][:, 0:wide], wsrc[ec * 128:(ec + 1) * 128, :]),
                            waits=[wslot_tok[p], t_xnT_done], inc="dwb%d" % p, inc_n=16)
                        if mat == 0:
                            wv4 = wbuf[p][:, 0:DM].rearrange("p (c j) -> p c j", j=4)
                            S.emit("dve", lambda e, wv4=wv4: e.tensor_add(
                                tmp1[:], wv4[:, :, 0], wv4[:, :, 1]), waits=[td], inc="dve")
                            S.emit("dve", lambda e, wv4=wv4: e.tensor_add(
                                tmp1[:], tmp1[:], wv4[:, :, 2]), inc="dve")
                            S.emit("dve", lambda e, wv4=wv4: e.tensor_add(
                                tmp1[:], tmp1[:], wv4[:, :, 3]), inc="dve")
                            tc = S.emit("dve", lambda e, p=p: e.tensor_copy(
                                wbf[p][:], tmp1[:]), waits=wbf_tok[p], inc="dve")
                        else:
                            tc = S.emit("dve", lambda e, p=p: e.tensor_copy(
                                wbf[p][:], wbuf[p][:, 0:1024]),
                                waits=[td] + wbf_tok[p], inc="dve")
                        wslot_tok[p] = tc
                        tpk = S.emit("pool", lambda e, p=p, mat=mat, ec=ec: e.dma_start(
                            wc_d[mat * DM + ec * 128: mat * DM + (ec + 1) * 128, :],
                            wbf[p][:]), waits=[tc], inc="dwc%d" % p, inc_n=16)
                        wpark[(mat, ec)] = tpk
                    else:
                        tc = S.emit("sp", lambda e, p=p, mat=mat, ec=ec: e.dma_start(
                            wbf[p][:],
                            wc_d[mat * DM + ec * 128: mat * DM + (ec + 1) * 128, :]),
                            waits=wbf_tok[p] + [wpark[(mat, ec)], wslot_tok[p]],
                            inc="dwb%d" % p, inc_n=16)
                        wslot_tok[p] = tc
                    for hp in range(8):
                        lhs = wbf[p][:, hp:1024:8]
                        rhs = xnT[:, ec * T + col * 512: ec * T + col * 512 + tw]
                        mm_toks[hp] = S.emit("pe", lambda e, lhs=lhs, rhs=rhs, hp=hp, ec=ec: e.matmul(
                            bank(hp, 0, tw), lhs, rhs,
                            start=(ec == 0), stop=(ec == EC - 1), skip_group_check=True),
                            waits=[tc, bank_evac[hp]], inc="pe")
                    wbf_tok[p] = [mm_toks[7]] + ([wpark[(mat, ec)]] if col == 0 else [])
                    if ec % 8 == 3:
                        passA_next(1)
                for hp in range(8):
                    dst = dm_dst[:].rearrange("p (t h) -> p t h", h=8)[
                        :, col * 512:(col + 1) * 512, hp]
                    bias_ap = bias_sb[:, hp:hp + 1]
                    bank_evac[hp] = S.emit("dve", lambda e, hp=hp, dst=dst,
                                           bias_ap=bias_ap, bscl=bscl, tw=tw: e.tensor_scalar(
                        dst, bank(hp, 0, tw), bscl, bias_ap, ALU.mult, ALU.add),
                        waits=[mm_toks[hp], t_bv, t_bqs], inc="dve")
        t_p1_all = bank_evac[7]
        p1es.close()

    t_p2_parks = None
    park_final = {}
    t_last_av = None
    if "p2" in phases:
        G = T // 4
        qkT_tok = [None, None]
        trq_war = None
        sc_war = [None, None]
        at_war = [None, None]
        o8_war = [None, None]
        o8sb_war = [None, None]
        o8stg_war = [None, None]
        st = {}
        for tick in range(G + 6):
            g = tick
            if g < G:
                tk = None
                for i in range(4):
                    t = 4 * g + i
                    S.emit("pe", lambda e, t=t, i=i: e.matmul(
                        bank(i // 2, (i % 2) * 256, 128)[0:8, :],
                        QS_dm[:, 8 * t:8 * t + 8], ident_bf[:], start=True, stop=True, skip_group_check=True),
                        waits=[t_p1_all, trq_war, t_id], inc="pe")
                    tk = S.emit("pe", lambda e, t=t, i=i: e.matmul(
                        bank(i // 2, (i % 2) * 256 + 128, 128)[0:8, :],
                        K_dm[:, 8 * t:8 * t + 8], ident_bf[:], start=True, stop=True, skip_group_check=True),
                        inc="pe")
                st[(g, "tp")] = tk
                st[(g, "qke")] = S.emit("act", lambda e, g=g: e.activation(
                    qkT[g % 2][0:8, :], bank(0, 0, 1024)[0:8, :], AF.Copy),
                    waits=[st[(g, "tp")], qkT_tok[g % 2]], inc="act")
                trq_war = st[(g, "qke")]
            g = tick - 1
            if 0 <= g < G:
                tk = None
                for i in range(4):
                    tk = S.emit("pe", lambda e, g=g, i=i: e.matmul(
                        bank(2 + g % 2, i * 128, 128),
                        qkT[g % 2][0:8, i * 256:i * 256 + 128],
                        qkT[g % 2][0:8, i * 256 + 128:i * 256 + 256],
                        start=True, stop=True, skip_group_check=True),
                        waits=[st[(g, "qke")], sc_war[g % 2]], inc="pe")
                st[(g, "sc")] = tk
                qkT_tok[g % 2] = tk
                st[(g, "nm")] = S.emit("dve", lambda e, g=g: e.reduce_max(
                    negm4[:, (g % 4):(g % 4) + 1], bank(2 + g % 2),
                    axis=AX.X, negate=True), waits=[st[(g, "sc")]], inc="dve")
            g = tick - 2
            if 0 <= g < G:
                tk = S.emit("act", lambda e, g=g: e.activation(
                    attn_bf[:, (g % 4) * 512:(g % 4) * 512 + 512],
                    bank(2 + g % 2), AF.Exp,
                    bias=negm4[:, (g % 4):(g % 4) + 1]),
                    waits=[st[(g, "nm")]], inc="act")
                tk = S.emit("dve", lambda e, g=g: e.reduce_sum(
                    sums4[:, (g % 4) * 4:(g % 4) * 4 + 4],
                    attn_bf[:, (g % 4) * 512:(g % 4) * 512 + 512].rearrange(
                        "p (a b) -> p a b", b=128), axis=AX.X),
                    waits=[tk], inc="dve")
                st[(g, "exp")] = tk
                sc_war[g % 2] = tk
            g = tick - 3
            if 0 <= g < G:
                tr = S.emit("dve", lambda e, g=g: e.reciprocal(
                    rec4[:, (g % 4) * 4:(g % 4) * 4 + 4],
                    sums4[:, (g % 4) * 4:(g % 4) * 4 + 4]),
                    waits=[st[(g, "exp")]], inc="dve")
                tk = None
                for i in range(4):
                    tk = S.emit("dve", lambda e, g=g, i=i: e.tensor_scalar_mul(
                        attn_s[:, (g % 2) * 512 + i * 128:(g % 2) * 512 + (i + 1) * 128],
                        attn_bf[:, (g % 4) * 512 + i * 128:(g % 4) * 512 + (i + 1) * 128],
                        rec4[:, (g % 4) * 4 + i:(g % 4) * 4 + i + 1]),
                        waits=[tr, st.get((g - 2, "atT"))], inc="dve")
                st[(g, "sca")] = tk
            g = tick - 4
            if 0 <= g < G:
                tk = None
                for i in range(4):
                    tk = S.emit("pe", lambda e, g=g, i=i: e.matmul(
                        bank(4 + g % 2, i * 128, 128),
                        attn_s[:, (g % 2) * 512 + i * 128:(g % 2) * 512 + (i + 1) * 128],
                        ident_bf[:], start=True, stop=True, skip_group_check=True),
                        waits=[st[(g, "sca")], at_war[g % 2]], inc="pe")
                st[(g, "atT")] = tk
                tk = None
                for i in range(4):
                    tk = S.emit("act", lambda e, g=g, i=i: e.activation(
                        attnT_sb[:, (g % 2) * 512 + i * 128:(g % 2) * 512 + (i + 1) * 128],
                        bank(4 + g % 2, i * 128, 128), AF.Copy),
                        waits=[st[(g, "atT")], st.get((g - 2, "av"))], inc="act")
                st[(g, "atE")] = tk
                at_war[g % 2] = tk
            g = tick - 5
            if 0 <= g < G:
                tb = (g // 16) % 2
                tk = None
                for i in range(4):
                    t = 4 * g + i
                    l = (g % 16) * 4 + i
                    tk = S.emit("pe", lambda e, i=i, g=g, t=t, l=l, tb=tb: e.matmul(
                        bank(6 + tb, l * 8, 8),
                        attnT_sb[:, (g % 2) * 512 + i * 128:(g % 2) * 512 + (i + 1) * 128],
                        V_dm[:, 8 * t:8 * t + 8], start=True, stop=True, skip_group_check=True),
                        waits=[st[(g, "atE")], o8_war[tb]], inc="pe")
                st[(g, "av")] = tk
                t_last_av = tk
                if g % 16 == 15:
                    tbk = g // 16
                    te = S.emit("act", lambda e, tb=tb, tbk=tbk: e.activation(
                        o8sb[tbk % 2][:], bank(6 + tb), AF.Copy),
                        waits=[tk, o8sb_war[tbk % 2]], inc="act")
                    tt4 = None
                    for j in range(4):
                        tt4 = S.emit("pe", lambda e, tb=tb, tbk=tbk, j=j: e.matmul(
                            bank(6 + tb, j * 128, 128),
                            o8sb[tbk % 2][:, j * 128:(j + 1) * 128], ident_f[:],
                            start=True, stop=True, skip_group_check=True), waits=[te, t_idf], inc="pe")
                    o8sb_war[tbk % 2] = tt4
                    tv = S.emit("dve", lambda e, tb=tb, tbk=tbk: e.tensor_copy(
                        o8stg[tbk % 2][:], bank(6 + tb)),
                        waits=[tt4, o8stg_war[tbk % 2]], inc="dve")
                    o8_war[tb] = tv
                    src = o8stg[tbk % 2][:].rearrange("p (j d) -> p j d", d=128)
                    dst = o8_d.rearrange("(b j p) d -> b p j d", j=4, p=128)[tbk]
                    tpk = S.emit("pool", lambda e, src=src, dst=dst: e.dma_start(
                        dst, src), waits=[tv], inc="dpk%d" % (tbk % 2), inc_n=16)
                    o8stg_war[tbk % 2] = tpk
                    park_final[tpk[0]] = tpk
            passA_next(2)
        t_p2_parks = list(park_final.values())

    if "p3" in phases:
        passA_next(len(pa_pieces))
        tev = None
        for mi in range(3):
            ca = mi * 2 + pa["par"][mi]
            tm = S.emit("pe", lambda e, ca=ca, mi=mi: e.matmul(
                bank(0, mi, 1)[0:1, :], acc[:, ca:ca + 1], cst[:, 0:1],
                start=True, stop=True, skip_group_check=True),
                waits=[pa["acc_tok"][mi], t_last_av], inc="pe")
            tev = S.emit("dve", lambda e, mi=mi: e.tensor_copy(
                pstg[0:1, mi:mi + 1], bank(0, mi, 1)[0:1, :]),
                waits=[tm, t_pstg0], inc="dve")
        def pub(e):
            q = e.partition_id() % 2
            return e.dma_start(bass.AP(wsc_h, q * 4, [[4, 1], [1, 4]]), pstg[0:1, :])
        tpub = S.emit("pool", pub, waits=[tev], inc="dwsc", inc_n=16)
        S.emit("pool", lambda e: e.wait_ge(sems["dwsc"], 16), raw=True)

    for nm, sbuf_t in (("xnT", xnT), ("QS_dm", QS_dm), ("K_dm", K_dm), ("V_dm", V_dm)):
        if nm in dump_d:
            S.emit("pool", lambda e, d=dump_d[nm], s=sbuf_t: e.dma_start(
                d, s[:]), waits=[t_p1_all, t_xnT_done, t_last_av], inc="ddump", inc_n=16)
    if "p2" in phases:
        S.emit("pool", lambda e: [e.wait_ge(sems[k], v) for k, v in
                                  {kk: vv for kk, vv in S.count.items()
                                   if kk.startswith("dpk")}.items()], raw=True)
    p12.close()

    t_quant_ready = None
    if "p3" in phases:
        S.seg = 1
        qwes = ExitStack()
        wsc_sb = mksb(qwes, "wsc_sb", 1, 8, F32)
        sbf = [mksb(qwes, f"sbf{i}", 128, 2048, F32) for i in range(6)]
        tqa1 = mksb(qwes, "tqa1", 128, 2048, BF16)
        tqa2 = mksb(qwes, "tqa2", 128, 2048, BF16)
        tqd1 = mksb(qwes, "tqd1", 128, 2048, BF16)
        tqd2 = mksb(qwes, "tqd2", 128, 2048, BF16)
        qg = [mksb(qwes, f"qg{i}", 128, 2048, BF16) for i in range(4)]
        stgw = [mksb(qwes, f"stgw{i}", 128, 8192, BF16) for i in range(3)]

        trd = S.emit("pool", lambda e: e.dma_start(
            wsc_sb[0:1, :], wsc.rearrange("a b -> (a b)").unsqueeze(0)),
            inc="dwsc", inc_n=16)
        ttot = S.emit("dve", lambda e: e.tensor_add(
            cst[0:1, 12:15], wsc_sb[0:1, 0:3], wsc_sb[0:1, 4:7]),
            waits=[trd], inc="dve")
        tsc = []
        for mi in range(3):
            tsc.append(S.emit("dve", lambda e, mi=mi: e.tensor_scalar(
                cst[0:1, 9 + mi:10 + mi], cst[0:1, 12 + mi:13 + mi],
                0.5 / NW, 0.5e-5, ALU.mult, ALU.add), waits=[ttot], inc="dve"))
        tsu = S.emit("dve", lambda e: e.tensor_scalar_mul(
            cst[0:1, 15:16], cst[0:1, 13:14], 1.0 / NW), waits=[ttot], inc="dve")
        tsg = S.emit("dve", lambda e: e.tensor_scalar_mul(
            cst[0:1, 13:14], cst[0:1, 12:13], 0.25 / NW),
            waits=[tsu], inc="dve")
        tsd = S.emit("dve", lambda e: e.tensor_scalar_mul(
            cst[0:1, 14:15], cst[0:1, 14:15], 0.5 / NW),
            waits=[tsc[2]], inc="dve")
        tguP = S.emit("dve", lambda e: e.tensor_mul(
            cst[0:1, 15:16], cst[0:1, 15:16], cst[0:1, 13:14]),
            waits=[tsu, tsg], inc="dve")
        bc = [(9, 1, tsc[0]), (10, 3, tsc[1]), (11, 5, tsc[2]),
              (15, 7, tguP), (14, 8, tsd)]
        tbc = []
        for sc_, dc, wt in bc:
            tm = S.emit("pe", lambda e, sc_=sc_, dc=dc: e.matmul(
                bank(1, dc, 1), ones1[0:1, :], cst[0:1, sc_:sc_ + 1],
                start=True, stop=True, skip_group_check=True), waits=[wt], inc="pe")
            tbc.append(S.emit("dve", lambda e, dc=dc: e.tensor_copy(
                cst[:, dc:dc + 1], bank(1, dc, 1)), waits=[tm], inc="dve"))
        tn1 = S.emit("dve", lambda e: e.tensor_scalar_mul(
            cst[:, 2:3], cst[:, 1:2], -1.0), waits=[tbc[0]], inc="dve")
        tn2 = S.emit("dve", lambda e: e.tensor_scalar_mul(
            cst[:, 4:5], cst[:, 3:4], -1.0), waits=[tbc[1]], inc="dve")
        tn3 = S.emit("dve", lambda e: e.tensor_scalar_mul(
            cst[:, 6:7], cst[:, 5:6], -1.0), waits=[tbc[2]], inc="dve")
        t_quant_ready = S.emit("dve", lambda e: e.engine_nop(),
                               waits=[tn1, tn2, tn3, tbc[3], tbc[4]], inc="dve")

        sb_tok = [None] * 6
        qg_tok = [None, None, None, None]
        tr_war = {4: None, 5: None}
        stg_tok = [None, None, None]
        last_comb = [None, None]
        cki = [0]
        sbi = [0]
        wri = [0]
        last_wr = {}
        pid_cache = {}

        def myq(e):
            if id(e) not in pid_cache:
                pid_cache[id(e)] = e.snap(e.partition_id() % 2,
                                          min_val=0, max_val=1)
            return pid_cache[id(e)]

        def qw_load(src, width):
            sl = sbi[0] % 6
            sbi[0] += 1
            t = S.emit("sp", lambda e, src=src, sl=sl, width=width: e.dma_start(
                sbf[sl][:, 0:width], src), waits=[sb_tok[sl]],
                inc="dsbf%d" % sl, inc_n=16)
            return sl, t

        def qw_quant(sl, off, width, ccol, nccol, wdma):
            k = cki[0]
            cki[0] += 1
            path = k % 2
            qs = path * 2 + (k // 2) % 2
            if path == 0:
                ta = S.emit("act", lambda e, sl=sl, off=off, width=width, c=nccol: e.sign(
                    tqa1[:, 0:width], sbf[sl][:, off:off + width],
                    bias=cst[:, c:c + 1]),
                    waits=[wdma, t_quant_ready, last_comb[0]], inc="act")
                tb = S.emit("act", lambda e, sl=sl, off=off, width=width, c=ccol: e.sign(
                    tqa2[:, 0:width], sbf[sl][:, off:off + width],
                    bias=cst[:, c:c + 1]), inc="act")
                tc = S.emit("pool", lambda e, width=width, qs=qs: e.tensor_add(
                    qg[qs][:, 0:width], tqa1[:, 0:width], tqa2[:, 0:width]),
                    waits=[ta, tb, qg_tok[qs]], inc="pool")
                last_comb[0] = tc
            else:
                ta = S.emit("dve", lambda e, sl=sl, off=off, width=width, c=ccol: e.tensor_scalar(
                    tqd1[:, 0:width], sbf[sl][:, off:off + width],
                    cst[:, c:c + 1], 2.0, ALU.is_gt, ALU.mult),
                    waits=[wdma, t_quant_ready, last_comb[1]], inc="dve")
                tb = S.emit("dve", lambda e, sl=sl, off=off, width=width, c=nccol: e.tensor_scalar(
                    tqd2[:, 0:width], sbf[sl][:, off:off + width],
                    cst[:, c:c + 1], -2.0, ALU.is_lt, ALU.mult), inc="dve")
                tc = S.emit("pool", lambda e, width=width, qs=qs: e.tensor_add(
                    qg[qs][:, 0:width], tqd1[:, 0:width], tqd2[:, 0:width]),
                    waits=[ta, tb, qg_tok[qs]], inc="pool")
                last_comb[1] = tc
            return qs, tc

        def qw_transposes(qs, nblk, sg, stoff, tq):
            tpe = None
            ev = None
            for j0 in range(0, nblk, 4):
                bk_ = 4 + (j0 // 4) % 2
                nn = min(4, nblk - j0)
                tts = None
                for j in range(j0, j0 + nn):
                    tts = S.emit("pe", lambda e, qs=qs, j=j, bk_=bk_, j0=j0: e.matmul(
                        bank(bk_, (j - j0) * 128, 128),
                        qg[qs][:, j * 128:(j + 1) * 128], ident_bf[:],
                        start=True, stop=True, skip_group_check=True),
                        waits=[tq, tr_war[bk_], t_id], inc="pe")
                tpe = tts
                ev = S.emit("dve", lambda e, bk_=bk_, nn=nn, sg=sg,
                            c0=stoff + j0 * 128: e.tensor_copy(
                    stgw[sg][:, c0:c0 + nn * 128], bank(bk_, 0, nn * 128)),
                    waits=[tts, stg_tok[sg]], inc="dve")
                tr_war[bk_] = ev
            return tpe, ev

        def qw_write(dst_c0, width, ev):
            q_ = wri[0] % 4
            sg = wri[0] % 3
            wri[0] += 1
            tw = S.emit("pool", lambda e, c0=dst_c0, width=width, sg=sg: e.dma_start(
                wloc[:, c0:c0 + width], stgw[sg][:, 0:width]),
                waits=[ev, last_wr.get(q_)], inc="dwr%d" % q_, inc_n=16)
            stg_tok[sg] = tw
            last_wr[q_] = tw
            return tw

        def biggu(icl0, icl1, sem):
            def cp(e, icl0=icl0, icl1=icl1):
                q = myq(e)
                nc_ = (icl1 - icl0) * 8192
                dst = bass.AP(wsh_h, q * (2 * MYC) + icl0 * 8192,
                              [[2 * FAMC, 128], [1, nc_]])
                src = bass.AP(wloc_h, icl0 * 8192, [[3 * MYC, 128], [1, nc_]])
                return e.dma_start(dst, src)
            S.emit("pool", cp, waits=list(last_wr.values()), inc=sem, inc_n=16)

        for icl in range(43):
            sg = wri[0] % 3
            ev = None
            for fam, wsrc, ccol, nccol in ((0, wg_d, 1, 2), (1, wu_d, 3, 4)):
                for hf in range(2):
                    sl, td = qw_load(
                        wsrc[icl * 128:(icl + 1) * 128,
                             hf * 2048:(hf + 1) * 2048], 2048)
                    qs, tc = qw_quant(sl, 0, 2048, ccol, nccol, td)
                    tpe, ev = qw_transposes(qs, 16, sg, fam * 4096 + hf * 2048, tc)
                    qg_tok[qs] = tpe
                    sb_tok[sl] = tc
            qw_write(icl * 8192, 8192, ev)
            if icl % 8 == 7:
                biggu(icl - 7, icl + 1, "dsh0")
        biggu(40, 43, "dsh0")

        def bigdn(hc0, hc1, sem):
            def cp(e, hc0=hc0, hc1=hc1):
                q = myq(e)
                dst = bass.AP(wshd_h, q * (43 * 128) + hc0 * (86 * 128),
                              [[FAMC, 128], [86 * 128, hc1 - hc0], [1, 43 * 128]])
                src = bass.AP(wloc_h, 2 * MYC + hc0 * (43 * 128),
                              [[3 * MYC, 128], [43 * 128, hc1 - hc0],
                               [1, 43 * 128]])
                return e.dma_start(dst, src)
            S.emit("pool", cp, waits=list(last_wr.values()), inc=sem, inc_n=16)

        for hc in range(32):
            sg = wri[0] % 3
            ev = None
            for pc in range(3):
                w0 = pc * 2048
                w1 = min(HI, w0 + 2048)
                sl, td = qw_load(wd_d[hc * 128:(hc + 1) * 128, w0:w1], w1 - w0)
                qs, tc = qw_quant(sl, 0, w1 - w0, 5, 6, td)
                tpe, ev = qw_transposes(qs, (w1 - w0) // 128, sg, pc * 2048, tc)
                qg_tok[qs] = tpe
                sb_tok[sl] = tc
            qw_write(2 * MYC + hc * 5504, 5504, ev)
            if hc % 8 == 7:
                bigdn(hc - 7, hc + 1, "dsh1")
        S.emit("pool", lambda e: [e.wait_ge(sems["dsh%d" % f],
                                            S.count["dsh%d" % f])
                                  for f in range(2)], raw=True)
        qwes.close()
        S.seg = 2

    t_ffn_parks = None
    if "p4" in phases:
        p4es = ExitStack()
        interT = mksb(p4es, "interT", 128, 22 * T, BF16)
        slabGU = [mksb(p4es, f"slabGU{i}", 128, 8192, BF16) for i in range(2)]
        slabD = [mksb(p4es, f"slabD{i}", 128, 2816, BF16) for i in range(2)]
        gtmp = mksb(p4es, "gtmp", 128, 1024, BF16)
        ptile = [mksb(p4es, f"pt{i}", 128, 1024, F32) for i in range(2)]
        dstg = [mksb(p4es, f"dstg{i}", 128, 1024, F32) for i in range(2)]
        fbuf = [mksb(p4es, f"fbuf{i}", 128, 1024, F32) for i in range(8)]

        gu_war = {0: None, 1: None, 2: None, 3: None}
        dn_war = {4: None, 5: None, 6: None, 7: None}
        tr_war = {0: None, 1: None}
        slab_tok = {0: None, 1: None}
        slabD_tok = [None, None]
        gtmp_tok = [None, None]
        ptile_tok = [None, None]
        dstg_tok = [None, None]
        fbuf_tok = [None] * 8
        fbuf_ev = [None] * 8
        dff_last = {0: None, 1: None}
        down_read_done = None
        last_park = {}
        ffn_park = {}

        def prep_gu(ic):
            par = ic % 2
            tok = S.emit("sp", lambda e, ic=ic, par=par: e.dma_start(
                slabGU[par][:], wsh[:, ic * 8192:(ic + 1) * 8192]),
                waits=[slab_tok[par]], inc="dslGU%d" % par, inc_n=16)
            return tok, par

        def comp_gu(b, lc, ctx):
            tok, par = ctx
            mmg = {}
            mmu = {}
            for hb in range(32):
                for c in range(NCOL):
                    tw = 512
                    rhs = xnT[:, hb * T + c * 512: hb * T + c * 512 + tw]
                    mmg[c] = S.emit("pe", lambda e, hb=hb, c=c, rhs=rhs, par=par: e.matmul(
                        bank(c, 0, tw), slabGU[par][:, hb * 128:(hb + 1) * 128], rhs,
                        start=(hb == 0), stop=(hb == 31), skip_group_check=True),
                        waits=[tok, gu_war[c]], inc="pe")
                for c in range(NCOL):
                    tw = 512
                    rhs = xnT[:, hb * T + c * 512: hb * T + c * 512 + tw]
                    mmu[c] = S.emit("pe", lambda e, hb=hb, c=c, rhs=rhs, par=par: e.matmul(
                        bank(2 + c, 0, tw),
                        slabGU[par][:, 4096 + hb * 128:4096 + (hb + 1) * 128], rhs,
                        start=(hb == 0), stop=(hb == 31), skip_group_check=True),
                        waits=[tok, gu_war[2 + c]], inc="pe")
            slab_tok[par] = mmu[NCOL - 1]
            for c in range(NCOL):
                tw = 512
                tgv = S.emit("dve", lambda e, c=c, tw=tw: e.tensor_scalar(
                    gtmp[:, c * 512:c * 512 + tw], bank(c, 0, tw),
                    cst[:, 7:8], None, ALU.mult),
                    waits=[mmg[c], gtmp_tok[c], t_quant_ready], inc="dve")
                gu_war[c] = tgv
                ti = S.emit("dve", lambda e, c=c, lc=lc, tw=tw: e.tensor_mul(
                    interT[:, lc * T + c * 512: lc * T + c * 512 + tw],
                    gtmp[:, c * 512:c * 512 + tw], bank(2 + c, 0, tw)),
                    waits=[tgv, mmu[c], dn_read[b]], inc="dve")
                gu_war[2 + c] = ti
                gtmp_tok[c] = ti
                inter_tok[(b, lc)] = ti

        def prep_dn(b, hc):
            dsl = hc % 2
            nch = BLK_CH[b]
            c0 = (hc * 86 + BLK_OFF[b]) * 128
            src = wshd[:, c0:c0 + nch * 128]
            dst = slabD[dsl][:, 0:nch * 128]
            t = S.emit("sp", lambda e, src=src, dst=dst: e.dma_start(dst, src),
                       waits=[slabD_tok[dsl]], inc="dsld%d" % (hc % 4), inc_n=16)
            return t, dsl

        def comp_dn(b, hc, ctx):
            tld, dsl = ctx
            nch = BLK_CH[b]
            t_inter_all = tmax(*[inter_tok[(b, l)] for l in range(nch)])
            last_dn = None
            mmc = {}
            dbkc = {}
            for c in range(NCOL):
                dbkc[c] = 4 + (hc * NCOL + c) % 4
            for j in range(nch):
                for c in range(NCOL):
                    tw = 512
                    dbk = dbkc[c]
                    mm = S.emit("pe", lambda e, j=j, c=c, dbk=dbk, dsl=dsl, tw=tw: e.matmul(
                        bank(dbk, 0, tw), slabD[dsl][:, j * 128:(j + 1) * 128],
                        interT[:, j * T + c * 512: j * T + c * 512 + tw],
                        start=(j == 0), stop=(j == nch - 1), skip_group_check=True),
                        waits=[tld, t_inter_all, dn_war[dbk]], inc="pe")
                    mmc[c] = mm
                    last_dn = mm
            if b == 0:
                evs = []
                for c in range(NCOL):
                    tev = S.emit("act", lambda e, dbk=dbkc[c], dsl=dsl, c=c: e.activation(
                        dstg[dsl][:, c * 512:(c + 1) * 512], bank(dbk), AF.Copy),
                        waits=[mmc[c], dstg_tok[dsl]], inc="act")
                    dn_war[dbkc[c]] = tev
                    evs.append(tev)
                tp = S.emit("pool", lambda e, hc=hc, dsl=dsl: e.dma_start(
                    part_d[hc * 128:(hc + 1) * 128, :], dstg[dsl][:]),
                    waits=evs, inc="ddst%d" % dsl, inc_n=16)
                dstg_tok[dsl] = tp
                last_park[hc] = tp
            else:
                tl = S.emit("pool", lambda e, hc=hc, dsl=dsl: e.dma_start(
                    ptile[dsl][:], part_d[hc * 128:(hc + 1) * 128, :]),
                    waits=[ptile_tok[dsl], last_park[hc]],
                    inc="dpt%d" % dsl, inc_n=16)
                tas = []
                for c in range(NCOL):
                    ta = S.emit("dve", lambda e, dbk=dbkc[c], dsl=dsl, c=c: e.tensor_add(
                        dstg[dsl][:, c * 512:(c + 1) * 512], bank(dbk),
                        ptile[dsl][:, c * 512:(c + 1) * 512]),
                        waits=[mmc[c], tl, dstg_tok[dsl]], inc="dve")
                    dn_war[dbkc[c]] = ta
                    tas.append(ta)
                ptile_tok[dsl] = tas[-1]
                if b < 3:
                    tp = S.emit("pool", lambda e, hc=hc, dsl=dsl: e.dma_start(
                        part_d[hc * 128:(hc + 1) * 128, :], dstg[dsl][:]),
                        waits=tas, inc="ddst%d" % dsl, inc_n=16)
                    dstg_tok[dsl] = tp
                    last_park[hc] = tp
                else:
                    last_tmf = None
                    for c in range(NCOL):
                        for tbj in range(4):
                            bkf = tbj % 2
                            fi = c * 4 + tbj
                            tmf = S.emit("pe", lambda e, c=c, tbj=tbj, dsl=dsl, bkf=bkf: e.matmul(
                                bank(bkf, 0, 128),
                                dstg[dsl][:, c * 512 + tbj * 128:c * 512 + (tbj + 1) * 128],
                                ident_f[:], start=True, stop=True, skip_group_check=True),
                                waits=[tas[c], tr_war[bkf], t_idf], inc="pe")
                            last_tmf = tmf
                            tef = S.emit("dve", lambda e, fi=fi, hc=hc, bkf=bkf: e.tensor_scalar(
                                fbuf[fi][:, (hc % 8) * 128:(hc % 8 + 1) * 128],
                                bank(bkf, 0, 128), cst[:, 8:9], None, ALU.mult),
                                waits=[tmf, fbuf_tok[fi]], inc="dve")
                            tr_war[bkf] = tef
                            fbuf_ev[fi] = tef
                    dstg_tok[dsl] = last_tmf
                    if hc % 8 == 7:
                        g8 = hc // 8
                        for c in range(NCOL):
                            for tbj in range(4):
                                fi = c * 4 + tbj
                                qi = fi % 2
                                tw_ = S.emit("pool", lambda e, c=c, tbj=tbj, fi=fi, g8=g8: e.dma_start(
                                    ffn_d[c * 512 + tbj * 128:c * 512 + (tbj + 1) * 128,
                                          g8 * 1024:(g8 + 1) * 1024],
                                    fbuf[fi][:]),
                                    waits=[fbuf_ev[fi], dff_last[qi]],
                                    inc="dff%d" % qi, inc_n=16)
                                fbuf_tok[fi] = tw_
                                dff_last[qi] = tw_
                                ffn_park[tw_[0]] = tw_
            slabD_tok[dsl] = last_dn
            dn_read[b + 1] = last_dn

        inter_tok = {}
        dn_read = {0: None, 1: None, 2: None, 3: None, 4: None}
        stages = []
        for b in range(4):
            stages += [("gu", b, lc) for lc in range(BLK_CH[b])]
            stages += [("dn", b, hc) for hc in range(32)]

        ctxs = [None] * len(stages)
        for i in range(len(stages) + 1):
            if i >= 1:
                kind, b, idx = stages[i - 1]
                if kind == "gu":
                    comp_gu(b, idx, ctxs[i - 1])
                else:
                    comp_dn(b, idx, ctxs[i - 1])
            if i < len(stages):
                kind, b, idx = stages[i]
                ctxs[i] = prep_gu(BLK_OFF[b] + idx) if kind == "gu" else prep_dn(b, idx)
        t_ffn_parks = list(ffn_park.values())
        for nm, sb_t in (("interT", interT), ("gtmp", gtmp)):
            if nm in dump_d:
                S.emit("pool", lambda e, d=dump_d[nm], s_=sb_t: e.dma_start(
                    d, s_[:]), waits=list(t_ffn_parks), inc="ddump", inc_n=16)
        p4es.close()

    if "p5" in phases:
        ffparks = t_ffn_parks if isinstance(t_ffn_parks, list) else [t_ffn_parks]
        o8parks = t_p2_parks if isinstance(t_p2_parks, list) else [t_p2_parks]
        p5es = ExitStack()
        fx = [mksb(p5es, f"fx{i}", 128, DM, F32) for i in range(2)]
        ff = [mksb(p5es, f"ff{i}", 128, DM, F32) for i in range(2)]
        fo8 = [mksb(p5es, f"fo8{i}", 128, 1024, F32) for i in range(2)]
        facc = [mksb(p5es, f"facc{i}", 128, DM, F32) for i in range(2)]
        fout = [mksb(p5es, f"fout{i}", 128, DM, F32) for i in range(2)]
        ft = [None, None]
        fa = [None, None]
        for tt in range(T // 128):
            p = tt % 2
            t1 = S.emit("pool", lambda e, tt=tt, p=p: e.dma_start(
                fx[p][:], x_d[tt * 128:(tt + 1) * 128, :]),
                waits=[fa[p]] + ffparks, inc="dp5x%d" % p, inc_n=16)
            t2 = S.emit("pool", lambda e, tt=tt, p=p: e.dma_start(
                ff[p][:], ffn_d[tt * 128:(tt + 1) * 128, :]),
                waits=[fa[p]] + ffparks, inc="dp5f%d" % p, inc_n=16)
            src = o8_d.rearrange("(t h) d -> t h d", h=8)[tt * 128:(tt + 1) * 128]
            t3 = S.emit("pool", lambda e, p=p, src=src: e.dma_start(
                fo8[p][:].rearrange("p (h d) -> p h d", d=128), src),
                waits=[fa[p]] + o8parks, inc="dp5o%d" % p, inc_n=16)
            t4 = S.emit("dve", lambda e, p=p: e.tensor_add(
                facc[p][:], fx[p][:], ff[p][:]), waits=[t1, t2, fa[p]], inc="dve")
            tj = None
            for j in range(4):
                ov = fout[p][:].rearrange("p (d g j) -> p d g j", g=8, j=4)[:, :, :, j]
                av_ = facc[p][:].rearrange("p (d g j) -> p d g j", g=8, j=4)[:, :, :, j]
                o8v = fo8[p][:].rearrange("p (h d) -> p d h", d=128)
                tj = S.emit("dve", lambda e, ov=ov, av_=av_, o8v=o8v: e.tensor_add(
                    ov, av_, o8v), waits=[t4, t3] + ([ft[p]] if j == 0 else []),
                    inc="dve")
            fa[p] = tj
            t5 = S.emit("pool", lambda e, tt=tt, p=p: e.dma_start(
                out_d[tt * 128:(tt + 1) * 128, :], fout[p][:]),
                waits=[tj], inc="dout%d" % p, inc_n=16)
            ft[p] = t5
        p5es.close()

    for nm, sb_t in (("cst", cst), ("acc", acc)):
        if nm in dump_d:
            S.emit("pool", lambda e, d=dump_d[nm], s_=sb_t: e.dma_start(d, s_[:]),
                   waits=[t_quant_ready], inc="ddump", inc_n=16)
    final_counts = {k: v for k, v in S.count.items() if k.startswith("d")}
    S.emit("pool", lambda e: [e.wait_ge(sems[k], v) for k, v in final_counts.items()],
           raw=True)

    used_sems = set(S.count)
    for ops in S.ops.values():
        for _sg, ws, _, _, _ in ops:
            for s_, _v in ws:
                used_sems.add(s_)
    for s_ in sorted(used_sems):
        sems[s_] = es.enter_context(nc.semaphore(s_))

    def replay(eng_name, seg):
        def go(e):
            for sg, ws, fn, inc, inc_n in S.ops[eng_name]:
                if sg != seg:
                    continue
                for s_, v in ws:
                    e.wait_ge(sems[s_], v)
                instr = fn(e)
                if inc is not None and instr is not None:
                    instr.then_inc(sems[inc], inc_n)
        return go

    nseg = S.seg + 1
    for seg in range(nseg):
        with nc.Block() as block:
            block.sync(replay("sp", seg))
            block.tensor(replay("pe", seg))
            block.vector(replay("dve", seg))
            block.scalar(replay("act", seg))
            block.gpsimd(replay("pool", seg))
        if seg < nseg - 1 and bar:
            nc.all_core_barrier()

    es.close()
    return nc


def make_in_maps(inputs, n_cores=8):
    def cvt(a, shp=None):
        a = np.asarray(a)
        if shp is not None:
            a = a.reshape(shp)
        return np.ascontiguousarray(a.astype(np.float32, copy=False))
    x = cvt(inputs["L0_attn_x"], (8192, 4096))
    wq = cvt(inputs["L0_attn_Wq"], (4096, 4096))
    bq = cvt(inputs["L0_attn_bq"])
    wk = cvt(inputs["L0_attn_Wk"], (4096, 1024))
    bk = cvt(inputs["L0_attn_bk"])
    wv = cvt(inputs["L0_attn_Wv"], (4096, 1024))
    bv = cvt(inputs["L0_attn_bv"])
    wg = cvt(inputs["L0_ffn_w_gate"])
    wu = cvt(inputs["L0_ffn_w_up"])
    wd = cvt(inputs["L0_ffn_w_down"])
    HI = 11008 // 2
    maps = []
    for c in range(n_cores):
        q = c % 2
        maps.append({
            "x": x[c * T:(c + 1) * T], "wq": wq, "bq": bq, "wk": wk, "bk": bk,
            "wv": wv, "bv": bv,
            "wg": np.ascontiguousarray(wg[q * HI:(q + 1) * HI]),
            "wu": np.ascontiguousarray(wu[q * HI:(q + 1) * HI]),
            "wd": np.ascontiguousarray(wd[:, q * HI:(q + 1) * HI]),
        })
    return maps


def kernel(**inputs):
    from concourse.bass_utils import run_bass_kernel_spmd
    nc = build_program()
    maps = make_in_maps(inputs)
    res = run_bass_kernel_spmd(nc, maps, core_ids=list(range(8)))
    out = np.concatenate([np.asarray(res.results[c]["out"]) for c in range(8)], axis=0)
    return out.reshape(2, 4096, 4096)


# version 13 (final)
# speedup vs baseline: 1.0455x; 1.0228x over previous
import numpy as np
from contextlib import ExitStack

import concourse.bass as bass
import concourse.mybir as mybir

F32 = mybir.dt.float32
BF16 = mybir.dt.bfloat16
AF = mybir.ActivationFunctionType
ALU = mybir.AluOpType
AX = mybir.AxisListType

T = 1024
DM = 4096
EC = DM // 128
NCOL = T // 512
IFFN = 11008
BLK_CH = [22, 22, 22, 20]
BLK_OFF = [0, 22, 44, 66]
NW = float(IFFN * DM)
SCL = 0.08838834764831845

ENGINES = ("sp", "pe", "dve", "act", "pool")


class Sched:
    SELF_EDGE = ("dve", "act", "pool")

    def __init__(self):
        self.seg = 0
        self.ops = {e: [] for e in ENGINES}
        self.count = {}
        self.lastwait = {e: {} for e in ENGINES}
        self.serialize = False
        self.last_tok = {}
        self.prev_tok = {e: None for e in ENGINES}

    def emit(self, eng, fn, waits=(), inc=None, inc_n=1, raw=False):
        is_dma = (inc is not None and inc.endswith("dma")) or raw
        if not is_dma and eng in self.SELF_EDGE:
            if inc is None:
                inc = eng
                inc_n = 1
            waits = list(waits) + [self.prev_tok[eng]]
        if self.serialize and not raw:
            waits = list(waits) + [t for e2, t in self.last_tok.items() if e2 != eng]
            if inc is None:
                inc = {"sp": "spdma", "pool": "pooldma"}.get(eng, eng)
                inc_n = 16 if inc.endswith("dma") else 1
        ws = []
        for w in waits:
            if w is None:
                continue
            s, v = w
            lw = self.lastwait[eng].get(s, 0)
            if v > lw:
                ws.append((s, v))
                self.lastwait[eng][s] = v
        tok = None
        if inc is not None:
            self.count[inc] = self.count.get(inc, 0) + inc_n
            tok = (inc, self.count[inc])
        if self.serialize and tok is not None:
            self.last_tok[eng] = tok
        if not is_dma and eng in self.SELF_EDGE and tok is not None:
            self.prev_tok[eng] = tok
        self.ops[eng].append((self.seg, ws, fn, inc, inc_n))
        return tok


def tmax(*toks):
    best = None
    for t in toks:
        if t is None:
            continue
        if best is None or t[1] > best[1]:
            best = t
    return best


def build_program(debug=(), phases=("p0", "p1", "p2", "p3", "p4", "p5"), ndev=8, rc=True, bar=True):
    nc = bass.Bass("TRN2", target_bir_lowering=False, debug=False, num_devices=ndev, detect_race_conditions=rc)
    nc.reset()
    dbg = set(debug)

    x_d = nc.dram_tensor("x", [T, DM], F32, kind="ExternalInput").ap()
    wq_d = nc.dram_tensor("wq", [DM, DM], F32, kind="ExternalInput").ap()
    bq_d = nc.dram_tensor("bq", [128, 32], F32, kind="ExternalInput").ap()
    wk_d = nc.dram_tensor("wk", [DM, 1024], F32, kind="ExternalInput").ap()
    bk_d = nc.dram_tensor("bk", [128, 8], F32, kind="ExternalInput").ap()
    wv_d = nc.dram_tensor("wv", [DM, 1024], F32, kind="ExternalInput").ap()
    bv_d = nc.dram_tensor("bv", [128, 8], F32, kind="ExternalInput").ap()
    HI = IFFN // 2
    wg_d = nc.dram_tensor("wg", [HI, DM], F32, kind="ExternalInput").ap()
    wu_d = nc.dram_tensor("wu", [HI, DM], F32, kind="ExternalInput").ap()
    wd_d = nc.dram_tensor("wd", [DM, HI], F32, kind="ExternalInput").ap()
    FAMC = 86 * 32 * 128
    wsh_h = nc.dram_tensor("wsh", [128, 2 * FAMC], BF16, kind="Internal",
                           addr_space="Shared")
    wsh = wsh_h.ap()
    wshd_h = nc.dram_tensor("wshd", [128, FAMC], BF16, kind="Internal",
                            addr_space="Shared")
    wshd = wshd_h.ap()
    wsc_h = nc.dram_tensor("wsc", [2, 4], F32, kind="Internal", addr_space="Shared")
    wsc = wsc_h.ap()
    MYC = 43 * 32 * 128
    wloc_h = nc.dram_tensor("wloc", [128, 3 * MYC], BF16, kind="Internal")
    wloc = wloc_h.ap()
    out_d = nc.dram_tensor("out", [T, DM], F32, kind="ExternalOutput").ap()

    def scratch(name, shape):
        kind = "ExternalOutput" if name in dbg else "Internal"
        return nc.dram_tensor(name, shape, F32, kind=kind).ap()

    o8_d = scratch("o8_scr", [T * 8, 128])
    wc_d = nc.dram_tensor("wqkv_scr", [3 * DM, 1024], BF16, kind="Internal").ap()
    part_d = scratch("part_scr", [DM, T])
    ffn_d = scratch("ffn_scr", [T, DM])
    dump_d = {}
    for nm, w in (("xnT", EC * T), ("QS_dm", T * 8), ("K_dm", T * 8), ("V_dm", T * 8),
                  ("cst", 16), ("acc", 8), ("interT", 22 * T),
                  ("gtmp", 1024)):
        if nm in dbg:
            dump_d[nm] = nc.dram_tensor("dump_" + nm, [128, w], F32,
                                        kind="ExternalOutput").ap()

    S = Sched()
    es = ExitStack()

    def mksb(stack, name, p, f, dt):
        return stack.enter_context(nc.sbuf_tensor(name, [p, f], dt))

    ident_bf = mksb(es, "ident_bf", 128, 128, BF16)
    ident_f = mksb(es, "ident_f", 128, 128, F32)
    xnT = mksb(es, "xnT", 128, EC * T, BF16)
    bq_sb = mksb(es, "bq_sb", 128, 32, F32)
    bk_sb = mksb(es, "bk_sb", 128, 8, F32)
    bv_sb = mksb(es, "bv_sb", 128, 8, F32)
    bqs = mksb(es, "bqs", 128, 8, F32)
    bscr = mksb(es, "bscr", 128, 16, F32)
    cst = mksb(es, "cst", 128, 16, F32)
    ones1 = mksb(es, "ones1", 1, 128, F32)
    acc = mksb(es, "acc", 128, 8, F32)
    epsc = mksb(es, "epsc", 128, 2, F32)
    pstg = mksb(es, "pstg", 1, 4, F32)

    pbt = es.enter_context(nc.psum_tensor("pb", [128, 4096], F32))

    def bank(k, lo=0, n=512):
        return pbt[:, k * 512 + lo: k * 512 + lo + n]

    sems = {}

    S.emit("pool", lambda e: e.memset(ident_bf[:], 0.0), inc="pool")
    t_id = S.emit("pool", lambda e: e.affine_select(
        out=ident_bf[:], in_=ident_bf[:], compare_op=ALU.not_equal, fill=1.0,
        base=0, pattern=[[-1, 128]], channel_multiplier=1), inc="pool")
    S.emit("pool", lambda e: e.memset(ident_f[:], 0.0), inc="pool")
    t_idf = S.emit("pool", lambda e: e.affine_select(
        out=ident_f[:], in_=ident_f[:], compare_op=ALU.not_equal, fill=1.0,
        base=0, pattern=[[-1, 128]], channel_multiplier=1), inc="pool")
    S.emit("pool", lambda e: e.memset(cst[:, 0:1], 1.0), inc="pool")
    S.emit("pool", lambda e: e.memset(ones1[:], 1.0), inc="pool")
    t_acc0 = S.emit("pool", lambda e: e.memset(acc[:], 0.0), inc="pool")
    t_pstg0 = S.emit("pool", lambda e: e.memset(pstg[:], 0.0), inc="pool")
    S.emit("pool", lambda e: e.memset(epsc[:, 0:1], 1e-5), inc="pool")
    t_eps = S.emit("pool", lambda e: e.memset(epsc[:, 1:2], 0.5e-5), inc="pool")
    t_bq = S.emit("pool", lambda e: e.dma_start(bq_sb[:], bq_d), inc="dbias", inc_n=16)
    S.emit("pool", lambda e: e.dma_start(bk_sb[:], bk_d), inc="dbias", inc_n=16)
    t_bv = S.emit("pool", lambda e: e.dma_start(bv_sb[:], bv_d), inc="dbias", inc_n=16)
    bqv = bq_sb[:].rearrange("p (h j) -> p h j", j=4)
    S.emit("dve", lambda e: e.tensor_add(bscr[:, 0:8], bqv[:, :, 0], bqv[:, :, 1]),
           waits=[t_bv], inc="dve")
    t_b1 = S.emit("dve", lambda e: e.tensor_add(bscr[:, 8:16], bqv[:, :, 2], bqv[:, :, 3]),
                  inc="dve")
    t_b2 = S.emit("dve", lambda e: e.tensor_add(bscr[:, 0:8], bscr[:, 0:8], bscr[:, 8:16]),
                  waits=[t_b1], inc="dve")
    t_bqs = S.emit("dve", lambda e: e.tensor_scalar_mul(bqs[:], bscr[:, 0:8], SCL),
                   waits=[t_b2], inc="dve")

    p12 = ExitStack()
    pab = []
    QS_dm = mksb(p12, "QS_dm", 128, T * 8, BF16)
    K_dm = mksb(p12, "K_dm", 128, T * 8, BF16)
    V_dm = mksb(p12, "V_dm", 128, T * 8, BF16)
    qkT = [mksb(p12, f"qkT{i}", 8, 1024, BF16) for i in range(2)]
    attn_bf = mksb(p12, "attn_bf", 128, 4 * 512, BF16)
    attn_s = mksb(p12, "attn_s", 128, 2 * 512, BF16)
    attnT_sb = mksb(p12, "attnT_sb", 128, 2 * 512, BF16)
    sums4 = mksb(p12, "sums4", 128, 16, F32)
    negm4 = mksb(p12, "negm4", 128, 16, F32)
    rec4 = mksb(p12, "rec4", 128, 16, F32)
    o8sb = [mksb(p12, f"o8sb{i}", 128, 512, F32) for i in range(2)]
    o8stg = [mksb(p12, f"o8stg{i}", 128, 512, F32) for i in range(2)]

    pa_pieces = []
    if "p3" in phases:
        for mi, wsrc in ((0, wg_d), (1, wu_d)):
            for ch in range(HI // 128):
                pa_pieces.append((wsrc[ch * 128:(ch + 1) * 128, :], DM, mi))
        for hc in range(32):
            pa_pieces.append((wd_d[hc * 128:(hc + 1) * 128, 0:4096], 4096, 2))
            pa_pieces.append((wd_d[hc * 128:(hc + 1) * 128, 4096:HI],
                              HI - 4096, 2))
    pa = {"i": 0, "slot_tok": [None, None], "par": [0, 0, 0],
          "acc_tok": [t_acc0, t_acc0, t_acc0], "scr_tok": None}

    def passA_next(n):
        for _ in range(n):
            i = pa["i"]
            if i >= len(pa_pieces):
                return
            pa["i"] = i + 1
            src, w, mi = pa_pieces[i]
            sl = i % 2
            td = S.emit("sp", lambda e, src=src, sl=sl, w=w: e.dma_start(
                pab[sl][:, 0:w], src), waits=[pa["slot_tok"][sl]],
                inc="dpab%d" % (i % 4), inc_n=16)
            tras = []
            for k0 in range(0, w, 2048):
                k1 = min(w, k0 + 2048)
                tras.append(S.emit("act", lambda e, sl=sl, k0=k0, k1=k1,
                                   kk=len(tras): e.activation(
                    pab[sl][:, k0:k1], pab[sl][:, k0:k1], AF.Abs,
                    accum_out=acc[:, 6 + kk:7 + kk]),
                    waits=[td, pa["scr_tok"]], inc="act"))
            if len(tras) > 1:
                tr = S.emit("dve", lambda e: e.tensor_add(
                    acc[:, 6:7], acc[:, 6:7], acc[:, 7:8]),
                    waits=tras, inc="dve")
            else:
                tr = tras[0]
            par = pa["par"][mi]
            ca, cb = mi * 2 + par, mi * 2 + (1 - par)
            ta = S.emit("dve", lambda e, ca=ca, cb=cb: e.tensor_add(
                acc[:, cb:cb + 1], acc[:, ca:ca + 1], acc[:, 6:7]),
                waits=[tr, pa["acc_tok"][mi]], inc="dve")
            pa["par"][mi] = 1 - par
            pa["acc_tok"][mi] = ta
            pa["slot_tok"][sl] = ta
            pa["scr_tok"] = ta

    t_xnT_done = None
    if "p0" in phases:
        p0es = ExitStack()
        xbuf = [mksb(p0es, f"xb{i}", 128, DM, F32) for i in range(2)]
        xn_bf = [mksb(p0es, f"xn{i}", 128, DM, BF16) for i in range(2)]
        stats = mksb(p0es, "lnst", 128, 8, F32)
        slot_tok = [None, None]
        xn_war = [None, None]
        evac_tok = {0: None, 1: None}
        for tt in range(T // 128):
            p = tt % 2
            td = S.emit("pool", lambda e, tt=tt, p=p: e.dma_start(
                xbuf[p][:], x_d[tt * 128:(tt + 1) * 128, :]),
                waits=[slot_tok[p]], inc="dxb%d" % p, inc_n=16)
            t1 = S.emit("dve", lambda e, p=p: e.reduce_sum(
                stats[:, 4:5], xbuf[p][:], axis=AX.X), waits=[td], inc="dve")
            t2 = S.emit("dve", lambda e, p=p: e.tensor_scalar_mul(
                stats[:, p:p + 1], stats[:, 4:5], -1.0 / DM), waits=[t1], inc="dve")
            t3a = S.emit("act", lambda e, p=p: e.activation(
                bank(2, 0, 2048), xbuf[p][:, 0:2048], AF.Square,
                bias=stats[:, p:p + 1], accum_out=stats[:, 2 + p:3 + p]),
                waits=[t2], inc="act")
            t3b = S.emit("act", lambda e, p=p: e.activation(
                bank(2, 0, 2048), xbuf[p][:, 2048:4096], AF.Square,
                bias=stats[:, p:p + 1], accum_out=stats[:, 6:7]),
                waits=[t3a], inc="act")
            t3c = S.emit("dve", lambda e, p=p: e.tensor_add(
                stats[:, 2 + p:3 + p], stats[:, 2 + p:3 + p], stats[:, 6:7]),
                waits=[t3b], inc="dve")
            t4 = S.emit("act", lambda e, p=p: e.activation(
                stats[:, 2 + p:3 + p], stats[:, 2 + p:3 + p], AF.Ln,
                bias=epsc[:, 0:1], scale=1.0 / DM), waits=[t3c, t_eps], inc="act")
            t5 = S.emit("act", lambda e, p=p: e.activation(
                stats[:, 2 + p:3 + p], stats[:, 2 + p:3 + p], AF.Exp,
                scale=-0.5), waits=[t4], inc="act")
            t6 = S.emit("dve", lambda e, p=p: e.tensor_scalar(
                xn_bf[p][:], xbuf[p][:], stats[:, p:p + 1], stats[:, 2 + p:3 + p],
                ALU.add, ALU.mult), waits=[t5, xn_war[p]], inc="dve")
            slot_tok[p] = t6
            last_tr = None
            for q in range(8):
                bk_ = q % 2
                tts = None
                for c in range(4 * q, 4 * q + 4):
                    tts = S.emit("pe", lambda e, p=p, c=c, bk_=bk_: e.matmul(
                        bank(bk_, (c % 4) * 128, 128)[0:128, :],
                        xn_bf[p][:, c * 128:(c + 1) * 128], ident_bf[:],
                        start=True, stop=True, skip_group_check=True),
                        waits=[t6, t_id, evac_tok[bk_]], inc="pe")
                last_tr = tts
                dst = xnT[:].rearrange("p (c t) -> p c t", t=T)[
                    :, 4 * q:4 * q + 4, tt * 128:(tt + 1) * 128]
                if q % 2 == 0:
                    evac_tok[bk_] = S.emit("dve", lambda e, bk_=bk_, dst=dst: e.tensor_copy(
                        dst, bank(bk_)[:].rearrange("p (c t) -> p c t", t=128)),
                        waits=[tts], inc="dve")
                else:
                    evac_tok[bk_] = S.emit("act", lambda e, bk_=bk_, dst=dst: e.activation(
                        dst, bank(bk_)[:].rearrange("p (c t) -> p c t", t=128),
                        AF.Copy), waits=[tts], inc="act")
            xn_war[p] = last_tr
        t_xnT_done = [evac_tok[0], evac_tok[1]]
        p0es.close()
    pab.extend(mksb(p12, f"pab{i}", 128, 4096, F32) for i in range(2))

    t_p1_all = None
    if "p1" in phases:
        p1es = ExitStack()
        wbuf = [mksb(p1es, f"wb{i}", 128, DM, F32) for i in range(2)]
        tmp1 = mksb(p1es, "tmp1", 128, 1024, F32)
        wbf = [mksb(p1es, f"wbf{i}", 128, 1024, BF16) for i in range(2)]
        wslot_tok = [None, None]
        wbf_tok = {0: [], 1: []}
        wpark = {}
        bank_evac = {h: None for h in range(8)}
        for col in range(NCOL):
            tw = 512
            for mat, (wsrc, bias_sb, bscl) in enumerate((
                    (wq_d, bqs, SCL), (wk_d, bk_sb, 1.0), (wv_d, bv_sb, 1.0))):
                dm_dst = (QS_dm, K_dm, V_dm)[mat]
                mm_toks = [None] * 8
                for ec in range(EC):
                    p = ec % 2
                    wide = DM if mat == 0 else 1024
                    if col == 0:
                        td = S.emit("sp", lambda e, p=p, ec=ec, wsrc=wsrc, wide=wide: e.dma_start(
                            wbuf[p][:, 0:wide], wsrc[ec * 128:(ec + 1) * 128, :]),
                            waits=[wslot_tok[p]] + t_xnT_done, inc="dwb%d" % p, inc_n=16)
                        if mat == 0:
                            wv4 = wbuf[p][:, 0:DM].rearrange("p (c j) -> p c j", j=4)
                            S.emit("dve", lambda e, wv4=wv4: e.tensor_add(
                                tmp1[:], wv4[:, :, 0], wv4[:, :, 1]), waits=[td], inc="dve")
                            S.emit("dve", lambda e, wv4=wv4: e.tensor_add(
                                tmp1[:], tmp1[:], wv4[:, :, 2]), inc="dve")
                            S.emit("dve", lambda e, wv4=wv4: e.tensor_add(
                                tmp1[:], tmp1[:], wv4[:, :, 3]), inc="dve")
                            tc = S.emit("dve", lambda e, p=p: e.tensor_copy(
                                wbf[p][:], tmp1[:]), waits=wbf_tok[p], inc="dve")
                        else:
                            tc = S.emit("dve", lambda e, p=p: e.tensor_copy(
                                wbf[p][:], wbuf[p][:, 0:1024]),
                                waits=[td] + wbf_tok[p], inc="dve")
                        wslot_tok[p] = tc
                        tpk = S.emit("pool", lambda e, p=p, mat=mat, ec=ec: e.dma_start(
                            wc_d[mat * DM + ec * 128: mat * DM + (ec + 1) * 128, :],
                            wbf[p][:]), waits=[tc], inc="dwc%d" % p, inc_n=16)
                        wpark[(mat, ec)] = tpk
                    else:
                        tc = S.emit("sp", lambda e, p=p, mat=mat, ec=ec: e.dma_start(
                            wbf[p][:],
                            wc_d[mat * DM + ec * 128: mat * DM + (ec + 1) * 128, :]),
                            waits=wbf_tok[p] + [wpark[(mat, ec)], wslot_tok[p]],
                            inc="dwb%d" % p, inc_n=16)
                        wslot_tok[p] = tc
                    for hp in range(8):
                        lhs = wbf[p][:, hp:1024:8]
                        rhs = xnT[:, ec * T + col * 512: ec * T + col * 512 + tw]
                        mm_toks[hp] = S.emit("pe", lambda e, lhs=lhs, rhs=rhs, hp=hp, ec=ec: e.matmul(
                            bank(hp, 0, tw), lhs, rhs,
                            start=(ec == 0), stop=(ec == EC - 1), skip_group_check=True),
                            waits=[tc, bank_evac[hp]], inc="pe")
                    wbf_tok[p] = [mm_toks[7]] + ([wpark[(mat, ec)]] if col == 0 else [])
                    if ec % 8 == 3:
                        passA_next(1)
                for hp in range(8):
                    dst = dm_dst[:].rearrange("p (t h) -> p t h", h=8)[
                        :, col * 512:(col + 1) * 512, hp]
                    bias_ap = bias_sb[:, hp:hp + 1]
                    bank_evac[hp] = S.emit("dve", lambda e, hp=hp, dst=dst,
                                           bias_ap=bias_ap, bscl=bscl, tw=tw: e.tensor_scalar(
                        dst, bank(hp, 0, tw), bscl, bias_ap, ALU.mult, ALU.add),
                        waits=[mm_toks[hp], t_bv, t_bqs], inc="dve")
        t_p1_all = bank_evac[7]
        p1es.close()

    t_p2_parks = None
    park_final = {}
    t_last_av = None
    if "p2" in phases:
        G = T // 4
        qkT_tok = [None, None]
        trq_war = None
        sc_war = [None, None]
        at_war = [None, None]
        o8_war = [None, None]
        o8sb_war = [None, None]
        o8stg_war = [None, None]
        st = {}
        for tick in range(G + 6):
            g = tick
            if g < G:
                tk = None
                for i in range(4):
                    t = 4 * g + i
                    S.emit("pe", lambda e, t=t, i=i: e.matmul(
                        bank(i // 2, (i % 2) * 256, 128)[0:8, :],
                        QS_dm[:, 8 * t:8 * t + 8], ident_bf[:], start=True, stop=True, skip_group_check=True),
                        waits=[t_p1_all, trq_war, t_id], inc="pe")
                    tk = S.emit("pe", lambda e, t=t, i=i: e.matmul(
                        bank(i // 2, (i % 2) * 256 + 128, 128)[0:8, :],
                        K_dm[:, 8 * t:8 * t + 8], ident_bf[:], start=True, stop=True, skip_group_check=True),
                        inc="pe")
                st[(g, "tp")] = tk
                st[(g, "qke")] = S.emit("act", lambda e, g=g: e.activation(
                    qkT[g % 2][0:8, :], bank(0, 0, 1024)[0:8, :], AF.Copy),
                    waits=[st[(g, "tp")], qkT_tok[g % 2]], inc="act")
                trq_war = st[(g, "qke")]
            g = tick - 1
            if 0 <= g < G:
                tk = None
                for i in range(4):
                    tk = S.emit("pe", lambda e, g=g, i=i: e.matmul(
                        bank(2 + g % 2, i * 128, 128),
                        qkT[g % 2][0:8, i * 256:i * 256 + 128],
                        qkT[g % 2][0:8, i * 256 + 128:i * 256 + 256],
                        start=True, stop=True, skip_group_check=True),
                        waits=[st[(g, "qke")], sc_war[g % 2]], inc="pe")
                st[(g, "sc")] = tk
                qkT_tok[g % 2] = tk
                st[(g, "nm")] = S.emit("dve", lambda e, g=g: e.reduce_max(
                    negm4[:, (g % 4):(g % 4) + 1], bank(2 + g % 2),
                    axis=AX.X, negate=True), waits=[st[(g, "sc")]], inc="dve")
            g = tick - 2
            if 0 <= g < G:
                tk = S.emit("act", lambda e, g=g: e.activation(
                    attn_bf[:, (g % 4) * 512:(g % 4) * 512 + 512],
                    bank(2 + g % 2), AF.Exp,
                    bias=negm4[:, (g % 4):(g % 4) + 1]),
                    waits=[st[(g, "nm")]], inc="act")
                tk = S.emit("dve", lambda e, g=g: e.reduce_sum(
                    sums4[:, (g % 4) * 4:(g % 4) * 4 + 4],
                    attn_bf[:, (g % 4) * 512:(g % 4) * 512 + 512].rearrange(
                        "p (a b) -> p a b", b=128), axis=AX.X),
                    waits=[tk], inc="dve")
                st[(g, "exp")] = tk
                sc_war[g % 2] = tk
            g = tick - 3
            if 0 <= g < G:
                tr = S.emit("dve", lambda e, g=g: e.reciprocal(
                    rec4[:, (g % 4) * 4:(g % 4) * 4 + 4],
                    sums4[:, (g % 4) * 4:(g % 4) * 4 + 4]),
                    waits=[st[(g, "exp")]], inc="dve")
                tk = None
                for i in range(4):
                    tk = S.emit("dve", lambda e, g=g, i=i: e.tensor_scalar_mul(
                        attn_s[:, (g % 2) * 512 + i * 128:(g % 2) * 512 + (i + 1) * 128],
                        attn_bf[:, (g % 4) * 512 + i * 128:(g % 4) * 512 + (i + 1) * 128],
                        rec4[:, (g % 4) * 4 + i:(g % 4) * 4 + i + 1]),
                        waits=[tr, st.get((g - 2, "atT"))], inc="dve")
                st[(g, "sca")] = tk
            g = tick - 4
            if 0 <= g < G:
                tk = None
                for i in range(4):
                    tk = S.emit("pe", lambda e, g=g, i=i: e.matmul(
                        bank(4 + g % 2, i * 128, 128),
                        attn_s[:, (g % 2) * 512 + i * 128:(g % 2) * 512 + (i + 1) * 128],
                        ident_bf[:], start=True, stop=True, skip_group_check=True),
                        waits=[st[(g, "sca")], at_war[g % 2]], inc="pe")
                st[(g, "atT")] = tk
                tk = None
                for i in range(4):
                    tk = S.emit("act", lambda e, g=g, i=i: e.activation(
                        attnT_sb[:, (g % 2) * 512 + i * 128:(g % 2) * 512 + (i + 1) * 128],
                        bank(4 + g % 2, i * 128, 128), AF.Copy),
                        waits=[st[(g, "atT")], st.get((g - 2, "av"))], inc="act")
                st[(g, "atE")] = tk
                at_war[g % 2] = tk
            g = tick - 5
            if 0 <= g < G:
                tb = (g // 16) % 2
                tk = None
                for i in range(4):
                    t = 4 * g + i
                    l = (g % 16) * 4 + i
                    tk = S.emit("pe", lambda e, i=i, g=g, t=t, l=l, tb=tb: e.matmul(
                        bank(6 + tb, l * 8, 8),
                        attnT_sb[:, (g % 2) * 512 + i * 128:(g % 2) * 512 + (i + 1) * 128],
                        V_dm[:, 8 * t:8 * t + 8], start=True, stop=True, skip_group_check=True),
                        waits=[st[(g, "atE")], o8_war[tb]], inc="pe")
                st[(g, "av")] = tk
                t_last_av = tk
                if g % 16 == 15:
                    tbk = g // 16
                    te = S.emit("act", lambda e, tb=tb, tbk=tbk: e.activation(
                        o8sb[tbk % 2][:], bank(6 + tb), AF.Copy),
                        waits=[tk, o8sb_war[tbk % 2]], inc="act")
                    tt4 = None
                    for j in range(4):
                        tt4 = S.emit("pe", lambda e, tb=tb, tbk=tbk, j=j: e.matmul(
                            bank(6 + tb, j * 128, 128),
                            o8sb[tbk % 2][:, j * 128:(j + 1) * 128], ident_f[:],
                            start=True, stop=True, skip_group_check=True), waits=[te, t_idf], inc="pe")
                    o8sb_war[tbk % 2] = tt4
                    tv = S.emit("dve", lambda e, tb=tb, tbk=tbk: e.tensor_copy(
                        o8stg[tbk % 2][:], bank(6 + tb)),
                        waits=[tt4, o8stg_war[tbk % 2]], inc="dve")
                    o8_war[tb] = tv
                    src = o8stg[tbk % 2][:].rearrange("p (j d) -> p j d", d=128)
                    dst = o8_d.rearrange("(b j p) d -> b p j d", j=4, p=128)[tbk]
                    tpk = S.emit("pool", lambda e, src=src, dst=dst: e.dma_start(
                        dst, src), waits=[tv], inc="dpk%d" % (tbk % 2), inc_n=16)
                    o8stg_war[tbk % 2] = tpk
                    park_final[tpk[0]] = tpk
            passA_next(2)
        t_p2_parks = list(park_final.values())

    if "p3" in phases:
        passA_next(len(pa_pieces))
        tev = None
        for mi in range(3):
            ca = mi * 2 + pa["par"][mi]
            tm = S.emit("pe", lambda e, ca=ca, mi=mi: e.matmul(
                bank(0, mi, 1)[0:1, :], acc[:, ca:ca + 1], cst[:, 0:1],
                start=True, stop=True, skip_group_check=True),
                waits=[pa["acc_tok"][mi], t_last_av], inc="pe")
            tev = S.emit("dve", lambda e, mi=mi: e.tensor_copy(
                pstg[0:1, mi:mi + 1], bank(0, mi, 1)[0:1, :]),
                waits=[tm, t_pstg0], inc="dve")
        def pub(e):
            q = e.partition_id() % 2
            return e.dma_start(bass.AP(wsc_h, q * 4, [[4, 1], [1, 4]]), pstg[0:1, :])
        tpub = S.emit("pool", pub, waits=[tev], inc="dwsc", inc_n=16)
        S.emit("pool", lambda e: e.wait_ge(sems["dwsc"], 16), raw=True)

    for nm, sbuf_t in (("xnT", xnT), ("QS_dm", QS_dm), ("K_dm", K_dm), ("V_dm", V_dm)):
        if nm in dump_d:
            S.emit("pool", lambda e, d=dump_d[nm], s=sbuf_t: e.dma_start(
                d, s[:]), waits=[t_p1_all, t_last_av] + (t_xnT_done or []), inc="ddump", inc_n=16)
    if "p2" in phases:
        S.emit("pool", lambda e: [e.wait_ge(sems[k], v) for k, v in
                                  {kk: vv for kk, vv in S.count.items()
                                   if kk.startswith("dpk")}.items()], raw=True)
    p12.close()

    t_quant_ready = None
    if "p3" in phases:
        S.seg = 1
        qwes = ExitStack()
        wsc_sb = mksb(qwes, "wsc_sb", 1, 8, F32)
        sbf = [mksb(qwes, f"sbf{i}", 128, 2048, F32) for i in range(6)]
        tqa1 = mksb(qwes, "tqa1", 128, 2048, BF16)
        tqa2 = mksb(qwes, "tqa2", 128, 2048, BF16)
        tqd1 = mksb(qwes, "tqd1", 128, 2048, BF16)
        tqd2 = mksb(qwes, "tqd2", 128, 2048, BF16)
        qg = [mksb(qwes, f"qg{i}", 128, 2048, BF16) for i in range(4)]
        stgw = [mksb(qwes, f"stgw{i}", 128, 8192, BF16) for i in range(3)]

        trd = S.emit("pool", lambda e: e.dma_start(
            wsc_sb[0:1, :], wsc.rearrange("a b -> (a b)").unsqueeze(0)),
            inc="dwsc", inc_n=16)
        ttot = S.emit("dve", lambda e: e.tensor_add(
            cst[0:1, 12:15], wsc_sb[0:1, 0:3], wsc_sb[0:1, 4:7]),
            waits=[trd], inc="dve")
        tsc = []
        for mi in range(3):
            tsc.append(S.emit("dve", lambda e, mi=mi: e.tensor_scalar(
                cst[0:1, 9 + mi:10 + mi], cst[0:1, 12 + mi:13 + mi],
                0.5 / NW, 0.5e-5, ALU.mult, ALU.add), waits=[ttot], inc="dve"))
        tsu = S.emit("dve", lambda e: e.tensor_scalar_mul(
            cst[0:1, 15:16], cst[0:1, 13:14], 1.0 / NW), waits=[ttot], inc="dve")
        tsg = S.emit("dve", lambda e: e.tensor_scalar_mul(
            cst[0:1, 13:14], cst[0:1, 12:13], 0.25 / NW),
            waits=[tsu], inc="dve")
        tsd = S.emit("dve", lambda e: e.tensor_scalar_mul(
            cst[0:1, 14:15], cst[0:1, 14:15], 0.5 / NW),
            waits=[tsc[2]], inc="dve")
        tguP = S.emit("dve", lambda e: e.tensor_mul(
            cst[0:1, 15:16], cst[0:1, 15:16], cst[0:1, 13:14]),
            waits=[tsu, tsg], inc="dve")
        bc = [(9, 1, tsc[0]), (10, 3, tsc[1]), (11, 5, tsc[2]),
              (15, 7, tguP), (14, 8, tsd)]
        tbc = []
        for sc_, dc, wt in bc:
            tm = S.emit("pe", lambda e, sc_=sc_, dc=dc: e.matmul(
                bank(1, dc, 1), ones1[0:1, :], cst[0:1, sc_:sc_ + 1],
                start=True, stop=True, skip_group_check=True), waits=[wt], inc="pe")
            tbc.append(S.emit("dve", lambda e, dc=dc: e.tensor_copy(
                cst[:, dc:dc + 1], bank(1, dc, 1)), waits=[tm], inc="dve"))
        tn1 = S.emit("dve", lambda e: e.tensor_scalar_mul(
            cst[:, 2:3], cst[:, 1:2], -1.0), waits=[tbc[0]], inc="dve")
        tn2 = S.emit("dve", lambda e: e.tensor_scalar_mul(
            cst[:, 4:5], cst[:, 3:4], -1.0), waits=[tbc[1]], inc="dve")
        tn3 = S.emit("dve", lambda e: e.tensor_scalar_mul(
            cst[:, 6:7], cst[:, 5:6], -1.0), waits=[tbc[2]], inc="dve")
        t_quant_ready = S.emit("dve", lambda e: e.engine_nop(),
                               waits=[tn1, tn2, tn3, tbc[3], tbc[4]], inc="dve")

        sb_tok = [None] * 6
        qg_tok = [None, None, None, None]
        tr_war = {4: None, 5: None}
        stg_tok = [None, None, None]
        last_comb = [None, None]
        cki = [0]
        sbi = [0]
        wri = [0]
        last_wr = {}
        pid_cache = {}

        def myq(e):
            if id(e) not in pid_cache:
                pid_cache[id(e)] = e.snap(e.partition_id() % 2,
                                          min_val=0, max_val=1)
            return pid_cache[id(e)]

        def qw_load(src, width):
            sl = sbi[0] % 6
            sbi[0] += 1
            t = S.emit("sp", lambda e, src=src, sl=sl, width=width: e.dma_start(
                sbf[sl][:, 0:width], src), waits=[sb_tok[sl]],
                inc="dsbf%d" % sl, inc_n=16)
            return sl, t

        def qw_quant(sl, off, width, ccol, nccol, wdma):
            k = cki[0]
            cki[0] += 1
            path = k % 2
            qs = path * 2 + (k // 2) % 2
            if path == 0:
                ta = S.emit("act", lambda e, sl=sl, off=off, width=width, c=nccol: e.sign(
                    tqa1[:, 0:width], sbf[sl][:, off:off + width],
                    bias=cst[:, c:c + 1]),
                    waits=[wdma, t_quant_ready, last_comb[0]], inc="act")
                tb = S.emit("act", lambda e, sl=sl, off=off, width=width, c=ccol: e.sign(
                    tqa2[:, 0:width], sbf[sl][:, off:off + width],
                    bias=cst[:, c:c + 1]), inc="act")
                tc = S.emit("pool", lambda e, width=width, qs=qs: e.tensor_add(
                    qg[qs][:, 0:width], tqa1[:, 0:width], tqa2[:, 0:width]),
                    waits=[ta, tb, qg_tok[qs]], inc="pool")
                last_comb[0] = tc
            else:
                ta = S.emit("dve", lambda e, sl=sl, off=off, width=width, c=ccol: e.tensor_scalar(
                    tqd1[:, 0:width], sbf[sl][:, off:off + width],
                    cst[:, c:c + 1], 2.0, ALU.is_gt, ALU.mult),
                    waits=[wdma, t_quant_ready, last_comb[1]], inc="dve")
                tb = S.emit("dve", lambda e, sl=sl, off=off, width=width, c=nccol: e.tensor_scalar(
                    tqd2[:, 0:width], sbf[sl][:, off:off + width],
                    cst[:, c:c + 1], -2.0, ALU.is_lt, ALU.mult), inc="dve")
                tc = S.emit("pool", lambda e, width=width, qs=qs: e.tensor_add(
                    qg[qs][:, 0:width], tqd1[:, 0:width], tqd2[:, 0:width]),
                    waits=[ta, tb, qg_tok[qs]], inc="pool")
                last_comb[1] = tc
            return qs, tc

        def qw_transposes(qs, nblk, sg, stoff, tq):
            tpe = None
            ev = None
            for j0 in range(0, nblk, 4):
                bk_ = 4 + (j0 // 4) % 2
                nn = min(4, nblk - j0)
                tts = None
                for j in range(j0, j0 + nn):
                    tts = S.emit("pe", lambda e, qs=qs, j=j, bk_=bk_, j0=j0: e.matmul(
                        bank(bk_, (j - j0) * 128, 128),
                        qg[qs][:, j * 128:(j + 1) * 128], ident_bf[:],
                        start=True, stop=True, skip_group_check=True),
                        waits=[tq, tr_war[bk_], t_id], inc="pe")
                tpe = tts
                ev = S.emit("dve", lambda e, bk_=bk_, nn=nn, sg=sg,
                            c0=stoff + j0 * 128: e.tensor_copy(
                    stgw[sg][:, c0:c0 + nn * 128], bank(bk_, 0, nn * 128)),
                    waits=[tts, stg_tok[sg]], inc="dve")
                tr_war[bk_] = ev
            return tpe, ev

        def qw_write(dst_c0, width, ev):
            q_ = wri[0] % 4
            sg = wri[0] % 3
            wri[0] += 1
            tw = S.emit("pool", lambda e, c0=dst_c0, width=width, sg=sg: e.dma_start(
                wloc[:, c0:c0 + width], stgw[sg][:, 0:width]),
                waits=[ev, last_wr.get(q_)], inc="dwr%d" % q_, inc_n=16)
            stg_tok[sg] = tw
            last_wr[q_] = tw
            return tw

        def biggu(icl0, icl1, sem):
            def cp(e, icl0=icl0, icl1=icl1):
                q = myq(e)
                nc_ = (icl1 - icl0) * 8192
                dst = bass.AP(wsh_h, q * (2 * MYC) + icl0 * 8192,
                              [[2 * FAMC, 128], [1, nc_]])
                src = bass.AP(wloc_h, icl0 * 8192, [[3 * MYC, 128], [1, nc_]])
                return e.dma_start(dst, src)
            S.emit("pool", cp, waits=list(last_wr.values()), inc=sem, inc_n=16)

        for icl in range(43):
            sg = wri[0] % 3
            ev = None
            for fam, wsrc, ccol, nccol in ((0, wg_d, 1, 2), (1, wu_d, 3, 4)):
                for hf in range(2):
                    sl, td = qw_load(
                        wsrc[icl * 128:(icl + 1) * 128,
                             hf * 2048:(hf + 1) * 2048], 2048)
                    qs, tc = qw_quant(sl, 0, 2048, ccol, nccol, td)
                    tpe, ev = qw_transposes(qs, 16, sg, fam * 4096 + hf * 2048, tc)
                    qg_tok[qs] = tpe
                    sb_tok[sl] = tc
            qw_write(icl * 8192, 8192, ev)
            if icl % 8 == 7:
                biggu(icl - 7, icl + 1, "dsh0")
        biggu(40, 43, "dsh0")

        def bigdn(hc0, hc1, sem):
            def cp(e, hc0=hc0, hc1=hc1):
                q = myq(e)
                dst = bass.AP(wshd_h, q * (43 * 128) + hc0 * (86 * 128),
                              [[FAMC, 128], [86 * 128, hc1 - hc0], [1, 43 * 128]])
                src = bass.AP(wloc_h, 2 * MYC + hc0 * (43 * 128),
                              [[3 * MYC, 128], [43 * 128, hc1 - hc0],
                               [1, 43 * 128]])
                return e.dma_start(dst, src)
            S.emit("pool", cp, waits=list(last_wr.values()), inc=sem, inc_n=16)

        for hc in range(32):
            sg = wri[0] % 3
            ev = None
            for pc in range(3):
                w0 = pc * 2048
                w1 = min(HI, w0 + 2048)
                sl, td = qw_load(wd_d[hc * 128:(hc + 1) * 128, w0:w1], w1 - w0)
                qs, tc = qw_quant(sl, 0, w1 - w0, 5, 6, td)
                tpe, ev = qw_transposes(qs, (w1 - w0) // 128, sg, pc * 2048, tc)
                qg_tok[qs] = tpe
                sb_tok[sl] = tc
            qw_write(2 * MYC + hc * 5504, 5504, ev)
            if hc % 8 == 7:
                bigdn(hc - 7, hc + 1, "dsh1")
        S.emit("pool", lambda e: [e.wait_ge(sems["dsh%d" % f],
                                            S.count["dsh%d" % f])
                                  for f in range(2)], raw=True)
        qwes.close()
        S.seg = 2

    t_ffn_parks = None
    if "p4" in phases:
        p4es = ExitStack()
        interT = mksb(p4es, "interT", 128, 22 * T, BF16)
        slabGU = [mksb(p4es, f"slabGU{i}", 128, 8192, BF16) for i in range(2)]
        slabD = [mksb(p4es, f"slabD{i}", 128, 2816, BF16) for i in range(2)]
        gtmp = mksb(p4es, "gtmp", 128, 1024, BF16)
        ptile = [mksb(p4es, f"pt{i}", 128, 1024, F32) for i in range(2)]
        dstg = [mksb(p4es, f"dstg{i}", 128, 1024, F32) for i in range(2)]
        fbuf = [mksb(p4es, f"fbuf{i}", 128, 1024, F32) for i in range(8)]

        gu_war = {0: None, 1: None, 2: None, 3: None}
        dn_war = {4: None, 5: None, 6: None, 7: None}
        tr_war = {0: None, 1: None}
        slab_tok = {0: None, 1: None}
        slabD_tok = [None, None]
        gtmp_tok = [None, None]
        ptile_tok = [None, None]
        dstg_tok = [None, None]
        fbuf_tok = [None] * 8
        fbuf_ev = [None] * 8
        dff_last = {0: None, 1: None}
        down_read_done = None
        last_park = {}
        ffn_park = {}

        def prep_gu(ic):
            par = ic % 2
            tok = S.emit("sp", lambda e, ic=ic, par=par: e.dma_start(
                slabGU[par][:], wsh[:, ic * 8192:(ic + 1) * 8192]),
                waits=[slab_tok[par]], inc="dslGU%d" % par, inc_n=16)
            return tok, par

        def comp_gu(b, lc, ctx):
            tok, par = ctx
            mmg = {}
            mmu = {}
            for hb in range(32):
                for c in range(NCOL):
                    tw = 512
                    rhs = xnT[:, hb * T + c * 512: hb * T + c * 512 + tw]
                    mmg[c] = S.emit("pe", lambda e, hb=hb, c=c, rhs=rhs, par=par: e.matmul(
                        bank(c, 0, tw), slabGU[par][:, hb * 128:(hb + 1) * 128], rhs,
                        start=(hb == 0), stop=(hb == 31), skip_group_check=True),
                        waits=[tok, gu_war[c]], inc="pe")
                for c in range(NCOL):
                    tw = 512
                    rhs = xnT[:, hb * T + c * 512: hb * T + c * 512 + tw]
                    mmu[c] = S.emit("pe", lambda e, hb=hb, c=c, rhs=rhs, par=par: e.matmul(
                        bank(2 + c, 0, tw),
                        slabGU[par][:, 4096 + hb * 128:4096 + (hb + 1) * 128], rhs,
                        start=(hb == 0), stop=(hb == 31), skip_group_check=True),
                        waits=[tok, gu_war[2 + c]], inc="pe")
            slab_tok[par] = mmu[NCOL - 1]
            for c in range(NCOL):
                tw = 512
                tgv = S.emit("dve", lambda e, c=c, tw=tw: e.tensor_scalar(
                    gtmp[:, c * 512:c * 512 + tw], bank(c, 0, tw),
                    cst[:, 7:8], None, ALU.mult),
                    waits=[mmg[c], gtmp_tok[c], t_quant_ready], inc="dve")
                gu_war[c] = tgv
                ti = S.emit("dve", lambda e, c=c, lc=lc, tw=tw: e.tensor_mul(
                    interT[:, lc * T + c * 512: lc * T + c * 512 + tw],
                    gtmp[:, c * 512:c * 512 + tw], bank(2 + c, 0, tw)),
                    waits=[tgv, mmu[c], dn_read[b]], inc="dve")
                gu_war[2 + c] = ti
                gtmp_tok[c] = ti
                inter_tok[(b, lc)] = ti

        def prep_dn(b, hc):
            dsl = hc % 2
            nch = BLK_CH[b]
            c0 = (hc * 86 + BLK_OFF[b]) * 128
            src = wshd[:, c0:c0 + nch * 128]
            dst = slabD[dsl][:, 0:nch * 128]
            t = S.emit("sp", lambda e, src=src, dst=dst: e.dma_start(dst, src),
                       waits=[slabD_tok[dsl]], inc="dsld%d" % (hc % 4), inc_n=16)
            return t, dsl

        def comp_dn(b, hc, ctx):
            tld, dsl = ctx
            nch = BLK_CH[b]
            t_inter_all = tmax(*[inter_tok[(b, l)] for l in range(nch)])
            last_dn = None
            mmc = {}
            dbkc = {}
            for c in range(NCOL):
                dbkc[c] = 4 + (hc * NCOL + c) % 4
            for j in range(nch):
                for c in range(NCOL):
                    tw = 512
                    dbk = dbkc[c]
                    mm = S.emit("pe", lambda e, j=j, c=c, dbk=dbk, dsl=dsl, tw=tw: e.matmul(
                        bank(dbk, 0, tw), slabD[dsl][:, j * 128:(j + 1) * 128],
                        interT[:, j * T + c * 512: j * T + c * 512 + tw],
                        start=(j == 0), stop=(j == nch - 1), skip_group_check=True),
                        waits=[tld, t_inter_all, dn_war[dbk]], inc="pe")
                    mmc[c] = mm
                    last_dn = mm
            if b == 0:
                evs = []
                for c in range(NCOL):
                    tev = S.emit("act", lambda e, dbk=dbkc[c], dsl=dsl, c=c: e.activation(
                        dstg[dsl][:, c * 512:(c + 1) * 512], bank(dbk), AF.Copy),
                        waits=[mmc[c], dstg_tok[dsl]], inc="act")
                    dn_war[dbkc[c]] = tev
                    evs.append(tev)
                tp = S.emit("pool", lambda e, hc=hc, dsl=dsl: e.dma_start(
                    part_d[hc * 128:(hc + 1) * 128, :], dstg[dsl][:]),
                    waits=evs, inc="ddst%d" % dsl, inc_n=16)
                dstg_tok[dsl] = tp
                last_park[hc] = tp
            else:
                tl = S.emit("pool", lambda e, hc=hc, dsl=dsl: e.dma_start(
                    ptile[dsl][:], part_d[hc * 128:(hc + 1) * 128, :]),
                    waits=[ptile_tok[dsl], last_park[hc]],
                    inc="dpt%d" % dsl, inc_n=16)
                tas = []
                for c in range(NCOL):
                    ta = S.emit("dve", lambda e, dbk=dbkc[c], dsl=dsl, c=c: e.tensor_add(
                        dstg[dsl][:, c * 512:(c + 1) * 512], bank(dbk),
                        ptile[dsl][:, c * 512:(c + 1) * 512]),
                        waits=[mmc[c], tl, dstg_tok[dsl]], inc="dve")
                    dn_war[dbkc[c]] = ta
                    tas.append(ta)
                ptile_tok[dsl] = tas[-1]
                if b < 3:
                    tp = S.emit("pool", lambda e, hc=hc, dsl=dsl: e.dma_start(
                        part_d[hc * 128:(hc + 1) * 128, :], dstg[dsl][:]),
                        waits=tas, inc="ddst%d" % dsl, inc_n=16)
                    dstg_tok[dsl] = tp
                    last_park[hc] = tp
                else:
                    last_tmf = None
                    for c in range(NCOL):
                        for tbj in range(4):
                            bkf = tbj % 2
                            fi = c * 4 + tbj
                            tmf = S.emit("pe", lambda e, c=c, tbj=tbj, dsl=dsl, bkf=bkf: e.matmul(
                                bank(bkf, 0, 128),
                                dstg[dsl][:, c * 512 + tbj * 128:c * 512 + (tbj + 1) * 128],
                                ident_f[:], start=True, stop=True, skip_group_check=True),
                                waits=[tas[c], tr_war[bkf], t_idf], inc="pe")
                            last_tmf = tmf
                            tef = S.emit("dve", lambda e, fi=fi, hc=hc, bkf=bkf: e.tensor_scalar(
                                fbuf[fi][:, (hc % 8) * 128:(hc % 8 + 1) * 128],
                                bank(bkf, 0, 128), cst[:, 8:9], None, ALU.mult),
                                waits=[tmf, fbuf_tok[fi]], inc="dve")
                            tr_war[bkf] = tef
                            fbuf_ev[fi] = tef
                    dstg_tok[dsl] = last_tmf
                    if hc % 8 == 7:
                        g8 = hc // 8
                        for c in range(NCOL):
                            for tbj in range(4):
                                fi = c * 4 + tbj
                                qi = fi % 2
                                tw_ = S.emit("pool", lambda e, c=c, tbj=tbj, fi=fi, g8=g8: e.dma_start(
                                    ffn_d[c * 512 + tbj * 128:c * 512 + (tbj + 1) * 128,
                                          g8 * 1024:(g8 + 1) * 1024],
                                    fbuf[fi][:]),
                                    waits=[fbuf_ev[fi], dff_last[qi]],
                                    inc="dff%d" % qi, inc_n=16)
                                fbuf_tok[fi] = tw_
                                dff_last[qi] = tw_
                                ffn_park[tw_[0]] = tw_
            slabD_tok[dsl] = last_dn
            dn_read[b + 1] = last_dn

        inter_tok = {}
        dn_read = {0: None, 1: None, 2: None, 3: None, 4: None}
        stages = []
        for b in range(4):
            stages += [("gu", b, lc) for lc in range(BLK_CH[b])]
            stages += [("dn", b, hc) for hc in range(32)]

        ctxs = [None] * len(stages)
        for i in range(len(stages) + 1):
            if i >= 1:
                kind, b, idx = stages[i - 1]
                if kind == "gu":
                    comp_gu(b, idx, ctxs[i - 1])
                else:
                    comp_dn(b, idx, ctxs[i - 1])
            if i < len(stages):
                kind, b, idx = stages[i]
                ctxs[i] = prep_gu(BLK_OFF[b] + idx) if kind == "gu" else prep_dn(b, idx)
        t_ffn_parks = list(ffn_park.values())
        for nm, sb_t in (("interT", interT), ("gtmp", gtmp)):
            if nm in dump_d:
                S.emit("pool", lambda e, d=dump_d[nm], s_=sb_t: e.dma_start(
                    d, s_[:]), waits=list(t_ffn_parks), inc="ddump", inc_n=16)
        p4es.close()

    if "p5" in phases:
        ffparks = t_ffn_parks if isinstance(t_ffn_parks, list) else [t_ffn_parks]
        o8parks = t_p2_parks if isinstance(t_p2_parks, list) else [t_p2_parks]
        p5es = ExitStack()
        fx = [mksb(p5es, f"fx{i}", 128, DM, F32) for i in range(2)]
        ff = [mksb(p5es, f"ff{i}", 128, DM, F32) for i in range(2)]
        fo8 = [mksb(p5es, f"fo8{i}", 128, 1024, F32) for i in range(2)]
        facc = [mksb(p5es, f"facc{i}", 128, DM, F32) for i in range(2)]
        fout = [mksb(p5es, f"fout{i}", 128, DM, F32) for i in range(2)]
        ft = [None, None]
        fa = [None, None]
        for tt in range(T // 128):
            p = tt % 2
            t1 = S.emit("pool", lambda e, tt=tt, p=p: e.dma_start(
                fx[p][:], x_d[tt * 128:(tt + 1) * 128, :]),
                waits=[fa[p]] + ffparks, inc="dp5x%d" % p, inc_n=16)
            t2 = S.emit("pool", lambda e, tt=tt, p=p: e.dma_start(
                ff[p][:], ffn_d[tt * 128:(tt + 1) * 128, :]),
                waits=[fa[p]] + ffparks, inc="dp5f%d" % p, inc_n=16)
            src = o8_d.rearrange("(t h) d -> t h d", h=8)[tt * 128:(tt + 1) * 128]
            t3 = S.emit("pool", lambda e, p=p, src=src: e.dma_start(
                fo8[p][:].rearrange("p (h d) -> p h d", d=128), src),
                waits=[fa[p]] + o8parks, inc="dp5o%d" % p, inc_n=16)
            t4 = S.emit("dve", lambda e, p=p: e.tensor_add(
                facc[p][:], fx[p][:], ff[p][:]), waits=[t1, t2, fa[p]], inc="dve")
            tj = None
            for j in range(4):
                ov = fout[p][:].rearrange("p (d g j) -> p d g j", g=8, j=4)[:, :, :, j]
                av_ = facc[p][:].rearrange("p (d g j) -> p d g j", g=8, j=4)[:, :, :, j]
                o8v = fo8[p][:].rearrange("p (h d) -> p d h", d=128)
                tj = S.emit("dve", lambda e, ov=ov, av_=av_, o8v=o8v: e.tensor_add(
                    ov, av_, o8v), waits=[t4, t3] + ([ft[p]] if j == 0 else []),
                    inc="dve")
            fa[p] = tj
            t5 = S.emit("pool", lambda e, tt=tt, p=p: e.dma_start(
                out_d[tt * 128:(tt + 1) * 128, :], fout[p][:]),
                waits=[tj], inc="dout%d" % p, inc_n=16)
            ft[p] = t5
        p5es.close()

    for nm, sb_t in (("cst", cst), ("acc", acc)):
        if nm in dump_d:
            S.emit("pool", lambda e, d=dump_d[nm], s_=sb_t: e.dma_start(d, s_[:]),
                   waits=[t_quant_ready], inc="ddump", inc_n=16)
    final_counts = {k: v for k, v in S.count.items() if k.startswith("d")}
    S.emit("pool", lambda e: [e.wait_ge(sems[k], v) for k, v in final_counts.items()],
           raw=True)

    used_sems = set(S.count)
    for ops in S.ops.values():
        for _sg, ws, _, _, _ in ops:
            for s_, _v in ws:
                used_sems.add(s_)
    for s_ in sorted(used_sems):
        sems[s_] = es.enter_context(nc.semaphore(s_))

    def replay(eng_name, seg):
        def go(e):
            for sg, ws, fn, inc, inc_n in S.ops[eng_name]:
                if sg != seg:
                    continue
                for s_, v in ws:
                    e.wait_ge(sems[s_], v)
                instr = fn(e)
                if inc is not None and instr is not None:
                    instr.then_inc(sems[inc], inc_n)
        return go

    nseg = S.seg + 1
    for seg in range(nseg):
        with nc.Block() as block:
            block.sync(replay("sp", seg))
            block.tensor(replay("pe", seg))
            block.vector(replay("dve", seg))
            block.scalar(replay("act", seg))
            block.gpsimd(replay("pool", seg))
        if seg < nseg - 1 and bar:
            nc.all_core_barrier()

    es.close()
    return nc


def make_in_maps(inputs, n_cores=8):
    def cvt(a, shp=None):
        a = np.asarray(a)
        if shp is not None:
            a = a.reshape(shp)
        return np.ascontiguousarray(a.astype(np.float32, copy=False))
    x = cvt(inputs["L0_attn_x"], (8192, 4096))
    wq = cvt(inputs["L0_attn_Wq"], (4096, 4096))
    bq = cvt(inputs["L0_attn_bq"])
    wk = cvt(inputs["L0_attn_Wk"], (4096, 1024))
    bk = cvt(inputs["L0_attn_bk"])
    wv = cvt(inputs["L0_attn_Wv"], (4096, 1024))
    bv = cvt(inputs["L0_attn_bv"])
    wg = cvt(inputs["L0_ffn_w_gate"])
    wu = cvt(inputs["L0_ffn_w_up"])
    wd = cvt(inputs["L0_ffn_w_down"])
    HI = 11008 // 2
    maps = []
    for c in range(n_cores):
        q = c % 2
        maps.append({
            "x": x[c * T:(c + 1) * T], "wq": wq, "bq": bq, "wk": wk, "bk": bk,
            "wv": wv, "bv": bv,
            "wg": np.ascontiguousarray(wg[q * HI:(q + 1) * HI]),
            "wu": np.ascontiguousarray(wu[q * HI:(q + 1) * HI]),
            "wd": np.ascontiguousarray(wd[:, q * HI:(q + 1) * HI]),
        })
    return maps


def kernel(**inputs):
    from concourse.bass_utils import run_bass_kernel_spmd
    nc = build_program()
    maps = make_in_maps(inputs)
    res = run_bass_kernel_spmd(nc, maps, core_ids=list(range(8)))
    out = np.concatenate([np.asarray(res.results[c]["out"]) for c in range(8)], axis=0)
    return out.reshape(2, 4096, 4096)
